# version 04
# baseline (speedup 1.0000x reference)
import numpy as np
import concourse.bass as bass
import concourse.mybir as mybir
import bass_rust
from concourse.alu_op_type import AluOpType as ALU
from concourse.bass_utils import run_bass_kernel_spmd

F32 = mybir.dt.float32
BF16 = mybir.dt.bfloat16
AFT = mybir.ActivationFunctionType
AXX = mybir.AxisListType.X

S, D, H, HD, FD = 2048, 1024, 16, 64, 256
NT, NB, NCH = 16, 4, 8
ENGS = ("t", "v", "s", "g", "y")


class Sched:
    def __init__(self):
        self.items = {e: [] for e in ENGS}
        self.ndma = {e: 0 for e in ENGS}

    def op(self, e, fn, after=()):
        self.items[e].append([[t for t in after if t is not None], fn, False, False])
        return ("i", e, len(self.items[e]) - 1)

    def dma(self, e, fn, after=()):
        self.items[e].append([[t for t in after if t is not None], fn, False, True])
        self.ndma[e] += 1
        return ("d", e, self.ndma[e] * 16)

    def finalize(self, nc, es):
        for e in ENGS:
            for it in self.items[e]:
                for tok in it[0]:
                    if tok[0] in ("i", "h"):
                        assert not self.items[tok[1]][tok[2]][3]
                        self.items[tok[1]][tok[2]][2] = True
        self.val = {}
        for e in ENGS:
            c, vals = 0, []
            for it in self.items[e]:
                if it[2]:
                    c += 1
                vals.append(c)
            self.val[e] = vals
        self.psem = {e: es.enter_context(nc.semaphore(f"p_{e}")) for e in ENGS}
        self.dsem = {e: es.enter_context(nc.semaphore(f"d_{e}"))
                     for e in ENGS if self.ndma[e]}

    def emit(self, e, eng):
        waited = {}
        for waits, fn, inc, is_dma in self.items[e]:
            for tok in waits:
                if tok[0] in ("i", "h"):
                    pe, idx = tok[1], tok[2]
                    if pe == e and tok[0] == "i":
                        continue
                    v = self.val[pe][idx]
                    if waited.get(("i", pe), 0) < v:
                        eng.wait_ge(self.psem[pe], v)
                        waited[("i", pe)] = v
                else:
                    qe, v = tok[1], tok[2]
                    if waited.get(("d", qe), 0) < v:
                        eng.wait_ge(self.dsem[qe], v)
                        waited[("d", qe)] = v
            inst = fn(eng)
            if is_dma:
                inst.then_inc(self.dsem[e], 16)
            elif inc:
                inst.then_inc(self.psem[e], 1)


def hard(tok):
    assert tok[0] == "i"
    return ("h", tok[1], tok[2])


class Bank:
    def __init__(self):
        self.w = None
        self.r = None

    def for_write(self):
        out = [t for t in (self.w, self.r) if t is not None]
        return out

    def wrote(self, tok):
        self.w, self.r = tok, None

    def for_read(self):
        return [self.w] if self.w is not None else []

    def read(self, tok):
        self.r = tok


class StopBuild(Exception):
    pass


def build_nc(n_sub=4, dbg=None):
    nc = bass.Bass("TRN2", target_bir_lowering=False)
    B = Sched()

    def op(eng, meth, *args, after=(), **kw):
        return B.op(eng, (lambda e, m=meth, a=args, k=kw: getattr(e, m)(*a, **k)),
                    after=after)

    def dma(q, dst, src, after=()):
        return B.dma(q, (lambda e, d=dst, s=src: e.dma_start(out=d, in_=s)), after=after)

    dbg_outs = []

    def dbg_point(label, toks):
        if dbg != label:
            return
        dumps = [("xn", t_xn, 16384, BF16), ("qc", t_qc, 16384, BF16),
                 ("kp", t_kp, 4096, BF16), ("vp", t_vp, 6208, BF16),
                 ("et", t_et, 2048, BF16), ("scrf", t_scrf, 1024, F32),
                 ("xt", t_xt, 1024, BF16), ("st", t_st, 16, F32),
                 ("gtm", t_fgtm, 1024, F32), ("srow", t_fsrow, 128, F32),
                 ("mq", t_fmq, 8, F32), ("madd", t_fmadd, 128, F32),
                 ("og", t_fog, 1024, BF16), ("ctxf", t_fctx, 1024, BF16),
                 ("hh", t_fh, 256, F32), ("gfm", t_fgfm, 2048, BF16),
                 ("cmp", t_fcmp, 128, F32), ("cnt", t_fcnt, 8, F32), ("scol", t_fscol, 8, F32),
                 ("w1h", t_fw1h, 4096, BF16)]
        for nm, th, n, dt in dumps:
            o = nc.dram_tensor("dbg_" + nm, [128, n], F32, kind="ExternalOutput").ap()
            srcap = bass_rust.AP(th, 0, [[n, 128], [1, n]])
            dbg_outs.append(B.dma("g", (lambda e, oo=o, ss=srcap: e.dma_start(out=oo, in_=ss)),
                                  after=[t for t in toks if t is not None]))
        raise StopBuild()

    def mm(dst, lhs, rhs, start, stop, after=(), tp=None):
        return B.op("t", (lambda e, d=dst, l=lhs, r=rhs, st=start, sp=stop, t=tp:
                          e.matmul(d, l, r, start=st, stop=sp, tile_position=t)),
                    after=after)

    def tr(dst, src, after=()):
        return B.op("t", (lambda e, d=dst, s=src: e.transpose(d, s, identb_late[0])),
                    after=after)

    DI = lambda name, shape: nc.dram_tensor(name, shape, F32, kind="ExternalInput").ap()
    dx = DI("x_in", [S, D])
    dout = nc.dram_tensor("x_out", [S, D], F32, kind="ExternalOutput").ap()
    d_aWq, d_aWk, d_aWv, d_aWo = DI("aWq", [D, D]), DI("aWk", [D, D]), DI("aWv", [D, D]), DI("aWo", [D, D])
    d_abq, d_abk = DI("abq", [128, 8]), DI("abk", [128, 8])
    d_abv, d_abo = DI("abv", [1, D]), DI("abo", [1, D])
    d_bWq, d_bWk, d_bWv, d_bWo = DI("bWqT", [D, D]), DI("bWkT", [D, D]), DI("bWvT", [D, D]), DI("bWoT", [D, D])
    d_bWf = DI("bWfT", [HD, FD])
    d_fWq, d_fWk, d_fWv, d_fWo = DI("fWq", [D, D]), DI("fWk", [D, D]), DI("fWv", [D, D]), DI("fWo", [D, D])
    d_fWs1 = DI("fWs1", [D, 256])
    d_fbs1, d_fWs2, d_fbs2 = DI("fbs1", [128, 2]), DI("fWs2", [128, 2]), DI("fbs2", [1, 1])
    d_sin, d_cos = DI("ropesin", [4, 128, S]), DI("ropecos", [4, 128, S])
    d_ident, d_lowtri = DI("identc", [128, 128]), DI("lowtri", [128, 128])
    d_causadd = DI("causadd", [128, 128])
    d_tri01 = DI("tri01", [128, 128])

    _probe = nc.alloc_sbuf_tensor("sb_probe", [128, 8], F32)
    SB_BASE = nc.lookup_mloc(_probe).addr
    assert SB_BASE <= 16512 + 4096, SB_BASE
    cur = [SB_BASE]

    def alloc(name, shape, dtype, at=None):
        esz = 2 if dtype == BF16 else 4
        n = 1
        for d_ in shape[1:]:
            n *= d_
        if at is None:
            off = (cur[0] + 31) // 32 * 32
            cur[0] = off + n * esz
            assert cur[0] <= SB_BASE + 212800, f"SBUF overflow at {name}: {cur[0]}"
        else:
            off = at
        return nc.alloc_sbuf_tensor_at(name, shape, dtype, offset=off)

    t_x = alloc("xres", [128, NT, D], F32)
    QC_OFF = cur[0]
    t_qc = alloc("qc", [128, NCH, S], BF16)
    XN_OFF = cur[0]
    t_xn = alloc("xn", [128, NCH, S], BF16)
    t_kp = alloc("kp", [128, 2, S], BF16)
    t_vp = alloc("vp", [128, 6208], BF16)
    TBL_OFF = (cur[0] + 31) // 32 * 32
    t_tbl = alloc("tbl", [128, 2, 2, 512], F32)
    t_wq = alloc("wq_s", [128, 8, 2, 128], BF16)
    t_wk = alloc("wk_s", [128, 8, 2, 128], BF16)
    t_wv = alloc("wv_s", [128, 8, 2, 128], BF16)
    t_wB = alloc("wB", [128, 8, 1024], BF16)
    t_wsl = alloc("wsl", [128, 2, 8, 128], BF16)
    WSL_OFF = cur[0] - 4096
    t_bca = alloc("bca", [128, 512], F32, at=WSL_OFF)
    t_bias = alloc("biasrows", [128, 2, 1024], BF16)
    t_bqc = alloc("bq_col", [128, 8], F32)
    t_bkc = alloc("bk_col", [128, 8], F32)
    SCRF_OFF = (cur[0] + 31) // 32 * 32
    t_scrf = alloc("scr_f32", [128, 1024], F32)
    t_xt = alloc("xt_tile", [128, 1024], BF16)
    t_xe = alloc("xt2", [128, 1024], BF16, at=WSL_OFF + 2048)
    t_et = alloc("et", [128, 2, 1024], BF16)
    ET_OFF = cur[0] - 4096
    t_rr = alloc("rr", [128, 512], F32)
    t_st = alloc("stats", [128, 16], F32)
    t_id = alloc("identb", [128, 128], BF16)
    t_of = alloc("onesf", [128, 128], F32)
    t_ob = alloc("onesb", [128, 128], BF16)
    t_b1 = alloc("bs1c", [128, 2], F32)
    t_w2 = alloc("ws2c", [128, 2], F32)
    t_w2h = alloc("ws2hl", [128, 4], BF16)
    t_b2 = alloc("bs2c", [128, 1], F32)
    t_cm = alloc("causm", [128, 128], F32)
    t_tb = alloc("trib", [128, 128], BF16)
    t_eg = alloc("egen", [128, 128], BF16)
    t_pgn = alloc("pgen", [128, 128], BF16)
    t_cc = alloc("constcol", [128, 8], F32)
    t_qf4 = alloc("qfb4", [128, 4, 2, 512], BF16, at=TBL_OFF)
    t_kf = alloc("kfb", [128, 2, 1024], BF16, at=ET_OFF)
    t_kv = alloc("kvsb", [128, 4, 2, 65], BF16, at=SCRF_OFF)
    t_ot = alloc("otm", [128, 256], BF16, at=SCRF_OFF + 2112)
    t_wft = alloc("wftb", [128, FD], BF16, at=SCRF_OFF + 2624)
    t_fws1 = alloc("f_ws1", [128, 8, 256], F32, at=QC_OFF)
    t_fw1h = alloc("f_w1h", [128, 2, 8, 256], BF16, at=QC_OFF + 8192)
    t_fgtm = alloc("f_gtm", [128, 1024], F32, at=QC_OFF + 16384)
    t_fghl = alloc("f_ghl", [128, 2, 1024], BF16, at=QC_OFF + 20480)
    t_fgfm = alloc("f_gfm", [128, 2, 8, 128], BF16, at=QC_OFF + 24576)
    t_fh = alloc("f_h", [128, 2, 128], F32, at=QC_OFF + 28672)
    t_fhhl = alloc("f_hhl", [128, 2, 2, 128], BF16, at=QC_OFF + 29696)
    t_fsrow = alloc("f_srow", [128, 128], F32, at=QC_OFF + 30720)
    t_fscol = alloc("f_scol", [128, 8], F32, at=QC_OFF + 31232)
    t_fcnt = alloc("f_cnt", [128, 8], F32, at=QC_OFF + 31264)
    t_fmq = alloc("f_mq", [128, 8], F32, at=QC_OFF + 31296)
    t_fmqb = alloc("f_mqb", [128, 16], BF16, at=QC_OFF + 31328)
    t_fmrow = alloc("f_mrow", [128, 128], F32, at=QC_OFF + 31360)
    t_fcmp = alloc("f_cmp", [128, 128], F32, at=XN_OFF)
    t_fmadd = alloc("f_madd", [128, 128], F32, at=XN_OFF + 512)
    t_fqg = alloc("f_qg", [128, 8, 128], BF16, at=XN_OFF + 1024)
    t_fkg = alloc("f_kg", [128, 8, 128], BF16, at=XN_OFF + 3072)
    t_fvg = alloc("f_vg", [128, 8, 128], BF16, at=XN_OFF + 5120)
    t_fctx = alloc("f_ctx", [128, 8, 128], BF16, at=XN_OFF + 7168)
    t_fog = alloc("f_og", [128, 1024], BF16, at=XN_OFF + 9216)
    t_fsm = alloc("f_sm", [128, 128], F32, at=XN_OFF + 11264)
    t_fpb = alloc("f_pb", [128, 128], BF16, at=XN_OFF + 11776)
    t_fpt = alloc("f_pt", [128, 128], BF16, at=XN_OFF + 12032)
    t_fxlo = alloc("f_xlo", [128, 1024], BF16, at=XN_OFF + 12288)
    t_lt = alloc("lowtrib", [128, 128], F32, at=XN_OFF + 14336)
    t_eg16 = alloc("egen16", [128, NT, 128], BF16, at=XN_OFF + 16384)
    t_pg16 = alloc("pgen16", [128, NT, 128], BF16, at=XN_OFF + 20480)

    h_pA0 = nc.alloc_psum_tensor("pA0", [128, 1024], F32)
    h_pA1 = nc.alloc_psum_tensor("pA1", [128, 1024], F32)
    h_pB0 = nc.alloc_psum_tensor("pB0", [128, 512], F32)
    h_pB1 = nc.alloc_psum_tensor("pB1", [128, 512], F32)
    h_pD0 = nc.alloc_psum_tensor("pD0", [128, 1024], BF16)
    h_pD1 = nc.alloc_psum_tensor("pD1", [128, 1024], BF16)
    pA = [h_pA0.ap(), h_pA1.ap()]
    pB = [h_pB0.ap(), h_pB1.ap()]
    pD = [h_pD0.ap(), h_pD1.ap()]
    pDf = [h_pD0.ap().bitcast(F32), h_pD1.ap().bitcast(F32)]
    bank = {"A0": Bank(), "A1": Bank(), "B0": Bank(), "B1": Bank(),
            "D0": Bank(), "D1": Bank()}

    identb_late = [None]
    x = t_x.ap(); qc = t_qc.ap(); xn = t_xn.ap(); kp = t_kp.ap(); vp = t_vp.ap()
    tbl = t_tbl.ap(); wq_s = t_wq.ap(); wk_s = t_wk.ap(); wv_s = t_wv.ap()
    wB = t_wB.ap(); wsl = t_wsl.ap(); biasr = t_bias.ap()
    bqc = t_bqc.ap(); bkc = t_bkc.ap(); scrf = t_scrf.ap(); xt = t_xt.ap()
    xlo = t_fxlo.ap(); et = t_et.ap(); rr = t_rr.ap(); bca = t_bca.ap()
    xt2 = t_xe.ap()
    stats = t_st.ap(); identb = t_id.ap(); lowtri = t_lt.ap(); wft = t_wft.ap()
    identb_late[0] = identb[:]
    onesf = t_of.ap(); onesb = t_ob.ap()
    bs1c = t_b1.ap(); ws2c = t_w2.ap(); ws2hl = t_w2h.ap(); bs2c = t_b2.ap()
    causm = t_cm.ap(); trib = t_tb.ap()
    egen = t_eg.ap(); pgen = t_pgn.ap(); kvsb = t_kv.ap(); otm = t_ot.ap()
    qfb4 = t_qf4.ap(); kfb = t_kf.ap()
    f_ws1 = t_fws1.ap(); f_w1h = t_fw1h.ap(); f_gtm = t_fgtm.ap(); f_ghl = t_fghl.ap()
    f_gfm = t_fgfm.ap(); f_h = t_fh.ap(); f_hhl = t_fhhl.ap(); f_srow = t_fsrow.ap()
    f_scol = t_fscol.ap(); f_cnt = t_fcnt.ap(); f_mq = t_fmq.ap(); f_mqb = t_fmqb.ap()
    f_mrow = t_fmrow.ap(); f_cmp = t_fcmp.ap(); f_madd = t_fmadd.ap()
    f_qg = t_fqg.ap(); f_kg = t_fkg.ap(); f_vg = t_fvg.ap(); f_ctx = t_fctx.ap()
    f_og = t_fog.ap(); f_sm = t_fsm.ap(); f_pb = t_fpb.ap(); f_pt = t_fpt.ap()

    def rap(th, rowext, pstart, nparts, foff, dims):
        return bass_rust.AP(th, pstart * rowext + foff, [[rowext, nparts]] + dims)

    VA_TT = 386
    VA_OFF = [0, 65, 193, 258]
    VA_W = [65, 128, 65, 128]

    tok_x = [dma("y", x[:, tt, :], dx[tt * 128:(tt + 1) * 128, :]) for tt in range(NT)]
    tok_id = dma("g", identb[:], d_ident[:])
    tok_cau = dma("y", causm[:], d_causadd[:])
    tok_tri = dma("g", trib[:], d_tri01[:])
    tok_bqc = dma("y", bqc[:], d_abq[:])
    tok_bkc = dma("y", bkc[:], d_abk[:])
    tok_bvr = dma("g", biasr[0:1, 0, :], d_abv[:])
    tok_bor = dma("g", biasr[0:1, 1, :], d_abo[:])
    tok_b1 = dma("y", bs1c[:], d_fbs1[:])
    tok_w2 = dma("y", ws2c[:], d_fWs2[:])
    tok_b2 = dma("y", bs2c[0:1, 0:1], d_fbs2[:])
    tok_onesf = op("v", "memset", onesf[:], 1.0)
    tok_onesb = op("v", "memset", onesb[:], 1.0)
    constcol = t_cc.ap()
    tok_cc = op("v", "memset", constcol[:, 0:1], -16.0)
    tok_cc2 = op("v", "memset", constcol[:, 1:2], 63.5)

    def wslab_load(dst, dsrc, j, after=()):
        src4 = dsrc.rearrange("(a p) (c n) -> p a c n", p=128, n=128)
        toks = []
        for slot, jj in enumerate((j, j + 4)):
            toks.append(dma("g", dst[:, :, slot, :], src4[:, :, jj, :], after=after))
        return toks

    def wfull_load(dst, dsrc, after=()):
        return [dma("g", dst[:], dsrc.rearrange("(a p) n -> p a n", p=128), after=after)]

    def emit_ln_tile(tt, after_tok, dump_ap, dump_bank, want_f32, apply_after=()):
        so = (tt % 2) * 8
        xtile = rap(t_x, NT * D, 0, 128, tt * D, [[1, D]])
        t_r = op("v", "tensor_reduce", stats[:, so:so + 1], xtile, AXX, ALU.add,
                 negate=True, after=[after_tok])
        t_nm = op("s", "mul", stats[:, so + 1:so + 2], stats[:, so:so + 1], 1.0 / D,
                  after=[t_r])
        t_sq = op("s", "activation", dump_ap, xtile, AFT.Square,
                  bias=stats[:, so + 1:so + 2], scale=1.0,
                  accum_out=stats[:, so + 2:so + 3],
                  after=[hard(t_nm)] + (bank[dump_bank].for_write() if dump_bank else []))
        if dump_bank:
            bank[dump_bank].wrote(t_sq)
        t_var = op("v", "tensor_scalar", stats[:, so + 3:so + 4], stats[:, so + 2:so + 3],
                   1.0 / D, 1e-5, ALU.mult, ALU.add, after=[t_sq])
        t_sqrt = op("s", "activation", stats[:, so + 4:so + 5], stats[:, so + 3:so + 4],
                    AFT.Sqrt, after=[t_var])
        t_rstd = op("v", "reciprocal", stats[:, so + 5:so + 6], stats[:, so + 4:so + 5],
                    after=[t_sqrt])
        dst = scrf[:] if want_f32 else xt[:]
        t_ap = op("v", "tensor_scalar", dst, xtile, stats[:, so + 1:so + 2],
                  stats[:, so + 5:so + 6], ALU.add, ALU.mult,
                  after=[hard(t_rstd)] + list(apply_after))
        return t_ap

    def emit_xn_for_attn(tok_res):
        tok_xn = [None] * NT
        xt_guard = [None]
        for tt in range(NT):
            dmp = "A0" if tt % 2 == 0 else "A1"
            t_ap = emit_ln_tile(tt, tok_res[tt], pA[tt % 2][:], dmp, False,
                                apply_after=[xt_guard[0]])
            last_ev = None
            for half in range(2):
                db = "D0" if half == 0 else "D1"
                tts = []
                for c4 in range(4):
                    c = half * 4 + c4
                    tts.append(tr(pD[half][:, c4 * 128:(c4 + 1) * 128],
                                  xt[:, c * 128:(c + 1) * 128],
                                  after=[t_ap, tok_id] + (bank[db].for_write() if c4 == 0 else [])))
                bank[db].wrote(tts[-1])
                dstap = rap(t_xn, NCH * S, 0, 128, (half * 4) * S + tt * 128,
                            [[S, 4], [1, 128]])
                srcap = rap(h_pD0 if half == 0 else h_pD1, 1024, 0, 128, 0,
                            [[128, 4], [1, 128]])
                ev = op("v", "tensor_copy", dstap, srcap, after=[tts[-1]])
                bank[db].read(ev)
                last_ev = ev
                xt_guard[0] = tts[-1]
            tok_xn[tt] = last_ev
        return tok_xn

    def emit_attn_a(tok_res):
        tok_xn = emit_xn_for_attn(tok_res)
        dbg_point("xn", tok_xn)
        vp_init = []
        for hh in range(4):
            if hh % 2 == 0:
                vp_init.append(op("v", "memset",
                                  rap(t_vp, 6208, 0, 128, VA_OFF[hh] + 64,
                                      [[VA_TT, NT], [1, 1]]), 1.0))
            else:
                vp_init.append(op("v", "memset",
                                  rap(t_vp, 6208, 0, 128, VA_OFF[hh],
                                      [[VA_TT, NT], [1, 1]]), 1.0))
                vp_init.append(op("v", "memset",
                                  rap(t_vp, 6208, 0, 128, VA_OFF[hh] + 1,
                                      [[VA_TT, NT], [1, 63]]), 0.0))
        wo_tok = wfull_load(wB, d_aWo)
        PS = [pB[0][:, 0:512], pB[1][:, 0:512], pA[0][:, 0:512], pA[1][:, 0:512]]
        PSB = ["B0", "B1", "A0", "A1"]
        tbl_cons = [None, None]
        slab_tok = {"q": wslab_load(wq_s, d_aWq, 0),
                    "k": wslab_load(wk_s, d_aWk, 0),
                    "v": wslab_load(wv_s, d_aWv, 0)}
        slab_cons = {"q": None, "k": None, "v": None}
        et_cons = [None, None]
        last_ctx_ev = None
        rope_done = [None]
        v_ev_last = [None]

        for j in range(4):
            for phase in ("q", "k"):
                slab_t = {"q": t_wq, "k": t_wk}[phase]
                bcol = bqc if phase == "q" else bkc
                btok = tok_bqc if phase == "q" else tok_bkc
                for b in range(NB):
                    sslot = b % 2
                    t_s = dma("y", tbl[:, sslot, 0, :], d_sin[j, :, b * 512:(b + 1) * 512],
                              after=[tbl_cons[sslot]])
                    t_c = dma("y", tbl[:, sslot, 1, :], d_cos[j, :, b * 512:(b + 1) * 512],
                              after=[tbl_cons[sslot]])
                    si = 2 * (b % 2)
                    mm_t = []
                    for ci in range(2):
                        dst, bnm = PS[si + ci], PSB[si + ci]
                        first_after = ([tok_xn[4 * b + 3]] + slab_tok[phase]
                                       + bank[bnm].for_write())
                        last = None
                        for a in range(NCH):
                            last = mm(dst, rap(slab_t, 2048, 0, 128, a * 256 + ci * 128,
                                               [[1, 128]]),
                                      xn[:, a, b * 512:(b + 1) * 512],
                                      a == 0, a == NCH - 1,
                                      after=(first_after if a == 0 else []))
                        bank[bnm].wrote(last)
                        mm_t.append(last)
                    slab_cons[phase] = mm_t[-1]
                    Pe, Po = PS[si], PS[si + 1]
                    sin_ap, cos_ap = tbl[:, sslot, 0, :], tbl[:, sslot, 1, :]
                    if phase == "q":
                        d_lo = qc[:, j, b * 512:(b + 1) * 512]
                        d_hi = qc[:, j + 4, b * 512:(b + 1) * 512]
                    else:
                        d_lo = kp[:, 0, b * 512:(b + 1) * 512]
                        d_hi = kp[:, 1, b * 512:(b + 1) * 512]
                    be, bo_ = bcol[:, j:j + 1], bcol[:, j + 4:j + 5]
                    s1, s2 = scrf[:, 0:512], scrf[:, 512:1024]
                    op("v", "scalar_tensor_tensor", s1, Pe, be, cos_ap, ALU.add, ALU.mult,
                       after=[mm_t[0], t_c, btok])
                    op("v", "scalar_tensor_tensor", s2, Po, bo_, sin_ap, ALU.add, ALU.mult,
                       after=[mm_t[1], t_s])
                    op("v", "tensor_sub", d_lo, s1, s2)
                    t3 = op("v", "scalar_tensor_tensor", s1, Pe, be, sin_ap, ALU.add, ALU.mult)
                    t4 = op("v", "scalar_tensor_tensor", s2, Po, bo_, cos_ap, ALU.add, ALU.mult)
                    t_hi = op("v", "tensor_sub", d_hi, s1, s2)
                    tbl_cons[sslot] = t_hi
                    rope_done[0] = t_hi
                    bank[PSB[si]].read(t3)
                    bank[PSB[si + 1]].read(t4)
                if j < 3:
                    slab_tok[phase] = wslab_load(
                        {"q": wq_s, "k": wk_s}[phase],
                        {"q": d_aWq, "k": d_aWk}[phase], j + 1,
                        after=[slab_cons[phase]])
            dbg_point("qk", [rope_done[0]])
            for tt in range(NT):
                bnm = "B0" if tt % 2 == 0 else "B1"
                dstp = pB[tt % 2][:, 0:256]
                first_after = ([tok_xn[tt]] + slab_tok["v"] + vp_init
                               + bank[bnm].for_write())
                last = None
                for a in range(NCH):
                    last = mm(dstp, xn[:, a, tt * 128:(tt + 1) * 128], wv_s[:, a],
                              a == 0, False, after=(first_after if a == 0 else []))
                last = mm(dstp, onesb[0:1, 0:128],
                          rap(t_bias, 2048, 0, 1, j * 128, [[512, 2], [1, 128]]),
                          False, True, after=[tok_bvr, tok_onesb])
                bank[bnm].wrote(last)
                slab_cons["v"] = last
                pbh = h_pB0 if tt % 2 == 0 else h_pB1
                ev_e = op("v", "tensor_copy",
                          rap(t_vp, 6208, 0, 128, tt * VA_TT, [[193, 2], [1, 64]]),
                          rap(pbh, 512, 0, 128, 0, [[128, 2], [1, 64]]), after=[last])
                ev_o = op("v", "tensor_copy",
                          rap(t_vp, 6208, 0, 128, tt * VA_TT + 65 + 64, [[193, 2], [1, 64]]),
                          rap(pbh, 512, 0, 128, 64, [[128, 2], [1, 64]]))
                bank[bnm].read(ev_o)
                v_ev_last[0] = ev_o
            if j < 3:
                slab_tok["v"] = wslab_load(wv_s, d_aWv, j + 1, after=[slab_cons["v"]])
            dbg_point("v", [v_ev_last[0]])

            for hp in range(2):
                lanes = [2 * hp, 2 * hp + 1]
                qch = [j if hh < 2 else j + 4 for hh in lanes]
                ksl = [0 if hh < 2 else 1 for hh in lanes]
                for qt in range(NB):
                    nkt = 4 * qt + 4
                    ctx_started = [False, False]
                    pv_lane = [None, None]
                    exp_lane = [None, None]
                    for p in range((nkt + 1) // 2):
                        smm_lane = [None, None]
                        msk_lane = [[], []]
                        for L in range(2):
                            hh = lanes[L]
                            prow = 64 * L
                            sreg = pA[L]
                            sb = "A0" if L == 0 else "A1"
                            smm = []
                            for sub in range(2):
                                kt = 2 * p + sub
                                lhs = rap(t_kp, 2 * S, prow, 64,
                                          ksl[L] * S + kt * 128, [[1, 128]])
                                rhs = rap(t_qc, NCH * S, prow, 64,
                                          qch[L] * S + qt * 512, [[1, 512]])
                                smm.append(mm(sreg[:, sub * 512:(sub + 1) * 512], lhs,
                                              rhs, True, True,
                                              after=(bank[sb].for_write()
                                                     + [rope_done[0], v_ev_last[0]]
                                                     if sub == 0 else [])))
                            bank[sb].wrote(smm[-1])
                            smm_lane[L] = smm
                        for L in range(2):
                            sb = "A0" if L == 0 else "A1"
                            t_exp = op("s", "activation", et[:, L, :], pA[L][:],
                                       AFT.Exp, bias=constcol[:, 0:1], scale=0.125,
                                       after=[smm_lane[L][-1], et_cons[L], tok_cc])
                            bank[sb].read(t_exp)
                            exp_lane[L] = t_exp
                        for L in range(2):
                            for sub in range(2):
                                kt = 2 * p + sub
                                d_ = kt - 4 * qt
                                if d_ >= 0:
                                    sl = et[:, L, sub * 512 + d_ * 128:
                                            sub * 512 + d_ * 128 + 128]
                                    msk_lane[L].append(op("v", "tensor_tensor", sl, sl,
                                                          trib[:], ALU.mult,
                                                          after=[exp_lane[L], tok_tri]))
                        for L in range(2):
                            hh = lanes[L]
                            ctxp = pB[L]
                            cb = "B0" if L == 0 else "B1"
                            pv_last = None
                            for sub in range(2):
                                kt = 2 * p + sub
                                d_ = kt - 4 * qt
                                doff = max(0, d_) * 128
                                if L == 0:
                                    lhsv = rap(t_vp, 6208, 0, 128,
                                               kt * VA_TT + VA_OFF[hh], [[1, 65]])
                                    dstc = ctxp[0:65, doff:512]
                                else:
                                    lhsv = rap(t_vp, 6208, 0, 128,
                                               kt * VA_TT + VA_OFF[hh], [[1, 128]])
                                    dstc = ctxp[:, doff:512]
                                aft = [exp_lane[L]] + msk_lane[L]
                                if not ctx_started[L]:
                                    aft += bank[cb].for_write()
                                pv_last = mm(dstc, lhsv,
                                             et[:, L, sub * 512 + doff:(sub + 1) * 512],
                                             not ctx_started[L], kt == nkt - 1,
                                             after=aft)
                                ctx_started[L] = True
                            et_cons[L] = pv_last
                            pv_lane[L] = pv_last
                        if dbg == "et" and hp == 0 and qt == 0 and p == 0:
                            dbg_point("et", pv_lane)
                    for L in range(2):
                        hh = lanes[L]
                        ctxp = pB[L]
                        cb = "B0" if L == 0 else "B1"
                        bank[cb].wrote(pv_lane[L])
                        rowd = 64 if L == 0 else 0
                        t_rc = op("v", "reciprocal", rr[rowd:rowd + 1, :],
                                  ctxp[rowd:rowd + 1, :], after=[pv_lane[L]])
                        db = "D0" if L == 0 else "D1"
                        dpf = pDf[L]
                        if L == 0:
                            t_bc = mm(dpf[0:64, :], onesf[64:65, 0:64], rr[64:65, :],
                                      True, True,
                                      after=[t_rc, tok_onesf] + bank[db].for_write())
                            rows = slice(0, 64)
                        else:
                            t_bc = mm(dpf[64:128, :], onesf[0:1, 0:64], rr[0:1, :],
                                      True, True,
                                      after=[t_rc, tok_onesf] + bank[db].for_write(),
                                      tp=(0, 64))
                            rows = slice(64, 128)
                        bank[db].wrote(t_bc)
                        t_be = op("s", "copy", bca[rows, :], dpf[rows, :], after=[t_bc])
                        bank[db].read(t_be)
                        dst = rap(t_qc, NCH * S, rows.start, 64,
                                  qch[L] * S + qt * 512, [[1, 512]])
                        srcc = ctxp[rows, 0:512] if L == 1 else ctxp[0:64, 0:512]
                        t_ev = op("v", "tensor_tensor", dst, srcc, bca[rows, :],
                                  ALU.mult, after=[t_be])
                        bank[cb].read(t_ev)
                        last_ctx_ev = t_ev
                        if dbg == "ctx" and j == 0 and hh == 0 and qt == 0:
                            dbg_point("ctx", [t_ev])

        tok_y = [None] * NT
        for tt in range(NT):
            for ch in range(2):
                bnm = "B0" if (tt * 2 + ch) % 2 == 0 else "B1"
                dstp = pB[0] if bnm == "B0" else pB[1]
                first_after = [last_ctx_ev] + wo_tok + bank[bnm].for_write()
                last = None
                for a in range(NCH):
                    last = mm(dstp[:, 0:512], qc[:, a, tt * 128:(tt + 1) * 128],
                              wB[:, a, ch * 512:(ch + 1) * 512],
                              a == 0, False, after=(first_after if a == 0 else []))
                last = mm(dstp[:, 0:512], onesb[0:1, 0:128],
                          rap(t_bias, 2048, 0, 1, 1024 + ch * 512, [[1, 512]]),
                          False, True, after=[tok_bor])
                bank[bnm].wrote(last)
                t_add = op("v", "tensor_tensor", x[:, tt, ch * 512:(ch + 1) * 512],
                           dstp[:, 0:512], x[:, tt, ch * 512:(ch + 1) * 512], ALU.add,
                           after=[last])
                bank[bnm].read(t_add)
                tok_y[tt] = t_add
        wb_free[0] = last
        return tok_y

    def emit_ffn(tok_res, second):
        eg16 = t_eg16.ap()
        pg16 = t_pg16.ap()
        eg_toks = []
        prev = tok_res[0]
        for tt in range(NT):
            eview = rap(t_eg16, NT * 128, 0, 128, tt * 128, [[16, 8], [1, 16]])
            t0 = op("g", "memset", eview, 1.0, after=[prev])
            t1 = op("g", "affine_select", out=eview, in_=eview,
                    pattern=[[-1, 8], [0, 16]], compare_op=ALU.is_ge, fill=0.0,
                    base=-8 * tt, channel_multiplier=1, after=[t0])
            t2 = op("g", "affine_select", out=eview, in_=eview,
                    pattern=[[1, 8], [0, 16]], compare_op=ALU.is_ge, fill=0.0,
                    base=8 * tt, channel_multiplier=-1, after=[t1])
            prev = t2
            eg_toks.append(t2)
        pg_toks = []
        for tt in range(NT):
            db = "D0" if tt % 2 == 0 else "D1"
            dp = pD[0] if db == "D0" else pD[1]
            t_tp = tr(dp[:, 0:128], eg16[:, tt, :],
                      after=[eg_toks[tt], tok_id] + bank[db].for_write())
            bank[db].wrote(t_tp)
            t_pg = op("v", "tensor_scalar", pg16[:, tt, :], dp[:, 0:128], 1.0 / 16.0,
                      None, ALU.mult, after=[t_tp])
            bank[db].read(t_pg)
            pg_toks.append(t_pg)

        pool_last = None
        for tt in range(NT):
            t_ap = emit_ln_tile(tt, tok_res[tt], pA[1][:], "A1", True)
            t_hi = op("v", "tensor_copy", xt[:], scrf[:], after=[hard(t_ap), pool_last])
            t_lo = op("v", "tensor_sub", xlo[:], scrf[:], xt[:], after=[hard(t_hi)])
            for src, src_tok in ((xt, t_hi), (xlo, t_lo)):
                for ch in range(2):
                    aft = [pg_toks[tt], src_tok]
                    if tt == 0 and src is xt and ch == 0:
                        aft += bank["A0"].for_write()
                    pool_last = mm(pA[0][:, ch * 512:(ch + 1) * 512], pg16[:, tt, :],
                                   src[:, ch * 512:(ch + 1) * 512],
                                   tt == 0 and src is xt,
                                   tt == NT - 1 and src is xlo and ch == 1,
                                   after=aft)
        bank["A0"].wrote(pool_last)
        tok_lt = dma("y", lowtri[:], d_lowtri[:], after=[pool_last])
        tok_ws1 = dma("y", f_ws1[:], d_fWs1.rearrange("(a p) n -> p a n", p=128),
                      after=[wb_free[0]] if wb_free[0] else [])

        t_g = op("v", "tensor_copy", f_gtm[:], pA[0][:], after=[pool_last])
        dbg_point("fG", [t_g])
        bank["A0"].read(t_g)
        t_gh = op("v", "tensor_copy", f_ghl[:, 0, :], f_gtm[:], after=[hard(t_g)])
        t_gl = op("v", "tensor_sub", f_ghl[:, 1, :], f_gtm[:], f_ghl[:, 0, :],
                  after=[hard(t_gh)])
        gt_toks = []
        for hl in range(2):
            src_tok = t_gh if hl == 0 else t_gl
            for half in range(2):
                db = "D0" if half == 0 else "D1"
                dp = pD[0] if half == 0 else pD[1]
                tts = []
                for c4 in range(4):
                    c = half * 4 + c4
                    tts.append(tr(dp[:, c4 * 128:(c4 + 1) * 128],
                                  f_ghl[:, hl, c * 128:(c + 1) * 128],
                                  after=[src_tok] + (bank[db].for_write() if c4 == 0 else [])))
                bank[db].wrote(tts[-1])
                dstap = rap(t_fgfm, 2048, 0, 128, hl * 1024 + (half * 4) * 128,
                            [[128, 4], [1, 128]])
                srcap = rap(h_pD0 if half == 0 else h_pD1, 1024, 0, 128, 0,
                            [[128, 4], [1, 128]])
                ev = op("v", "tensor_copy", dstap, srcap, after=[tts[-1]])
                bank[db].read(ev)
                gt_toks.append(ev)
        t_w1 = op("v", "tensor_copy", f_w1h[:, 0], f_ws1.rearrange("p a n -> p (a n)"),
                  after=[tok_ws1])
        t_w1l = op("v", "tensor_sub", f_w1h[:, 1], f_ws1.rearrange("p a n -> p (a n)"),
                   f_w1h[:, 0], after=[hard(t_w1)])
        t_hm = [None, None]
        for m_ in range(2):
            bnm = "B0"
            first = True
            last = None
            for a in range(NCH):
                for wi in range(2):
                    for gi in range(2):
                        aft = []
                        if first:
                            aft = [t_w1l] + gt_toks + bank[bnm].for_write()
                        last = mm(pB[0][:, 0:128],
                                  rap(t_fw1h, 4096, 0, 128, wi * 2048 + a * 256 + m_ * 128,
                                      [[1, 128]]),
                                  f_gfm[:, gi, a, :], first,
                                  a == NCH - 1 and wi == 1 and gi == 1, after=aft)
                        first = False
            bank[bnm].wrote(last)
            t_hm[m_] = op("s", "activation", f_h[:, m_, :], pB[0][:, 0:128], AFT.Relu,
                          bias=bs1c[:, m_:m_ + 1], after=[last, tok_b1])
            bank[bnm].read(t_hm[m_])
        t_hh = op("v", "tensor_copy", f_hhl[:, 0], f_h.rearrange("p a n -> p (a n)"),
                  after=[t_hm[0], t_hm[1]])
        t_hl = op("v", "tensor_sub", f_hhl[:, 1], f_h.rearrange("p a n -> p (a n)"),
                  f_hhl[:, 0], after=[hard(t_hh)])
        t_2h = op("v", "tensor_copy", ws2hl[:, 0:2], ws2c[:], after=[tok_w2])
        t_2l = op("v", "tensor_sub", ws2hl[:, 2:4], ws2c[:], ws2hl[:, 0:2],
                  after=[hard(t_2h)])
        first = True
        last = None
        for m_ in range(2):
            for wi in range(2):
                for gi in range(2):
                    aft = []
                    if first:
                        aft = [t_hl, t_2l] + bank["B1"].for_write()
                    last = mm(pB[1][0:1, 0:128], ws2hl[:, 2 * wi + m_:2 * wi + m_ + 1],
                              f_hhl[:, gi, m_, :], first,
                              m_ == 1 and wi == 1 and gi == 1, after=aft)
                    first = False
        bank["B1"].wrote(last)
        t_sr = op("v", "tensor_scalar", f_srow[0:1, 0:128], pB[1][0:1, 0:128],
                  bs2c[0:1, 0:1], None, ALU.add, after=[last, tok_b2])
        dbg_point("fscore", [t_sr, t_hm[0], t_hm[1]])
        bank["B1"].read(t_sr)
        t_sc = dma("y", f_scol[:, 0:1], f_srow[0:1, 0:128], after=[t_sr])
        t_sb = mm(pA[1][:, 0:128], onesf[0:1, 0:128], f_srow[0:1, 0:128], True, True,
                  after=[t_sr, tok_onesf] + bank["A1"].for_write())
        bank["A1"].wrote(t_sb)
        t_cmp = op("v", "tensor_scalar", f_cmp[:], pA[1][:, 0:128], f_scol[:, 0:1], None,
                   ALU.is_gt, after=[t_sb, t_sc])
        t_eq = op("v", "scalar_tensor_tensor", f_madd[:], pA[1][:, 0:128], f_scol[:, 0:1],
                  lowtri[:], ALU.is_equal, ALU.mult, after=[hard(t_cmp), t_sc, tok_lt])
        bank["A1"].read(t_eq)
        t_add_ = op("v", "tensor_add", f_cmp[:], f_cmp[:], f_madd[:],
                    after=[hard(t_eq)])
        t_red_ = op("v", "tensor_reduce", f_cnt[:, 0:1], f_cmp[:], AXX, ALU.add,
                    after=[hard(t_add_)])
        t_mq = op("v", "tensor_tensor", f_mq[:, 0:1], f_cnt[:, 0:1], constcol[:, 1:2],
                  ALU.is_lt, after=[tok_cc2, hard(t_red_)])
        dbg_point("fcount", [t_mq])
        t_mqb = op("v", "tensor_copy", f_mqb[:, 0:1], f_mq[:, 0:1], after=[hard(t_mq)])
        t_mr = mm(pB[1][0:1, 0:128], f_mqb[:, 0:1], identb[:], True, True,
                  after=[t_mqb] + bank["B1"].for_write())
        bank["B1"].wrote(t_mr)
        t_mre = op("v", "tensor_copy", f_mrow[0:1, 0:128], pB[1][0:1, 0:128], after=[t_mr])
        bank["B1"].read(t_mre)
        t_mb = mm(pA[1][:, 0:128], onesf[0:1, 0:128], f_mrow[0:1, 0:128], True, True,
                  after=[t_mre] + bank["A1"].for_write())
        bank["A1"].wrote(t_mb)
        t_madd = op("v", "tensor_scalar", f_madd[:], pA[1][:, 0:128], 1.0, 1e30,
                    ALU.subtract, ALU.mult, after=[t_mb])
        bank["A1"].read(t_madd)
        dbg_point("fmask", [t_madd, t_mq])

        wo_tok = wfull_load(wB, d_fWo, after=[wb_free[0]])
        sl_i = [0]
        sl_cons = [None, None]
        def stream_mm(dsrc, out_sb, token_major, after_extra):
            toks = []
            src4 = dsrc.rearrange("(a p) (c n) -> p a c n", p=128, n=128)
            for m_ in range(NCH):
                par_ = sl_i[0] % 2
                sl_i[0] += 1
                t_ld = dma("g", wsl[:, par_], src4[:, :, m_, :], after=[sl_cons[par_]])
                bnm = "B0" if m_ % 2 == 0 else "B1"
                dstp = pB[0] if bnm == "B0" else pB[1]
                last = None
                for a in range(NCH):
                    aft = []
                    if a == 0:
                        aft = [t_ld] + after_extra + bank[bnm].for_write()
                    if token_major:
                        last = mm(dstp[:, 0:128], f_gfm[:, 0, a, :], wsl[:, par_, a, :],
                                  a == 0, a == NCH - 1, after=aft)
                    else:
                        last = mm(dstp[:, 0:128], wsl[:, par_, a, :], f_gfm[:, 0, a, :],
                                  a == 0, a == NCH - 1, after=aft)
                bank[bnm].wrote(last)
                sl_cons[par_] = last
                ev = op("v", "tensor_copy", out_sb[:, m_, :], dstp[:, 0:128], after=[last])
                bank[bnm].read(ev)
                toks.append(ev)
            return toks

        tq = stream_mm(d_fWq, f_qg, False, gt_toks)
        tk = stream_mm(d_fWk, f_kg, False, [])
        tv = stream_mm(d_fWv, f_vg, True, [])

        ctx_toks = []
        for h in range(H):
            ch_, pr = h // 2, 64 * (h % 2)
            lq = rap(t_fqg, 1024, pr, 64, ch_ * 128, [[1, 128]])
            lk = rap(t_fkg, 1024, pr, 64, ch_ * 128, [[1, 128]])
            t_sg = mm(pB[0][:, 0:128], lq, lk, True, True,
                      after=[tq[-1], tk[-1], t_madd] + bank["B0"].for_write())
            bank["B0"].wrote(t_sg)
            t_sm = op("v", "tensor_add", f_sm[:], pB[0][:, 0:128], f_madd[:], after=[t_sg])
            bank["B0"].read(t_sm)
            t_nmx = op("v", "tensor_reduce", f_cnt[:, 1:2], f_sm[:], AXX, ALU.max,
                       negate=True)
            t_bias_ = op("s", "mul", f_cnt[:, 2:3], f_cnt[:, 1:2], 0.125, after=[t_nmx])
            t_p = op("s", "activation", f_pb[:], f_sm[:], AFT.Exp,
                     bias=f_cnt[:, 2:3], scale=0.125, accum_out=f_cnt[:, 3:4],
                     after=[hard(t_bias_)])
            t_rd = op("v", "reciprocal", f_cnt[:, 1:2], f_cnt[:, 3:4], after=[t_p])
            t_pn = op("v", "tensor_scalar", f_pb[:], f_pb[:], f_cnt[:, 1:2], None,
                      ALU.mult, after=[hard(t_rd)])
            db = "D0" if h % 2 == 0 else "D1"
            dp = pD[0] if h % 2 == 0 else pD[1]
            t_tp = tr(dp[:, 0:128], f_pb[:],
                      after=[t_pn] + bank[db].for_write())
            bank[db].wrote(t_tp)
            t_pt = op("v", "tensor_copy", f_pt[:], dp[:, 0:128], after=[t_tp])
            bank[db].read(t_pt)
            lv = rap(t_fvg, 1024, 0, 128, ch_ * 128 + pr, [[1, 64]])
            if h % 2 == 0:
                t_cx = mm(pB[1][0:64, 0:128], lv, f_pt[:], True, True,
                          after=[t_pt, tv[-1]] + bank["B1"].for_write())
                rows = slice(0, 64)
            else:
                t_cx = mm(pB[1][64:128, 0:128], lv, f_pt[:], True, True,
                          after=[t_pt, tv[-1]] + bank["B1"].for_write(), tp=(0, 64))
                rows = slice(64, 128)
            bank["B1"].wrote(t_cx)
            t_cv = op("v", "tensor_copy",
                      rap(t_fctx, 1024, rows.start, 64, ch_ * 128, [[1, 128]]),
                      pB[1][rows, 0:128], after=[t_cx])
            bank["B1"].read(t_cv)
            ctx_toks.append(t_cv)

        og_toks = []
        last_og_mm = [None]
        for ch in range(2):
            bnm = "B0"
            last = None
            for a in range(NCH):
                aft = []
                if a == 0:
                    aft = [ctx_toks[-1]] + wo_tok + bank[bnm].for_write()
                last = mm(pB[0][:, 0:512], f_ctx[:, a, :], wB[:, a, ch * 512:(ch + 1) * 512],
                          a == 0, a == NCH - 1, after=aft)
            bank[bnm].wrote(last)
            last_og_mm[0] = last
            t_og = op("v", "tensor_scalar", f_og[:, ch * 512:(ch + 1) * 512],
                      pB[0][:, 0:512], f_mq[:, 0:1], None, ALU.mult, after=[last])
            bank[bnm].read(t_og)
            og_toks.append(t_og)
        dbg_point("fog", og_toks + ctx_toks)
        tok_y = [None] * NT
        for tt in range(NT):
            for ch in range(2):
                bnm = "B1" if (tt * 2 + ch) % 2 == 0 else "B0"
                dstp = pB[1] if bnm == "B1" else pB[0]
                t_y = mm(dstp[:, 0:512], eg16[:, tt, :],
                         f_og[:, ch * 512:(ch + 1) * 512],
                         True, True,
                         after=[eg_toks[tt]] + og_toks + bank[bnm].for_write())
                bank[bnm].wrote(t_y)
                t_add = op("v", "tensor_tensor", x[:, tt, ch * 512:(ch + 1) * 512],
                           dstp[:, 0:512], x[:, tt, ch * 512:(ch + 1) * 512], ALU.add,
                           after=[t_y, tok_res[tt]])
                bank[bnm].read(t_add)
                tok_y[tt] = t_add
        wb_free[0] = last_og_mm[0]
        return tok_y

    def emit_attn_b(tok_res):
        tok_xn = emit_xn_for_attn(tok_res)
        vpi = [op("v", "memset",
                  rap(t_vp, 6208, 0, 128, hh * 65 + 64, [[260, NT], [1, 1]]), 1.0)
               for hh in range(4)]
        slab_tok = {"q": wslab_load(wq_s, d_bWq, 0),
                    "k": wslab_load(wk_s, d_bWk, 0),
                    "v": wslab_load(wv_s, d_bWv, 0)}
        slab_cons = {"q": None, "k": None, "v": None}
        wo_tok = wfull_load(wB, d_bWo, after=[wb_free[0]])
        tok_wft0 = dma("g", wft[0:HD, :], d_bWf[:], after=[wb_free[0]])
        tok_wft = dma("g", wft[HD:128, :], d_bWf[:], after=[wb_free[0]])
        PS = [pB[0][:, 0:512], pB[1][:, 0:512], pA[0][:, 0:512], pA[1][:, 0:512]]
        PSB = ["B0", "B1", "A0", "A1"]
        last_out_ev = None
        qk_ev = [None]
        v_ev = [None]
        for j in range(4):
            for phase in ("q", "k"):
                slab_t = {"q": t_wq, "k": t_wk}[phase]
                for b in range(NB):
                    si = 2 * (b % 2)
                    for ci in range(2):
                        dst, bnm = PS[si + ci], PSB[si + ci]
                        first_after = ([tok_xn[4 * b + 3]] + slab_tok[phase]
                                       + bank[bnm].for_write())
                        last = None
                        for a in range(NCH):
                            last = mm(dst, rap(slab_t, 2048, 0, 128, a * 256 + ci * 128,
                                               [[1, 128]]),
                                      xn[:, a, b * 512:(b + 1) * 512],
                                      a == 0, a == NCH - 1,
                                      after=(first_after if a == 0 else []))
                        bank[bnm].wrote(last)
                        slab_cons[phase] = last
                        if phase == "q":
                            dstsb = qc[:, (j, j + 4)[ci], b * 512:(b + 1) * 512]
                        else:
                            dstsb = kp[:, ci, b * 512:(b + 1) * 512]
                        ev = op("v", "tensor_copy", dstsb, dst, after=[last])
                        bank[bnm].read(ev)
                        qk_ev[0] = ev
                if j < 3:
                    slab_tok[phase] = wslab_load(
                        {"q": wq_s, "k": wk_s}[phase],
                        {"q": d_bWq, "k": d_bWk}[phase], j + 1, after=[slab_cons[phase]])
            for tt in range(NT):
                bnm = "B0" if tt % 2 == 0 else "B1"
                dstp = pB[tt % 2][:, 0:256]
                first_after = [tok_xn[tt]] + slab_tok["v"] + vpi + bank[bnm].for_write()
                last = None
                for a in range(NCH):
                    last = mm(dstp, xn[:, a, tt * 128:(tt + 1) * 128], wv_s[:, a],
                              a == 0, a == NCH - 1,
                              after=(first_after if a == 0 else []))
                bank[bnm].wrote(last)
                slab_cons["v"] = last
                pbh = h_pB0 if tt % 2 == 0 else h_pB1
                ev = op("v", "tensor_copy",
                        rap(t_vp, 6208, 0, 128, tt * 260, [[65, 4], [1, 64]]),
                        rap(pbh, 512, 0, 128, 0, [[64, 4], [1, 64]]), after=[last])
                bank[bnm].read(ev)
                v_ev[0] = ev
            if j < 3:
                slab_tok["v"] = wslab_load(wv_s, d_bWv, j + 1, after=[slab_cons["v"]])
            if j == 0:
                dbg_point("bqk", [qk_ev[0], v_ev[0]])
            HHROW = [0, 64, 0, 64]
            HHCH = [j, j, j + 4, j + 4]
            HHKS = [0, 0, 1, 1]
            kv_started = False
            kv_last = None
            kf_guard = [None, None]
            eb_guard = [None, None]
            for tt in range(NT):
                ebuf = xt[:] if tt % 2 == 0 else xt2[:]
                sb = "A0" if tt % 2 == 0 else "A1"
                pfp = pA[0] if tt % 2 == 0 else pA[1]
                pf_last = None
                PFCOL = [0, 512, 256, 768]
                for hh in range(4):
                    aft = [tok_wft0, tok_wft, qk_ev[0]]
                    if hh == 0:
                        aft += bank[sb].for_write()
                    pf_last = mm(pfp[:, PFCOL[hh]:PFCOL[hh] + 256],
                                 rap(t_kp, 2 * S, HHROW[hh], 64,
                                     HHKS[hh] * S + tt * 128, [[1, 128]]),
                                 wft[HHROW[hh]:HHROW[hh] + 64, :], True, True,
                                 after=aft)
                bank[sb].wrote(pf_last)
                t_e = op("s", "activation", ebuf, pfp[:], AFT.Exp,
                         after=[pf_last, eb_guard[tt % 2]])
                t_m1 = op("v", "tensor_scalar", ebuf, ebuf, 1.0, None, ALU.min,
                          after=[t_e])
                t_kf = op("v", "scalar_tensor_tensor", kfb[:, tt % 2, :], pfp[:],
                          0.0, ebuf, ALU.max, ALU.add,
                          after=[hard(t_m1), kf_guard[tt % 2]])
                bank[sb].read(t_kf)
                eb_guard[tt % 2] = t_kf
                if j == 0 and tt == 0:
                    dbg_point("bkf1", [t_kf])
                for hh in range(4):
                    for fc in range(2):
                        aft = [t_kf]
                        if not kv_started:
                            aft += (bank["B0"].for_write() + bank["B1"].for_write()
                                    + [v_ev[0]])
                        first_in_bank = (tt == 0 and hh % 2 == 0 and fc == 0)
                        kv_last = mm(pB[hh // 2][:, (hh % 2) * 130 + fc * 65:
                                                 (hh % 2) * 130 + fc * 65 + 65],
                                     kfb[:, tt % 2, PFCOL[hh] + fc * 128:
                                         PFCOL[hh] + fc * 128 + 128],
                                     rap(t_vp, 6208, 0, 128, tt * 260 + hh * 65,
                                         [[1, 65]]),
                                     first_in_bank,
                                     tt == NT - 1 and hh == 3 and fc == 1,
                                     after=aft)
                        kv_started = True
                kf_guard[tt % 2] = kv_last
                if j == 0 and tt == 0:
                    dbg_point("bkv2", [kv_last])
                if j == 0 and tt == 2:
                    dbg_point("bkv3", [kv_last])
            bank["B0"].wrote(kv_last)
            bank["B1"].wrote(kv_last)
            t_kv0 = op("v", "tensor_copy", kvsb[:, 0:2], pB[0][:, 0:260], after=[kv_last])
            t_kv = op("v", "tensor_copy", kvsb[:, 2:4], pB[1][:, 0:260], after=[kv_last])
            bank["B0"].read(t_kv0)
            bank["B1"].read(t_kv)
            if j == 0:
                dbg_point("bkv", [t_kv0, t_kv])
            out_guard = None
            for b in range(NB):
                qf_toks = []
                for hh in range(4):
                    sb = "A0" if hh % 2 == 0 else "A1"
                    dstp = pA[0] if hh % 2 == 0 else pA[1]
                    ebuf = xt[:] if hh % 2 == 0 else xt2[:]
                    last = None
                    for fc in range(2):
                        aft = [tok_wft0, tok_wft, qk_ev[0]]
                        if fc == 0:
                            aft += bank[sb].for_write()
                        last = mm(dstp[:, fc * 512:(fc + 1) * 512],
                                  rap(t_wft, FD, HHROW[hh], 64, fc * 128, [[1, 128]]),
                                  rap(t_qc, NCH * S, HHROW[hh], 64,
                                      HHCH[hh] * S + b * 512, [[1, 512]]),
                                  True, True, after=aft)
                    bank[sb].wrote(last)
                    t_e = op("s", "activation", ebuf, dstp[:], AFT.Exp,
                             after=[last, eb_guard[hh % 2]])
                    t_m1 = op("v", "tensor_scalar", ebuf, ebuf, 1.0, None, ALU.min,
                              after=[t_e])
                    t_qf = op("v", "scalar_tensor_tensor",
                              rap(t_qf4, 4096, 0, 128, hh * 1024, [[1, 1024]]),
                              dstp[:], 0.0, ebuf, ALU.max, ALU.add,
                              after=[hard(t_m1), out_guard])
                    bank[sb].read(t_qf)
                    eb_guard[hh % 2] = t_qf
                    qf_toks.append(t_qf)
                for t4 in range(4):
                    tt = b * 4 + t4
                    last = None
                    for hh in range(4):
                        for fc in range(2):
                            aft = [t_kv, qf_toks[hh]]
                            if hh == 0 and fc == 0:
                                aft += bank["B1"].for_write()
                            last = mm(pB[1][:, hh * 65:hh * 65 + 65],
                                      rap(t_qf4, 4096, 0, 128,
                                          hh * 1024 + fc * 512 + t4 * 128, [[1, 128]]),
                                      kvsb[:, hh, fc, :],
                                      fc == 0, fc == 1, after=aft)
                    bank["B1"].wrote(last)
                    out_guard = last
                    t_n = op("v", "tensor_scalar", rr[:, 0:4],
                             rap(h_pB1, 512, 0, 128, 64, [[65, 4]]),
                             1e-6, None, ALU.add, after=[last])
                    t_r = op("v", "reciprocal", rr[:, 4:8], rr[:, 0:4],
                             after=[hard(t_n)])
                    ot_toks = []
                    for hh in range(4):
                        blk, par2 = hh // 2, hh % 2
                        ot_toks.append(op("v", "tensor_scalar",
                                          otm[:, blk * 128 + par2 * 64:
                                              blk * 128 + par2 * 64 + 64],
                                          pB[1][:, hh * 65:hh * 65 + 64],
                                          rr[:, 4 + hh:5 + hh], None, ALU.mult,
                                          after=[hard(t_r)]))
                    bank["B1"].read(ot_toks[-1])
                    for blk in range(2):
                        db = "D0" if (tt * 2 + blk) % 2 == 0 else "D1"
                        dp = pD[0] if db == "D0" else pD[1]
                        dph = h_pD0 if db == "D0" else h_pD1
                        t_tp = tr(dp[:, 0:128], otm[:, blk * 128:(blk + 1) * 128],
                                  after=[ot_toks[2 * blk], ot_toks[2 * blk + 1]]
                                  + bank[db].for_write())
                        bank[db].wrote(t_tp)
                        ev = op("v", "tensor_copy",
                                rap(t_qc, NCH * S, 0, 128,
                                    HHCH[2 * blk] * S + tt * 128, [[1, 128]]),
                                rap(dph, 1024, 0, 128, 0, [[1, 128]]), after=[t_tp])
                        bank[db].read(ev)
                        last_out_ev = ev
            if j == 0:
                dbg_point("bout", [last_out_ev])
        tok_y = [None] * NT
        for tt in range(NT):
            for ch in range(2):
                bnm = "B0" if (tt * 2 + ch) % 2 == 0 else "B1"
                dstp = pB[0] if bnm == "B0" else pB[1]
                first_after = [last_out_ev] + wo_tok + bank[bnm].for_write()
                last = None
                for a in range(NCH):
                    last = mm(dstp[:, 0:512], qc[:, a, tt * 128:(tt + 1) * 128],
                              wB[:, a, ch * 512:(ch + 1) * 512],
                              a == 0, a == NCH - 1, after=(first_after if a == 0 else []))
                bank[bnm].wrote(last)
                t_add = op("v", "tensor_tensor", x[:, tt, ch * 512:(ch + 1) * 512],
                           dstp[:, 0:512], x[:, tt, ch * 512:(ch + 1) * 512], ALU.add,
                           after=[last])
                bank[bnm].read(t_add)
                tok_y[tt] = t_add
        wb_free[0] = last
        return tok_y

    wb_free = [None]
    toks = tok_x
    try:
        if n_sub >= 1:
            toks = emit_attn_a(toks)
        if n_sub >= 2:
            toks = emit_ffn(toks, False)
        if n_sub >= 3:
            toks = emit_attn_b(toks)
        if n_sub >= 4:
            toks = emit_ffn(toks, True)
    except StopBuild:
        pass

    st_toks = [B.dma("y", (lambda e, tt=tt: e.dma_start(
        out=dout[tt * 128:(tt + 1) * 128, :], in_=x[:, tt, :])), after=[toks[tt]])
        for tt in range(NT)]
    B.op("y", lambda e: e.nop(), after=st_toks + dbg_outs)

    import contextlib
    es = contextlib.ExitStack()
    B.finalize(nc, es)
    with nc.Block() as block:
        block.tensor(lambda e: B.emit("t", e))
        block.vector(lambda e: B.emit("v", e))
        block.scalar(lambda e: B.emit("s", e))
        block.gpsimd(lambda e: B.emit("g", e))
        block.sync(lambda e: B.emit("y", e))
    es.close()
    return nc


def _host_prep(inp):
    import jax
    import jax.numpy as jnp
    f = lambda a: np.ascontiguousarray(np.asarray(a, dtype=np.float32))
    perm = np.concatenate([np.arange(0, D, 2), np.arange(1, D, 2)])
    m = {}
    m["aWq"] = f(inp["L0_attn_a_Wq"])[:, perm]
    m["aWk"] = f(inp["L0_attn_a_Wk"])[:, perm]
    m["aWv"] = f(inp["L0_attn_a_Wv"])
    m["aWo"] = f(inp["L0_attn_a_Wo"])
    m["abq"] = np.ascontiguousarray(f(inp["L0_attn_a_bq"])[perm].reshape(8, 128).T)
    m["abk"] = np.ascontiguousarray(f(inp["L0_attn_a_bk"])[perm].reshape(8, 128).T)
    m["abv"] = f(inp["L0_attn_a_bv"]).reshape(1, D)
    m["abo"] = f(inp["L0_attn_a_bo"]).reshape(1, D)
    m["bWqT"] = np.ascontiguousarray(f(inp["L0_attn_b_Wq"]).T)
    m["bWkT"] = np.ascontiguousarray(f(inp["L0_attn_b_Wk"]).T)
    m["bWvT"] = np.ascontiguousarray(f(inp["L0_attn_b_Wv"]).T)
    m["bWoT"] = np.ascontiguousarray(f(inp["L0_attn_b_Wo"]).T)
    m["bWfT"] = np.ascontiguousarray(f(inp["L0_attn_b_Wf"]).T)
    m["fWq"] = f(inp["L0_ffn_Wq"])
    m["fWk"] = f(inp["L0_ffn_Wk"])
    m["fWv"] = f(inp["L0_ffn_Wv"])
    m["fWo"] = f(inp["L0_ffn_Wo"])
    m["fWs1"] = f(inp["L0_ffn_Ws1"])
    m["fbs1"] = np.ascontiguousarray(f(inp["L0_ffn_bs1"]).reshape(2, 128).T)
    m["fWs2"] = np.ascontiguousarray(f(inp["L0_ffn_Ws2"]).reshape(2, 128).T)
    m["fbs2"] = f(inp["L0_ffn_bs2"]).reshape(1, 1)
    cpu = jax.devices("cpu")[0]
    with jax.default_device(cpu):
        fr = jnp.arange(D // 2, dtype=jnp.float32)
        fc = jnp.power(10000.0, -2.0 * fr / D)
        pos = jnp.arange(S, dtype=jnp.float32)
        freq = jnp.einsum("i,j->ij", pos, fc)
        sin_e = np.asarray(jnp.sin(freq), dtype=np.float32)
        cos_e = np.asarray(jnp.cos(freq), dtype=np.float32)
    m["ropesin"] = np.ascontiguousarray(sin_e.T.reshape(4, 128, S))
    m["ropecos"] = np.ascontiguousarray(cos_e.T.reshape(4, 128, S))
    m["identc"] = np.eye(128, dtype=np.float32)
    m["lowtri"] = np.tril(np.ones((128, 128), dtype=np.float32), -1)
    ca = np.zeros((128, 128), dtype=np.float32)
    kk = np.arange(128)[:, None]
    qq = np.arange(128)[None, :]
    ca[qq < kk] = -1e30
    m["causadd"] = ca
    m["tri01"] = (qq >= kk).astype(np.float32)
    return m


_N_SUB = 4


def kernel(**inputs):
    xfull = np.asarray(inputs["L0_attn_a_x"], dtype=np.float32)
    common = _host_prep(inputs)
    nc = build_nc(_N_SUB)
    in_maps = []
    for b in range(8):
        im = dict(common)
        im["x_in"] = np.ascontiguousarray(xfull[b])
        in_maps.append(im)
    res = run_bass_kernel_spmd(nc, in_maps, core_ids=list(range(8)))
    return np.stack([np.asarray(res.results[b]["x_out"]) for b in range(8)], axis=0)


# version 05
# speedup vs baseline: 1.0483x; 1.0483x over previous
import numpy as np
import concourse.bass as bass
import concourse.mybir as mybir
import bass_rust
from concourse.alu_op_type import AluOpType as ALU
from concourse.bass_utils import run_bass_kernel_spmd

F32 = mybir.dt.float32
BF16 = mybir.dt.bfloat16
AFT = mybir.ActivationFunctionType
AXX = mybir.AxisListType.X

S, D, H, HD, FD = 2048, 1024, 16, 64, 256
NT, NB, NCH = 16, 4, 8
ENGS = ("t", "v", "s", "g", "y")


class Sched:
    def __init__(self):
        self.items = {e: [] for e in ENGS}
        self.ndma = {e: 0 for e in ENGS}

    def op(self, e, fn, after=()):
        self.items[e].append([[t for t in after if t is not None], fn, False, False])
        return ("i", e, len(self.items[e]) - 1)

    def dma(self, e, fn, after=()):
        self.items[e].append([[t for t in after if t is not None], fn, False, True])
        self.ndma[e] += 1
        return ("d", e, self.ndma[e] * 16)

    def finalize(self, nc, es):
        for e in ENGS:
            for it in self.items[e]:
                for tok in it[0]:
                    if tok[0] in ("i", "h"):
                        assert not self.items[tok[1]][tok[2]][3]
                        self.items[tok[1]][tok[2]][2] = True
        self.val = {}
        for e in ENGS:
            c, vals = 0, []
            for it in self.items[e]:
                if it[2]:
                    c += 1
                vals.append(c)
            self.val[e] = vals
        self.psem = {e: es.enter_context(nc.semaphore(f"p_{e}")) for e in ENGS}
        self.dsem = {e: es.enter_context(nc.semaphore(f"d_{e}"))
                     for e in ENGS if self.ndma[e]}

    def emit(self, e, eng):
        waited = {}
        for waits, fn, inc, is_dma in self.items[e]:
            for tok in waits:
                if tok[0] in ("i", "h"):
                    pe, idx = tok[1], tok[2]
                    if pe == e and tok[0] == "i":
                        continue
                    v = self.val[pe][idx]
                    if waited.get(("i", pe), 0) < v:
                        eng.wait_ge(self.psem[pe], v)
                        waited[("i", pe)] = v
                else:
                    qe, v = tok[1], tok[2]
                    if waited.get(("d", qe), 0) < v:
                        eng.wait_ge(self.dsem[qe], v)
                        waited[("d", qe)] = v
            inst = fn(eng)
            if is_dma:
                inst.then_inc(self.dsem[e], 16)
            elif inc:
                inst.then_inc(self.psem[e], 1)


def hard(tok):
    assert tok[0] == "i"
    return ("h", tok[1], tok[2])


class Bank:
    def __init__(self):
        self.w = None
        self.r = None

    def for_write(self):
        out = [t for t in (self.w, self.r) if t is not None]
        return out

    def wrote(self, tok):
        self.w, self.r = tok, None

    def for_read(self):
        return [self.w] if self.w is not None else []

    def read(self, tok):
        self.r = tok


class StopBuild(Exception):
    pass


def build_nc(n_sub=4, dbg=None):
    nc = bass.Bass("TRN2", target_bir_lowering=False)
    B = Sched()

    def op(eng, meth, *args, after=(), **kw):
        return B.op(eng, (lambda e, m=meth, a=args, k=kw: getattr(e, m)(*a, **k)),
                    after=after)

    def dma(q, dst, src, after=()):
        return B.dma(q, (lambda e, d=dst, s=src: e.dma_start(out=d, in_=s)), after=after)

    dbg_outs = []

    def dbg_point(label, toks):
        if dbg != label:
            return
        dumps = [("xn", t_xn, 16384, BF16), ("qc", t_qc, 16384, BF16),
                 ("kp", t_kp, 4096, BF16), ("vp", t_vp, 6208, BF16),
                 ("et", t_et, 2048, BF16), ("scrf", t_scrf, 1024, F32),
                 ("xt", t_xt, 1024, BF16), ("st", t_st, 16, F32),
                 ("gtm", t_fgtm, 1024, F32), ("srow", t_fsrow, 128, F32),
                 ("mq", t_fmq, 8, F32), ("madd", t_fmadd, 128, F32),
                 ("og", t_fog, 1024, BF16), ("ctxf", t_fctx, 1024, BF16),
                 ("hh", t_fh, 256, F32), ("gfm", t_fgfm, 2048, BF16),
                 ("cmp", t_fcmp, 128, F32), ("cnt", t_fcnt, 8, F32), ("scol", t_fscol, 8, F32),
                 ("w1h", t_fw1h, 4096, BF16)]
        for nm, th, n, dt in dumps:
            o = nc.dram_tensor("dbg_" + nm, [128, n], F32, kind="ExternalOutput").ap()
            srcap = bass_rust.AP(th, 0, [[n, 128], [1, n]])
            dbg_outs.append(B.dma("g", (lambda e, oo=o, ss=srcap: e.dma_start(out=oo, in_=ss)),
                                  after=[t for t in toks if t is not None]))
        raise StopBuild()

    def mm(dst, lhs, rhs, start, stop, after=(), tp=None):
        return B.op("t", (lambda e, d=dst, l=lhs, r=rhs, st=start, sp=stop, t=tp:
                          e.matmul(d, l, r, start=st, stop=sp, tile_position=t)),
                    after=after)

    def tr(dst, src, after=()):
        return B.op("t", (lambda e, d=dst, s=src: e.transpose(d, s, identb_late[0])),
                    after=after)

    DI = lambda name, shape: nc.dram_tensor(name, shape, F32, kind="ExternalInput").ap()
    dx = DI("x_in", [S, D])
    dout = nc.dram_tensor("x_out", [S, D], F32, kind="ExternalOutput").ap()
    d_aWq, d_aWk, d_aWv, d_aWo = DI("aWq", [D, D]), DI("aWk", [D, D]), DI("aWv", [D, D]), DI("aWo", [D, D])
    d_abq, d_abk = DI("abq", [128, 8]), DI("abk", [128, 8])
    d_abv, d_abo = DI("abv", [1, D]), DI("abo", [1, D])
    d_bWq, d_bWk, d_bWv, d_bWo = DI("bWqT", [D, D]), DI("bWkT", [D, D]), DI("bWvT", [D, D]), DI("bWoT", [D, D])
    d_bWf = DI("bWfT", [HD, FD])
    d_fWq, d_fWk, d_fWv, d_fWo = DI("fWq", [D, D]), DI("fWk", [D, D]), DI("fWv", [D, D]), DI("fWo", [D, D])
    d_fWs1 = DI("fWs1", [D, 256])
    d_fbs1, d_fWs2, d_fbs2 = DI("fbs1", [128, 2]), DI("fWs2", [128, 2]), DI("fbs2", [1, 1])
    d_sin, d_cos = DI("ropesin", [4, 128, S]), DI("ropecos", [4, 128, S])
    d_ident, d_lowtri = DI("identc", [128, 128]), DI("lowtri", [128, 128])
    d_causadd = DI("causadd", [128, 128])
    d_tri01 = DI("tri01", [128, 128])

    _probe = nc.alloc_sbuf_tensor("sb_probe", [128, 8], F32)
    SB_BASE = nc.lookup_mloc(_probe).addr
    assert SB_BASE <= 16512 + 4096, SB_BASE
    cur = [SB_BASE]

    def alloc(name, shape, dtype, at=None):
        esz = 2 if dtype == BF16 else 4
        n = 1
        for d_ in shape[1:]:
            n *= d_
        if at is None:
            off = (cur[0] + 31) // 32 * 32
            cur[0] = off + n * esz
            assert cur[0] <= SB_BASE + 212800, f"SBUF overflow at {name}: {cur[0]}"
        else:
            off = at
        return nc.alloc_sbuf_tensor_at(name, shape, dtype, offset=off)

    t_x = alloc("xres", [128, NT, D], F32)
    QC_OFF = cur[0]
    t_qc = alloc("qc", [128, NCH, S], BF16)
    XN_OFF = cur[0]
    t_xn = alloc("xn", [128, NCH, S], BF16)
    t_kp = alloc("kp", [128, 2, S], BF16)
    t_vp = alloc("vp", [128, 6208], BF16)
    TBL_OFF = (cur[0] + 31) // 32 * 32
    t_tbl = alloc("tbl", [128, 2, 2, 512], F32)
    t_wq = alloc("wq_s", [128, 8, 2, 128], BF16)
    t_wk = alloc("wk_s", [128, 8, 2, 128], BF16)
    t_wv = alloc("wv_s", [128, 8, 2, 128], BF16)
    t_wB = alloc("wB", [128, 8, 1024], BF16)
    t_wsl = alloc("wsl", [128, 2, 8, 128], BF16)
    WSL_OFF = cur[0] - 4096
    t_bca = alloc("bca", [128, 512], F32, at=WSL_OFF)
    t_bias = alloc("biasrows", [128, 2, 1024], BF16)
    t_bqc = alloc("bq_col", [128, 8], F32)
    t_bkc = alloc("bk_col", [128, 8], F32)
    SCRF_OFF = (cur[0] + 31) // 32 * 32
    t_scrf = alloc("scr_f32", [128, 1024], F32)
    t_xt = alloc("xt_tile", [128, 1024], BF16)
    t_xe = alloc("xt2", [128, 1024], BF16, at=WSL_OFF + 2048)
    t_et = alloc("et", [128, 2, 1024], BF16)
    ET_OFF = cur[0] - 4096
    t_rr = alloc("rr", [128, 512], F32)
    t_st = alloc("stats", [128, 16], F32)
    t_id = alloc("identb", [128, 128], BF16)
    t_of = alloc("onesf", [128, 128], F32)
    t_ob = alloc("onesb", [128, 128], BF16)
    t_b1 = alloc("bs1c", [128, 2], F32)
    t_w2 = alloc("ws2c", [128, 2], F32)
    t_w2h = alloc("ws2hl", [128, 4], BF16)
    t_b2 = alloc("bs2c", [128, 1], F32)
    t_cm = alloc("causm", [128, 128], F32)
    t_tb = alloc("trib", [128, 128], BF16)
    t_eg = alloc("egen", [128, 128], BF16)
    t_pgn = alloc("pgen", [128, 128], BF16)
    t_cc = alloc("constcol", [128, 8], F32)
    t_qf4 = alloc("qfb4", [128, 4, 2, 512], BF16, at=TBL_OFF)
    t_kf = alloc("kfb", [128, 2, 1024], BF16, at=ET_OFF)
    t_kv = alloc("kvsb", [128, 4, 2, 65], BF16, at=SCRF_OFF)
    t_ot = alloc("otm", [128, 256], BF16, at=SCRF_OFF + 2112)
    t_wft = alloc("wftb", [128, FD], BF16, at=SCRF_OFF + 2624)
    t_fws1 = alloc("f_ws1", [128, 8, 256], F32, at=QC_OFF)
    t_fw1h = alloc("f_w1h", [128, 2, 8, 256], BF16, at=QC_OFF + 8192)
    t_fgtm = alloc("f_gtm", [128, 1024], F32, at=QC_OFF + 16384)
    t_fghl = alloc("f_ghl", [128, 2, 1024], BF16, at=QC_OFF + 20480)
    t_fgfm = alloc("f_gfm", [128, 2, 8, 128], BF16, at=QC_OFF + 24576)
    t_fh = alloc("f_h", [128, 2, 128], F32, at=QC_OFF + 28672)
    t_fhhl = alloc("f_hhl", [128, 2, 2, 128], BF16, at=QC_OFF + 29696)
    t_fsrow = alloc("f_srow", [128, 128], F32, at=QC_OFF + 30720)
    t_fscol = alloc("f_scol", [128, 8], F32, at=QC_OFF + 31232)
    t_fcnt = alloc("f_cnt", [128, 8], F32, at=QC_OFF + 31264)
    t_fmq = alloc("f_mq", [128, 8], F32, at=QC_OFF + 31296)
    t_fmqb = alloc("f_mqb", [128, 16], BF16, at=QC_OFF + 31328)
    t_fmrow = alloc("f_mrow", [128, 128], F32, at=QC_OFF + 31360)
    t_fcmp = alloc("f_cmp", [128, 128], F32, at=XN_OFF)
    t_fmadd = alloc("f_madd", [128, 128], F32, at=XN_OFF + 512)
    t_fqg = alloc("f_qg", [128, 8, 128], BF16, at=XN_OFF + 1024)
    t_fkg = alloc("f_kg", [128, 8, 128], BF16, at=XN_OFF + 3072)
    t_fvg = alloc("f_vg", [128, 8, 128], BF16, at=XN_OFF + 5120)
    t_fctx = alloc("f_ctx", [128, 8, 128], BF16, at=XN_OFF + 7168)
    t_fog = alloc("f_og", [128, 1024], BF16, at=XN_OFF + 9216)
    t_fsm = alloc("f_sm", [128, 128], F32, at=XN_OFF + 11264)
    t_fpb = alloc("f_pb", [128, 128], BF16, at=XN_OFF + 11776)
    t_fpt = alloc("f_pt", [128, 128], BF16, at=XN_OFF + 12032)
    t_fxlo = alloc("f_xlo", [128, 1024], BF16, at=XN_OFF + 12288)
    t_lt = alloc("lowtrib", [128, 128], F32, at=XN_OFF + 14336)
    t_eg16 = alloc("egen16", [128, NT, 128], BF16, at=XN_OFF + 16384)
    t_pg16 = alloc("pgen16", [128, NT, 128], BF16, at=XN_OFF + 20480)

    h_pA0 = nc.alloc_psum_tensor("pA0", [128, 1024], F32)
    h_pA1 = nc.alloc_psum_tensor("pA1", [128, 1024], F32)
    h_pB0 = nc.alloc_psum_tensor("pB0", [128, 512], F32)
    h_pB1 = nc.alloc_psum_tensor("pB1", [128, 512], F32)
    h_pD0 = nc.alloc_psum_tensor("pD0", [128, 1024], BF16)
    h_pD1 = nc.alloc_psum_tensor("pD1", [128, 1024], BF16)
    pA = [h_pA0.ap(), h_pA1.ap()]
    pB = [h_pB0.ap(), h_pB1.ap()]
    pD = [h_pD0.ap(), h_pD1.ap()]
    pDf = [h_pD0.ap().bitcast(F32), h_pD1.ap().bitcast(F32)]
    bank = {"A0": Bank(), "A1": Bank(), "B0": Bank(), "B1": Bank(),
            "D0": Bank(), "D1": Bank()}

    identb_late = [None]
    x = t_x.ap(); qc = t_qc.ap(); xn = t_xn.ap(); kp = t_kp.ap(); vp = t_vp.ap()
    tbl = t_tbl.ap(); wq_s = t_wq.ap(); wk_s = t_wk.ap(); wv_s = t_wv.ap()
    wB = t_wB.ap(); wsl = t_wsl.ap(); biasr = t_bias.ap()
    bqc = t_bqc.ap(); bkc = t_bkc.ap(); scrf = t_scrf.ap(); xt = t_xt.ap()
    xlo = t_fxlo.ap(); et = t_et.ap(); rr = t_rr.ap(); bca = t_bca.ap()
    xt2 = t_xe.ap()
    stats = t_st.ap(); identb = t_id.ap(); lowtri = t_lt.ap(); wft = t_wft.ap()
    identb_late[0] = identb[:]
    onesf = t_of.ap(); onesb = t_ob.ap()
    bs1c = t_b1.ap(); ws2c = t_w2.ap(); ws2hl = t_w2h.ap(); bs2c = t_b2.ap()
    causm = t_cm.ap(); trib = t_tb.ap()
    egen = t_eg.ap(); pgen = t_pgn.ap(); kvsb = t_kv.ap(); otm = t_ot.ap()
    qfb4 = t_qf4.ap(); kfb = t_kf.ap()
    f_ws1 = t_fws1.ap(); f_w1h = t_fw1h.ap(); f_gtm = t_fgtm.ap(); f_ghl = t_fghl.ap()
    f_gfm = t_fgfm.ap(); f_h = t_fh.ap(); f_hhl = t_fhhl.ap(); f_srow = t_fsrow.ap()
    f_scol = t_fscol.ap(); f_cnt = t_fcnt.ap(); f_mq = t_fmq.ap(); f_mqb = t_fmqb.ap()
    f_mrow = t_fmrow.ap(); f_cmp = t_fcmp.ap(); f_madd = t_fmadd.ap()
    f_qg = t_fqg.ap(); f_kg = t_fkg.ap(); f_vg = t_fvg.ap(); f_ctx = t_fctx.ap()
    f_og = t_fog.ap(); f_sm = t_fsm.ap(); f_pb = t_fpb.ap(); f_pt = t_fpt.ap()

    def rap(th, rowext, pstart, nparts, foff, dims):
        return bass_rust.AP(th, pstart * rowext + foff, [[rowext, nparts]] + dims)

    VA_TT = 386
    VA_OFF = [0, 65, 193, 258]
    VA_W = [65, 128, 65, 128]

    tok_x = [dma("y", x[:, tt, :], dx[tt * 128:(tt + 1) * 128, :]) for tt in range(NT)]
    tok_id = dma("g", identb[:], d_ident[:])
    tok_cau = dma("y", causm[:], d_causadd[:])
    tok_tri = dma("g", trib[:], d_tri01[:])
    tok_bqc = dma("y", bqc[:], d_abq[:])
    tok_bkc = dma("y", bkc[:], d_abk[:])
    tok_bvr = dma("g", biasr[0:1, 0, :], d_abv[:])
    tok_bor = dma("g", biasr[0:1, 1, :], d_abo[:])
    tok_b1 = dma("y", bs1c[:], d_fbs1[:])
    tok_w2 = dma("y", ws2c[:], d_fWs2[:])
    tok_b2 = dma("y", bs2c[0:1, 0:1], d_fbs2[:])
    tok_onesf = op("v", "memset", onesf[:], 1.0)
    tok_onesb = op("v", "memset", onesb[:], 1.0)
    constcol = t_cc.ap()
    tok_cc = op("v", "memset", constcol[:, 0:1], -16.0)
    tok_cc2 = op("v", "memset", constcol[:, 1:2], 63.5)

    def wslab_load(dst, dsrc, j, after=()):
        src4 = dsrc.rearrange("(a p) (c n) -> p a c n", p=128, n=128)
        toks = []
        for slot, jj in enumerate((j, j + 4)):
            toks.append(dma("g", dst[:, :, slot, :], src4[:, :, jj, :], after=after))
        return toks

    def wfull_load(dst, dsrc, after=()):
        return [dma("g", dst[:], dsrc.rearrange("(a p) n -> p a n", p=128), after=after)]

    def emit_ln_tile(tt, after_tok, dump_ap, dump_bank, want_f32, apply_after=()):
        so = (tt % 2) * 8
        xtile = rap(t_x, NT * D, 0, 128, tt * D, [[1, D]])
        t_r = op("v", "tensor_reduce", stats[:, so:so + 1], xtile, AXX, ALU.add,
                 negate=True, after=[after_tok])
        t_nm = op("s", "mul", stats[:, so + 1:so + 2], stats[:, so:so + 1], 1.0 / D,
                  after=[t_r])
        t_sq = op("s", "activation", dump_ap, xtile, AFT.Square,
                  bias=stats[:, so + 1:so + 2], scale=1.0,
                  accum_out=stats[:, so + 2:so + 3],
                  after=[hard(t_nm)] + (bank[dump_bank].for_write() if dump_bank else []))
        if dump_bank:
            bank[dump_bank].wrote(t_sq)
        t_var = op("v", "tensor_scalar", stats[:, so + 3:so + 4], stats[:, so + 2:so + 3],
                   1.0 / D, 1e-5, ALU.mult, ALU.add, after=[t_sq])
        t_sqrt = op("s", "activation", stats[:, so + 4:so + 5], stats[:, so + 3:so + 4],
                    AFT.Sqrt, after=[t_var])
        t_rstd = op("v", "reciprocal", stats[:, so + 5:so + 6], stats[:, so + 4:so + 5],
                    after=[t_sqrt])
        dst = scrf[:] if want_f32 else xt[:]
        t_ap = op("v", "tensor_scalar", dst, xtile, stats[:, so + 1:so + 2],
                  stats[:, so + 5:so + 6], ALU.add, ALU.mult,
                  after=[hard(t_rstd)] + list(apply_after))
        return t_ap

    def emit_xn_for_attn(tok_res):
        tok_xn = [None] * NT
        xt_guard = [None]
        for tt in range(NT):
            dmp = "A0" if tt % 2 == 0 else "A1"
            t_ap = emit_ln_tile(tt, tok_res[tt], pA[tt % 2][:], dmp, False,
                                apply_after=[xt_guard[0]])
            last_ev = None
            for half in range(2):
                db = "D0" if half == 0 else "D1"
                tts = []
                for c4 in range(4):
                    c = half * 4 + c4
                    tts.append(tr(pD[half][:, c4 * 128:(c4 + 1) * 128],
                                  xt[:, c * 128:(c + 1) * 128],
                                  after=[t_ap, tok_id] + (bank[db].for_write() if c4 == 0 else [])))
                bank[db].wrote(tts[-1])
                dstap = rap(t_xn, NCH * S, 0, 128, (half * 4) * S + tt * 128,
                            [[S, 4], [1, 128]])
                srcap = rap(h_pD0 if half == 0 else h_pD1, 1024, 0, 128, 0,
                            [[128, 4], [1, 128]])
                ev = op("v", "tensor_copy", dstap, srcap, after=[tts[-1]])
                bank[db].read(ev)
                last_ev = ev
                xt_guard[0] = tts[-1]
            tok_xn[tt] = last_ev
        return tok_xn

    def emit_attn_a(tok_res):
        tok_xn = emit_xn_for_attn(tok_res)
        dbg_point("xn", tok_xn)
        vp_init = []
        for hh in range(4):
            if hh % 2 == 0:
                vp_init.append(op("v", "memset",
                                  rap(t_vp, 6208, 0, 128, VA_OFF[hh] + 64,
                                      [[VA_TT, NT], [1, 1]]), 1.0))
            else:
                vp_init.append(op("v", "memset",
                                  rap(t_vp, 6208, 0, 128, VA_OFF[hh],
                                      [[VA_TT, NT], [1, 1]]), 1.0))
                vp_init.append(op("v", "memset",
                                  rap(t_vp, 6208, 0, 128, VA_OFF[hh] + 1,
                                      [[VA_TT, NT], [1, 63]]), 0.0))
        wo_tok = wfull_load(wB, d_aWo)
        PS = [pB[0][:, 0:512], pB[1][:, 0:512], pA[0][:, 0:512], pA[1][:, 0:512]]
        PSB = ["B0", "B1", "A0", "A1"]
        tbl_cons = [None, None]
        slab_tok = {"q": wslab_load(wq_s, d_aWq, 0),
                    "k": wslab_load(wk_s, d_aWk, 0),
                    "v": wslab_load(wv_s, d_aWv, 0)}
        slab_cons = {"q": None, "k": None, "v": None}
        et_cons = [None, None]
        last_ctx_ev = None
        rope_done = [None]
        v_ev_last = [None]

        for j in range(4):
            for phase in ("q", "k"):
                slab_t = {"q": t_wq, "k": t_wk}[phase]
                bcol = bqc if phase == "q" else bkc
                btok = tok_bqc if phase == "q" else tok_bkc
                for b in range(NB):
                    sslot = b % 2
                    t_s = dma("y", tbl[:, sslot, 0, :], d_sin[j, :, b * 512:(b + 1) * 512],
                              after=[tbl_cons[sslot]])
                    t_c = dma("y", tbl[:, sslot, 1, :], d_cos[j, :, b * 512:(b + 1) * 512],
                              after=[tbl_cons[sslot]])
                    si = 2 * (b % 2)
                    mm_t = []
                    for ci in range(2):
                        dst, bnm = PS[si + ci], PSB[si + ci]
                        first_after = ([tok_xn[4 * b + 3]] + slab_tok[phase]
                                       + bank[bnm].for_write())
                        last = None
                        for a in range(NCH):
                            last = mm(dst, rap(slab_t, 2048, 0, 128, a * 256 + ci * 128,
                                               [[1, 128]]),
                                      xn[:, a, b * 512:(b + 1) * 512],
                                      a == 0, a == NCH - 1,
                                      after=(first_after if a == 0 else []))
                        bank[bnm].wrote(last)
                        mm_t.append(last)
                    slab_cons[phase] = mm_t[-1]
                    Pe, Po = PS[si], PS[si + 1]
                    sin_ap, cos_ap = tbl[:, sslot, 0, :], tbl[:, sslot, 1, :]
                    if phase == "q":
                        d_lo = qc[:, j, b * 512:(b + 1) * 512]
                        d_hi = qc[:, j + 4, b * 512:(b + 1) * 512]
                    else:
                        d_lo = kp[:, 0, b * 512:(b + 1) * 512]
                        d_hi = kp[:, 1, b * 512:(b + 1) * 512]
                    be, bo_ = bcol[:, j:j + 1], bcol[:, j + 4:j + 5]
                    s1, s2 = scrf[:, 0:512], scrf[:, 512:1024]
                    op("v", "scalar_tensor_tensor", s1, Pe, be, cos_ap, ALU.add, ALU.mult,
                       after=[mm_t[0], t_c, btok])
                    op("v", "scalar_tensor_tensor", s2, Po, bo_, sin_ap, ALU.add, ALU.mult,
                       after=[mm_t[1], t_s])
                    op("v", "tensor_sub", d_lo, s1, s2)
                    t3 = op("v", "scalar_tensor_tensor", s1, Pe, be, sin_ap, ALU.add, ALU.mult)
                    t4 = op("v", "scalar_tensor_tensor", s2, Po, bo_, cos_ap, ALU.add, ALU.mult)
                    t_hi = op("v", "tensor_sub", d_hi, s1, s2)
                    tbl_cons[sslot] = t_hi
                    rope_done[0] = t_hi
                    bank[PSB[si]].read(t3)
                    bank[PSB[si + 1]].read(t4)
                if j < 3:
                    slab_tok[phase] = wslab_load(
                        {"q": wq_s, "k": wk_s}[phase],
                        {"q": d_aWq, "k": d_aWk}[phase], j + 1,
                        after=[slab_cons[phase]])
            dbg_point("qk", [rope_done[0]])
            for tt in range(NT):
                bnm = "B0" if tt % 2 == 0 else "B1"
                dstp = pB[tt % 2][:, 0:256]
                first_after = ([tok_xn[tt]] + slab_tok["v"] + vp_init
                               + bank[bnm].for_write())
                last = None
                for a in range(NCH):
                    last = mm(dstp, xn[:, a, tt * 128:(tt + 1) * 128], wv_s[:, a],
                              a == 0, False, after=(first_after if a == 0 else []))
                last = mm(dstp, onesb[0:1, 0:128],
                          rap(t_bias, 2048, 0, 1, j * 128, [[512, 2], [1, 128]]),
                          False, True, after=[tok_bvr, tok_onesb])
                bank[bnm].wrote(last)
                slab_cons["v"] = last
                pbh = h_pB0 if tt % 2 == 0 else h_pB1
                ev_e = op("v", "tensor_copy",
                          rap(t_vp, 6208, 0, 128, tt * VA_TT, [[193, 2], [1, 64]]),
                          rap(pbh, 512, 0, 128, 0, [[128, 2], [1, 64]]), after=[last])
                ev_o = op("v", "tensor_copy",
                          rap(t_vp, 6208, 0, 128, tt * VA_TT + 65 + 64, [[193, 2], [1, 64]]),
                          rap(pbh, 512, 0, 128, 64, [[128, 2], [1, 64]]))
                bank[bnm].read(ev_o)
                v_ev_last[0] = ev_o
            if j < 3:
                slab_tok["v"] = wslab_load(wv_s, d_aWv, j + 1, after=[slab_cons["v"]])
            dbg_point("v", [v_ev_last[0]])

            for hp in range(2):
                lanes = [2 * hp, 2 * hp + 1]
                qch = [j if hh < 2 else j + 4 for hh in lanes]
                ksl = [0 if hh < 2 else 1 for hh in lanes]
                for qt in range(NB):
                    nkt = 4 * qt + 4
                    ctx_started = [False, False]
                    pv_lane = [None, None]
                    exp_lane = [None, None]
                    for p in range((nkt + 1) // 2):
                        smm_lane = [None, None]
                        msk_lane = [[], []]
                        for L in range(2):
                            hh = lanes[L]
                            prow = 64 * L
                            sreg = pA[L]
                            sb = "A0" if L == 0 else "A1"
                            smm = []
                            for sub in range(2):
                                kt = 2 * p + sub
                                lhs = rap(t_kp, 2 * S, prow, 64,
                                          ksl[L] * S + kt * 128, [[1, 128]])
                                rhs = rap(t_qc, NCH * S, prow, 64,
                                          qch[L] * S + qt * 512, [[1, 512]])
                                smm.append(mm(sreg[:, sub * 512:(sub + 1) * 512], lhs,
                                              rhs, True, True,
                                              after=(bank[sb].for_write()
                                                     + [rope_done[0], v_ev_last[0]]
                                                     if sub == 0 else [])))
                            bank[sb].wrote(smm[-1])
                            smm_lane[L] = smm
                        for L in range(2):
                            sb = "A0" if L == 0 else "A1"
                            t_exp = op("s", "activation", et[:, L, :], pA[L][:],
                                       AFT.Exp, bias=constcol[:, 0:1], scale=0.125,
                                       after=[smm_lane[L][-1], et_cons[L], tok_cc])
                            bank[sb].read(t_exp)
                            exp_lane[L] = t_exp
                        for L in range(2):
                            for sub in range(2):
                                kt = 2 * p + sub
                                d_ = kt - 4 * qt
                                if d_ >= 0:
                                    sl = et[:, L, sub * 512 + d_ * 128:
                                            sub * 512 + d_ * 128 + 128]
                                    msk_lane[L].append(op("v", "tensor_tensor", sl, sl,
                                                          trib[:], ALU.mult,
                                                          after=[exp_lane[L], tok_tri]))
                        for L in range(2):
                            hh = lanes[L]
                            ctxp = pB[L]
                            cb = "B0" if L == 0 else "B1"
                            pv_last = None
                            for sub in range(2):
                                kt = 2 * p + sub
                                d_ = kt - 4 * qt
                                doff = max(0, d_) * 128
                                if L == 0:
                                    lhsv = rap(t_vp, 6208, 0, 128,
                                               kt * VA_TT + VA_OFF[hh], [[1, 65]])
                                    dstc = ctxp[0:65, doff:512]
                                else:
                                    lhsv = rap(t_vp, 6208, 0, 128,
                                               kt * VA_TT + VA_OFF[hh], [[1, 128]])
                                    dstc = ctxp[:, doff:512]
                                aft = [exp_lane[L]] + msk_lane[L]
                                if not ctx_started[L]:
                                    aft += bank[cb].for_write()
                                pv_last = mm(dstc, lhsv,
                                             et[:, L, sub * 512 + doff:(sub + 1) * 512],
                                             not ctx_started[L], kt == nkt - 1,
                                             after=aft)
                                ctx_started[L] = True
                            et_cons[L] = pv_last
                            pv_lane[L] = pv_last
                        if dbg == "et" and hp == 0 and qt == 0 and p == 0:
                            dbg_point("et", pv_lane)
                    for L in range(2):
                        hh = lanes[L]
                        ctxp = pB[L]
                        cb = "B0" if L == 0 else "B1"
                        bank[cb].wrote(pv_lane[L])
                        rowd = 64 if L == 0 else 0
                        t_ln = op("s", "activation", rr[rowd:rowd + 1, :],
                                  ctxp[rowd:rowd + 1, :], AFT.Ln, after=[pv_lane[L]])
                        t_rc = op("s", "activation", rr[rowd:rowd + 1, :],
                                  rr[rowd:rowd + 1, :], AFT.Exp, scale=-1.0,
                                  after=[hard(t_ln)])
                        db = "D0" if L == 0 else "D1"
                        dpf = pDf[L]
                        if L == 0:
                            t_bc = mm(dpf[0:64, :], onesf[64:65, 0:64], rr[64:65, :],
                                      True, True,
                                      after=[t_rc, tok_onesf] + bank[db].for_write())
                            rows = slice(0, 64)
                        else:
                            t_bc = mm(dpf[64:128, :], onesf[0:1, 0:64], rr[0:1, :],
                                      True, True,
                                      after=[t_rc, tok_onesf] + bank[db].for_write(),
                                      tp=(0, 64))
                            rows = slice(64, 128)
                        bank[db].wrote(t_bc)
                        t_be = op("s", "copy", bca[rows, :], dpf[rows, :], after=[t_bc])
                        bank[db].read(t_be)
                        dst = rap(t_qc, NCH * S, rows.start, 64,
                                  qch[L] * S + qt * 512, [[1, 512]])
                        srcc = ctxp[rows, 0:512] if L == 1 else ctxp[0:64, 0:512]
                        t_ev = op("v", "tensor_tensor", dst, srcc, bca[rows, :],
                                  ALU.mult, after=[t_be])
                        bank[cb].read(t_ev)
                        last_ctx_ev = t_ev
                        if dbg == "ctx" and j == 0 and hh == 0 and qt == 0:
                            dbg_point("ctx", [t_ev])

        tok_y = [None] * NT
        for tt in range(NT):
            for ch in range(2):
                bnm = "B0" if (tt * 2 + ch) % 2 == 0 else "B1"
                dstp = pB[0] if bnm == "B0" else pB[1]
                first_after = [last_ctx_ev] + wo_tok + bank[bnm].for_write()
                last = None
                for a in range(NCH):
                    last = mm(dstp[:, 0:512], qc[:, a, tt * 128:(tt + 1) * 128],
                              wB[:, a, ch * 512:(ch + 1) * 512],
                              a == 0, False, after=(first_after if a == 0 else []))
                last = mm(dstp[:, 0:512], onesb[0:1, 0:128],
                          rap(t_bias, 2048, 0, 1, 1024 + ch * 512, [[1, 512]]),
                          False, True, after=[tok_bor])
                bank[bnm].wrote(last)
                t_add = op("v", "tensor_tensor", x[:, tt, ch * 512:(ch + 1) * 512],
                           dstp[:, 0:512], x[:, tt, ch * 512:(ch + 1) * 512], ALU.add,
                           after=[last])
                bank[bnm].read(t_add)
                tok_y[tt] = t_add
        wb_free[0] = last
        return tok_y

    def emit_ffn(tok_res, second):
        eg16 = t_eg16.ap()
        pg16 = t_pg16.ap()
        eg_toks = []
        prev = tok_res[0]
        for tt in range(NT):
            eview = rap(t_eg16, NT * 128, 0, 128, tt * 128, [[16, 8], [1, 16]])
            t0 = op("g", "memset", eview, 1.0, after=[prev])
            t1 = op("g", "affine_select", out=eview, in_=eview,
                    pattern=[[-1, 8], [0, 16]], compare_op=ALU.is_ge, fill=0.0,
                    base=-8 * tt, channel_multiplier=1, after=[t0])
            t2 = op("g", "affine_select", out=eview, in_=eview,
                    pattern=[[1, 8], [0, 16]], compare_op=ALU.is_ge, fill=0.0,
                    base=8 * tt, channel_multiplier=-1, after=[t1])
            prev = t2
            eg_toks.append(t2)
        pg_toks = []
        for tt in range(NT):
            db = "D0" if tt % 2 == 0 else "D1"
            dp = pD[0] if db == "D0" else pD[1]
            t_tp = tr(dp[:, 0:128], eg16[:, tt, :],
                      after=[eg_toks[tt], tok_id] + bank[db].for_write())
            bank[db].wrote(t_tp)
            t_pg = op("v", "tensor_scalar", pg16[:, tt, :], dp[:, 0:128], 1.0 / 16.0,
                      None, ALU.mult, after=[t_tp])
            bank[db].read(t_pg)
            pg_toks.append(t_pg)

        pool_last = None
        for tt in range(NT):
            t_ap = emit_ln_tile(tt, tok_res[tt], pA[1][:], "A1", True)
            t_hi = op("v", "tensor_copy", xt[:], scrf[:], after=[hard(t_ap), pool_last])
            t_lo = op("v", "tensor_sub", xlo[:], scrf[:], xt[:], after=[hard(t_hi)])
            for src, src_tok in ((xt, t_hi), (xlo, t_lo)):
                for ch in range(2):
                    aft = [pg_toks[tt], src_tok]
                    if tt == 0 and src is xt and ch == 0:
                        aft += bank["A0"].for_write()
                    pool_last = mm(pA[0][:, ch * 512:(ch + 1) * 512], pg16[:, tt, :],
                                   src[:, ch * 512:(ch + 1) * 512],
                                   tt == 0 and src is xt,
                                   tt == NT - 1 and src is xlo and ch == 1,
                                   after=aft)
        bank["A0"].wrote(pool_last)
        tok_lt = dma("y", lowtri[:], d_lowtri[:], after=[pool_last])
        tok_ws1 = dma("y", f_ws1[:], d_fWs1.rearrange("(a p) n -> p a n", p=128),
                      after=[wb_free[0]] if wb_free[0] else [])

        t_g = op("v", "tensor_copy", f_gtm[:], pA[0][:], after=[pool_last])
        dbg_point("fG", [t_g])
        bank["A0"].read(t_g)
        t_gh = op("v", "tensor_copy", f_ghl[:, 0, :], f_gtm[:], after=[hard(t_g)])
        t_gl = op("v", "tensor_sub", f_ghl[:, 1, :], f_gtm[:], f_ghl[:, 0, :],
                  after=[hard(t_gh)])
        gt_toks = []
        for hl in range(2):
            src_tok = t_gh if hl == 0 else t_gl
            for half in range(2):
                db = "D0" if half == 0 else "D1"
                dp = pD[0] if half == 0 else pD[1]
                tts = []
                for c4 in range(4):
                    c = half * 4 + c4
                    tts.append(tr(dp[:, c4 * 128:(c4 + 1) * 128],
                                  f_ghl[:, hl, c * 128:(c + 1) * 128],
                                  after=[src_tok] + (bank[db].for_write() if c4 == 0 else [])))
                bank[db].wrote(tts[-1])
                dstap = rap(t_fgfm, 2048, 0, 128, hl * 1024 + (half * 4) * 128,
                            [[128, 4], [1, 128]])
                srcap = rap(h_pD0 if half == 0 else h_pD1, 1024, 0, 128, 0,
                            [[128, 4], [1, 128]])
                ev = op("v", "tensor_copy", dstap, srcap, after=[tts[-1]])
                bank[db].read(ev)
                gt_toks.append(ev)
        t_w1 = op("v", "tensor_copy", f_w1h[:, 0], f_ws1.rearrange("p a n -> p (a n)"),
                  after=[tok_ws1])
        t_w1l = op("v", "tensor_sub", f_w1h[:, 1], f_ws1.rearrange("p a n -> p (a n)"),
                   f_w1h[:, 0], after=[hard(t_w1)])
        t_hm = [None, None]
        for m_ in range(2):
            bnm = "B0"
            first = True
            last = None
            for a in range(NCH):
                for wi in range(2):
                    for gi in range(2):
                        aft = []
                        if first:
                            aft = [t_w1l] + gt_toks + bank[bnm].for_write()
                        last = mm(pB[0][:, 0:128],
                                  rap(t_fw1h, 4096, 0, 128, wi * 2048 + a * 256 + m_ * 128,
                                      [[1, 128]]),
                                  f_gfm[:, gi, a, :], first,
                                  a == NCH - 1 and wi == 1 and gi == 1, after=aft)
                        first = False
            bank[bnm].wrote(last)
            t_hm[m_] = op("s", "activation", f_h[:, m_, :], pB[0][:, 0:128], AFT.Relu,
                          bias=bs1c[:, m_:m_ + 1], after=[last, tok_b1])
            bank[bnm].read(t_hm[m_])
        t_hh = op("v", "tensor_copy", f_hhl[:, 0], f_h.rearrange("p a n -> p (a n)"),
                  after=[t_hm[0], t_hm[1]])
        t_hl = op("v", "tensor_sub", f_hhl[:, 1], f_h.rearrange("p a n -> p (a n)"),
                  f_hhl[:, 0], after=[hard(t_hh)])
        t_2h = op("v", "tensor_copy", ws2hl[:, 0:2], ws2c[:], after=[tok_w2])
        t_2l = op("v", "tensor_sub", ws2hl[:, 2:4], ws2c[:], ws2hl[:, 0:2],
                  after=[hard(t_2h)])
        first = True
        last = None
        for m_ in range(2):
            for wi in range(2):
                for gi in range(2):
                    aft = []
                    if first:
                        aft = [t_hl, t_2l] + bank["B1"].for_write()
                    last = mm(pB[1][0:1, 0:128], ws2hl[:, 2 * wi + m_:2 * wi + m_ + 1],
                              f_hhl[:, gi, m_, :], first,
                              m_ == 1 and wi == 1 and gi == 1, after=aft)
                    first = False
        bank["B1"].wrote(last)
        t_sr = op("v", "tensor_scalar", f_srow[0:1, 0:128], pB[1][0:1, 0:128],
                  bs2c[0:1, 0:1], None, ALU.add, after=[last, tok_b2])
        dbg_point("fscore", [t_sr, t_hm[0], t_hm[1]])
        bank["B1"].read(t_sr)
        t_sc = dma("y", f_scol[:, 0:1], f_srow[0:1, 0:128], after=[t_sr])
        t_sb = mm(pA[1][:, 0:128], onesf[0:1, 0:128], f_srow[0:1, 0:128], True, True,
                  after=[t_sr, tok_onesf] + bank["A1"].for_write())
        bank["A1"].wrote(t_sb)
        t_cmp = op("v", "tensor_scalar", f_cmp[:], pA[1][:, 0:128], f_scol[:, 0:1], None,
                   ALU.is_gt, after=[t_sb, t_sc])
        t_eq = op("v", "scalar_tensor_tensor", f_madd[:], pA[1][:, 0:128], f_scol[:, 0:1],
                  lowtri[:], ALU.is_equal, ALU.mult, after=[hard(t_cmp), t_sc, tok_lt])
        bank["A1"].read(t_eq)
        t_add_ = op("v", "tensor_add", f_cmp[:], f_cmp[:], f_madd[:],
                    after=[hard(t_eq)])
        t_red_ = op("v", "tensor_reduce", f_cnt[:, 0:1], f_cmp[:], AXX, ALU.add,
                    after=[hard(t_add_)])
        t_mq = op("v", "tensor_tensor", f_mq[:, 0:1], f_cnt[:, 0:1], constcol[:, 1:2],
                  ALU.is_lt, after=[tok_cc2, hard(t_red_)])
        dbg_point("fcount", [t_mq])
        t_mqb = op("v", "tensor_copy", f_mqb[:, 0:1], f_mq[:, 0:1], after=[hard(t_mq)])
        t_mr = mm(pB[1][0:1, 0:128], f_mqb[:, 0:1], identb[:], True, True,
                  after=[t_mqb] + bank["B1"].for_write())
        bank["B1"].wrote(t_mr)
        t_mre = op("v", "tensor_copy", f_mrow[0:1, 0:128], pB[1][0:1, 0:128], after=[t_mr])
        bank["B1"].read(t_mre)
        t_mb = mm(pA[1][:, 0:128], onesf[0:1, 0:128], f_mrow[0:1, 0:128], True, True,
                  after=[t_mre] + bank["A1"].for_write())
        bank["A1"].wrote(t_mb)
        t_madd = op("v", "tensor_scalar", f_madd[:], pA[1][:, 0:128], 1.0, 1e30,
                    ALU.subtract, ALU.mult, after=[t_mb])
        bank["A1"].read(t_madd)
        dbg_point("fmask", [t_madd, t_mq])

        wo_tok = wfull_load(wB, d_fWo, after=[wb_free[0]])
        sl_i = [0]
        sl_cons = [None, None]
        def stream_mm(dsrc, out_sb, token_major, after_extra):
            toks = []
            src4 = dsrc.rearrange("(a p) (c n) -> p a c n", p=128, n=128)
            for m_ in range(NCH):
                par_ = sl_i[0] % 2
                sl_i[0] += 1
                t_ld = dma("g", wsl[:, par_], src4[:, :, m_, :], after=[sl_cons[par_]])
                bnm = "B0" if m_ % 2 == 0 else "B1"
                dstp = pB[0] if bnm == "B0" else pB[1]
                last = None
                for a in range(NCH):
                    aft = []
                    if a == 0:
                        aft = [t_ld] + after_extra + bank[bnm].for_write()
                    if token_major:
                        last = mm(dstp[:, 0:128], f_gfm[:, 0, a, :], wsl[:, par_, a, :],
                                  a == 0, a == NCH - 1, after=aft)
                    else:
                        last = mm(dstp[:, 0:128], wsl[:, par_, a, :], f_gfm[:, 0, a, :],
                                  a == 0, a == NCH - 1, after=aft)
                bank[bnm].wrote(last)
                sl_cons[par_] = last
                ev = op("v", "tensor_copy", out_sb[:, m_, :], dstp[:, 0:128], after=[last])
                bank[bnm].read(ev)
                toks.append(ev)
            return toks

        tq = stream_mm(d_fWq, f_qg, False, gt_toks)
        tk = stream_mm(d_fWk, f_kg, False, [])
        tv = stream_mm(d_fWv, f_vg, True, [])

        ctx_toks = []
        for h in range(H):
            ch_, pr = h // 2, 64 * (h % 2)
            lq = rap(t_fqg, 1024, pr, 64, ch_ * 128, [[1, 128]])
            lk = rap(t_fkg, 1024, pr, 64, ch_ * 128, [[1, 128]])
            t_sg = mm(pB[0][:, 0:128], lq, lk, True, True,
                      after=[tq[-1], tk[-1], t_madd] + bank["B0"].for_write())
            bank["B0"].wrote(t_sg)
            t_sm = op("v", "tensor_add", f_sm[:], pB[0][:, 0:128], f_madd[:], after=[t_sg])
            bank["B0"].read(t_sm)
            t_nmx = op("v", "tensor_reduce", f_cnt[:, 1:2], f_sm[:], AXX, ALU.max,
                       negate=True)
            t_bias_ = op("s", "mul", f_cnt[:, 2:3], f_cnt[:, 1:2], 0.125, after=[t_nmx])
            t_p = op("s", "activation", f_pb[:], f_sm[:], AFT.Exp,
                     bias=f_cnt[:, 2:3], scale=0.125, accum_out=f_cnt[:, 3:4],
                     after=[hard(t_bias_)])
            t_rd = op("v", "reciprocal", f_cnt[:, 1:2], f_cnt[:, 3:4], after=[t_p])
            t_pn = op("v", "tensor_scalar", f_pb[:], f_pb[:], f_cnt[:, 1:2], None,
                      ALU.mult, after=[hard(t_rd)])
            db = "D0" if h % 2 == 0 else "D1"
            dp = pD[0] if h % 2 == 0 else pD[1]
            t_tp = tr(dp[:, 0:128], f_pb[:],
                      after=[t_pn] + bank[db].for_write())
            bank[db].wrote(t_tp)
            t_pt = op("v", "tensor_copy", f_pt[:], dp[:, 0:128], after=[t_tp])
            bank[db].read(t_pt)
            lv = rap(t_fvg, 1024, 0, 128, ch_ * 128 + pr, [[1, 64]])
            if h % 2 == 0:
                t_cx = mm(pB[1][0:64, 0:128], lv, f_pt[:], True, True,
                          after=[t_pt, tv[-1]] + bank["B1"].for_write())
                rows = slice(0, 64)
            else:
                t_cx = mm(pB[1][64:128, 0:128], lv, f_pt[:], True, True,
                          after=[t_pt, tv[-1]] + bank["B1"].for_write(), tp=(0, 64))
                rows = slice(64, 128)
            bank["B1"].wrote(t_cx)
            t_cv = op("v", "tensor_copy",
                      rap(t_fctx, 1024, rows.start, 64, ch_ * 128, [[1, 128]]),
                      pB[1][rows, 0:128], after=[t_cx])
            bank["B1"].read(t_cv)
            ctx_toks.append(t_cv)

        og_toks = []
        last_og_mm = [None]
        for ch in range(2):
            bnm = "B0"
            last = None
            for a in range(NCH):
                aft = []
                if a == 0:
                    aft = [ctx_toks[-1]] + wo_tok + bank[bnm].for_write()
                last = mm(pB[0][:, 0:512], f_ctx[:, a, :], wB[:, a, ch * 512:(ch + 1) * 512],
                          a == 0, a == NCH - 1, after=aft)
            bank[bnm].wrote(last)
            last_og_mm[0] = last
            t_og = op("v", "tensor_scalar", f_og[:, ch * 512:(ch + 1) * 512],
                      pB[0][:, 0:512], f_mq[:, 0:1], None, ALU.mult, after=[last])
            bank[bnm].read(t_og)
            og_toks.append(t_og)
        dbg_point("fog", og_toks + ctx_toks)
        tok_y = [None] * NT
        for tt in range(NT):
            for ch in range(2):
                bnm = "B1" if (tt * 2 + ch) % 2 == 0 else "B0"
                dstp = pB[1] if bnm == "B1" else pB[0]
                t_y = mm(dstp[:, 0:512], eg16[:, tt, :],
                         f_og[:, ch * 512:(ch + 1) * 512],
                         True, True,
                         after=[eg_toks[tt]] + og_toks + bank[bnm].for_write())
                bank[bnm].wrote(t_y)
                t_add = op("v", "tensor_tensor", x[:, tt, ch * 512:(ch + 1) * 512],
                           dstp[:, 0:512], x[:, tt, ch * 512:(ch + 1) * 512], ALU.add,
                           after=[t_y, tok_res[tt]])
                bank[bnm].read(t_add)
                tok_y[tt] = t_add
        wb_free[0] = last_og_mm[0]
        return tok_y

    def emit_attn_b(tok_res):
        tok_xn = emit_xn_for_attn(tok_res)
        vpi = [op("v", "memset",
                  rap(t_vp, 6208, 0, 128, hh * 65 + 64, [[260, NT], [1, 1]]), 1.0)
               for hh in range(4)]
        slab_tok = {"q": wslab_load(wq_s, d_bWq, 0),
                    "k": wslab_load(wk_s, d_bWk, 0),
                    "v": wslab_load(wv_s, d_bWv, 0)}
        slab_cons = {"q": None, "k": None, "v": None}
        wo_tok = wfull_load(wB, d_bWo, after=[wb_free[0]])
        tok_wft0 = dma("g", wft[0:HD, :], d_bWf[:], after=[wb_free[0]])
        tok_wft = dma("g", wft[HD:128, :], d_bWf[:], after=[wb_free[0]])
        PS = [pB[0][:, 0:512], pB[1][:, 0:512], pA[0][:, 0:512], pA[1][:, 0:512]]
        PSB = ["B0", "B1", "A0", "A1"]
        last_out_ev = None
        qk_ev = [None]
        v_ev = [None]
        for j in range(4):
            for phase in ("q", "k"):
                slab_t = {"q": t_wq, "k": t_wk}[phase]
                for b in range(NB):
                    si = 2 * (b % 2)
                    for ci in range(2):
                        dst, bnm = PS[si + ci], PSB[si + ci]
                        first_after = ([tok_xn[4 * b + 3]] + slab_tok[phase]
                                       + bank[bnm].for_write())
                        last = None
                        for a in range(NCH):
                            last = mm(dst, rap(slab_t, 2048, 0, 128, a * 256 + ci * 128,
                                               [[1, 128]]),
                                      xn[:, a, b * 512:(b + 1) * 512],
                                      a == 0, a == NCH - 1,
                                      after=(first_after if a == 0 else []))
                        bank[bnm].wrote(last)
                        slab_cons[phase] = last
                        if phase == "q":
                            dstsb = qc[:, (j, j + 4)[ci], b * 512:(b + 1) * 512]
                        else:
                            dstsb = kp[:, ci, b * 512:(b + 1) * 512]
                        ev = op("v", "tensor_copy", dstsb, dst, after=[last])
                        bank[bnm].read(ev)
                        qk_ev[0] = ev
                if j < 3:
                    slab_tok[phase] = wslab_load(
                        {"q": wq_s, "k": wk_s}[phase],
                        {"q": d_bWq, "k": d_bWk}[phase], j + 1, after=[slab_cons[phase]])
            for tt in range(NT):
                bnm = "B0" if tt % 2 == 0 else "B1"
                dstp = pB[tt % 2][:, 0:256]
                first_after = [tok_xn[tt]] + slab_tok["v"] + vpi + bank[bnm].for_write()
                last = None
                for a in range(NCH):
                    last = mm(dstp, xn[:, a, tt * 128:(tt + 1) * 128], wv_s[:, a],
                              a == 0, a == NCH - 1,
                              after=(first_after if a == 0 else []))
                bank[bnm].wrote(last)
                slab_cons["v"] = last
                pbh = h_pB0 if tt % 2 == 0 else h_pB1
                ev = op("v", "tensor_copy",
                        rap(t_vp, 6208, 0, 128, tt * 260, [[65, 4], [1, 64]]),
                        rap(pbh, 512, 0, 128, 0, [[64, 4], [1, 64]]), after=[last])
                bank[bnm].read(ev)
                v_ev[0] = ev
            if j < 3:
                slab_tok["v"] = wslab_load(wv_s, d_bWv, j + 1, after=[slab_cons["v"]])
            if j == 0:
                dbg_point("bqk", [qk_ev[0], v_ev[0]])
            HHROW = [0, 64, 0, 64]
            HHCH = [j, j, j + 4, j + 4]
            HHKS = [0, 0, 1, 1]
            kv_started = False
            kv_last = None
            kf_guard = [None, None]
            eb_guard = [None, None]
            for tt in range(NT):
                ebuf = xt[:] if tt % 2 == 0 else xt2[:]
                sb = "A0" if tt % 2 == 0 else "A1"
                pfp = pA[0] if tt % 2 == 0 else pA[1]
                pf_last = None
                PFCOL = [0, 512, 256, 768]
                for hh in range(4):
                    aft = [tok_wft0, tok_wft, qk_ev[0]]
                    if hh == 0:
                        aft += bank[sb].for_write()
                    pf_last = mm(pfp[:, PFCOL[hh]:PFCOL[hh] + 256],
                                 rap(t_kp, 2 * S, HHROW[hh], 64,
                                     HHKS[hh] * S + tt * 128, [[1, 128]]),
                                 wft[HHROW[hh]:HHROW[hh] + 64, :], True, True,
                                 after=aft)
                bank[sb].wrote(pf_last)
                t_e = op("s", "activation", ebuf, pfp[:], AFT.Exp,
                         after=[pf_last, eb_guard[tt % 2]])
                t_m1 = op("v", "tensor_scalar", ebuf, ebuf, 1.0, None, ALU.min,
                          after=[t_e])
                t_kf = op("v", "scalar_tensor_tensor", kfb[:, tt % 2, :], pfp[:],
                          0.0, ebuf, ALU.max, ALU.add,
                          after=[hard(t_m1), kf_guard[tt % 2]])
                bank[sb].read(t_kf)
                eb_guard[tt % 2] = t_kf
                if j == 0 and tt == 0:
                    dbg_point("bkf1", [t_kf])
                for hh in range(4):
                    for fc in range(2):
                        aft = [t_kf]
                        if not kv_started:
                            aft += (bank["B0"].for_write() + bank["B1"].for_write()
                                    + [v_ev[0]])
                        first_in_bank = (tt == 0 and hh % 2 == 0 and fc == 0)
                        kv_last = mm(pB[hh // 2][:, (hh % 2) * 130 + fc * 65:
                                                 (hh % 2) * 130 + fc * 65 + 65],
                                     kfb[:, tt % 2, PFCOL[hh] + fc * 128:
                                         PFCOL[hh] + fc * 128 + 128],
                                     rap(t_vp, 6208, 0, 128, tt * 260 + hh * 65,
                                         [[1, 65]]),
                                     first_in_bank,
                                     tt == NT - 1 and hh == 3 and fc == 1,
                                     after=aft)
                        kv_started = True
                kf_guard[tt % 2] = kv_last
                if j == 0 and tt == 0:
                    dbg_point("bkv2", [kv_last])
                if j == 0 and tt == 2:
                    dbg_point("bkv3", [kv_last])
            bank["B0"].wrote(kv_last)
            bank["B1"].wrote(kv_last)
            t_kv0 = op("v", "tensor_copy", kvsb[:, 0:2], pB[0][:, 0:260], after=[kv_last])
            t_kv = op("v", "tensor_copy", kvsb[:, 2:4], pB[1][:, 0:260], after=[kv_last])
            bank["B0"].read(t_kv0)
            bank["B1"].read(t_kv)
            if j == 0:
                dbg_point("bkv", [t_kv0, t_kv])
            out_guard = None
            for b in range(NB):
                qf_toks = []
                for hh in range(4):
                    sb = "A0" if hh % 2 == 0 else "A1"
                    dstp = pA[0] if hh % 2 == 0 else pA[1]
                    ebuf = xt[:] if hh % 2 == 0 else xt2[:]
                    last = None
                    for fc in range(2):
                        aft = [tok_wft0, tok_wft, qk_ev[0]]
                        if fc == 0:
                            aft += bank[sb].for_write()
                        last = mm(dstp[:, fc * 512:(fc + 1) * 512],
                                  rap(t_wft, FD, HHROW[hh], 64, fc * 128, [[1, 128]]),
                                  rap(t_qc, NCH * S, HHROW[hh], 64,
                                      HHCH[hh] * S + b * 512, [[1, 512]]),
                                  True, True, after=aft)
                    bank[sb].wrote(last)
                    t_e = op("s", "activation", ebuf, dstp[:], AFT.Exp,
                             after=[last, eb_guard[hh % 2]])
                    t_m1 = op("v", "tensor_scalar", ebuf, ebuf, 1.0, None, ALU.min,
                              after=[t_e])
                    t_qf = op("v", "scalar_tensor_tensor",
                              rap(t_qf4, 4096, 0, 128, hh * 1024, [[1, 1024]]),
                              dstp[:], 0.0, ebuf, ALU.max, ALU.add,
                              after=[hard(t_m1), out_guard])
                    bank[sb].read(t_qf)
                    eb_guard[hh % 2] = t_qf
                    qf_toks.append(t_qf)
                for t4 in range(4):
                    tt = b * 4 + t4
                    last = None
                    for hh in range(4):
                        for fc in range(2):
                            aft = [t_kv, qf_toks[hh]]
                            if hh == 0 and fc == 0:
                                aft += bank["B1"].for_write()
                            last = mm(pB[1][:, hh * 65:hh * 65 + 65],
                                      rap(t_qf4, 4096, 0, 128,
                                          hh * 1024 + fc * 512 + t4 * 128, [[1, 128]]),
                                      kvsb[:, hh, fc, :],
                                      fc == 0, fc == 1, after=aft)
                    bank["B1"].wrote(last)
                    out_guard = last
                    t_n = op("v", "tensor_scalar", rr[:, 0:4],
                             rap(h_pB1, 512, 0, 128, 64, [[65, 4]]),
                             1e-6, None, ALU.add, after=[last])
                    t_r = op("v", "reciprocal", rr[:, 4:8], rr[:, 0:4],
                             after=[hard(t_n)])
                    ot_toks = []
                    for hh in range(4):
                        blk, par2 = hh // 2, hh % 2
                        ot_toks.append(op("v", "tensor_scalar",
                                          otm[:, blk * 128 + par2 * 64:
                                              blk * 128 + par2 * 64 + 64],
                                          pB[1][:, hh * 65:hh * 65 + 64],
                                          rr[:, 4 + hh:5 + hh], None, ALU.mult,
                                          after=[hard(t_r)]))
                    bank["B1"].read(ot_toks[-1])
                    for blk in range(2):
                        db = "D0" if (tt * 2 + blk) % 2 == 0 else "D1"
                        dp = pD[0] if db == "D0" else pD[1]
                        dph = h_pD0 if db == "D0" else h_pD1
                        t_tp = tr(dp[:, 0:128], otm[:, blk * 128:(blk + 1) * 128],
                                  after=[ot_toks[2 * blk], ot_toks[2 * blk + 1]]
                                  + bank[db].for_write())
                        bank[db].wrote(t_tp)
                        ev = op("v", "tensor_copy",
                                rap(t_qc, NCH * S, 0, 128,
                                    HHCH[2 * blk] * S + tt * 128, [[1, 128]]),
                                rap(dph, 1024, 0, 128, 0, [[1, 128]]), after=[t_tp])
                        bank[db].read(ev)
                        last_out_ev = ev
            if j == 0:
                dbg_point("bout", [last_out_ev])
        tok_y = [None] * NT
        for tt in range(NT):
            for ch in range(2):
                bnm = "B0" if (tt * 2 + ch) % 2 == 0 else "B1"
                dstp = pB[0] if bnm == "B0" else pB[1]
                first_after = [last_out_ev] + wo_tok + bank[bnm].for_write()
                last = None
                for a in range(NCH):
                    last = mm(dstp[:, 0:512], qc[:, a, tt * 128:(tt + 1) * 128],
                              wB[:, a, ch * 512:(ch + 1) * 512],
                              a == 0, a == NCH - 1, after=(first_after if a == 0 else []))
                bank[bnm].wrote(last)
                t_add = op("v", "tensor_tensor", x[:, tt, ch * 512:(ch + 1) * 512],
                           dstp[:, 0:512], x[:, tt, ch * 512:(ch + 1) * 512], ALU.add,
                           after=[last])
                bank[bnm].read(t_add)
                tok_y[tt] = t_add
        wb_free[0] = last
        return tok_y

    wb_free = [None]
    toks = tok_x
    try:
        if n_sub >= 1:
            toks = emit_attn_a(toks)
        if n_sub >= 2:
            toks = emit_ffn(toks, False)
        if n_sub >= 3:
            toks = emit_attn_b(toks)
        if n_sub >= 4:
            toks = emit_ffn(toks, True)
    except StopBuild:
        pass

    st_toks = [B.dma("y", (lambda e, tt=tt: e.dma_start(
        out=dout[tt * 128:(tt + 1) * 128, :], in_=x[:, tt, :])), after=[toks[tt]])
        for tt in range(NT)]
    B.op("y", lambda e: e.nop(), after=st_toks + dbg_outs)

    import contextlib
    es = contextlib.ExitStack()
    B.finalize(nc, es)
    with nc.Block() as block:
        block.tensor(lambda e: B.emit("t", e))
        block.vector(lambda e: B.emit("v", e))
        block.scalar(lambda e: B.emit("s", e))
        block.gpsimd(lambda e: B.emit("g", e))
        block.sync(lambda e: B.emit("y", e))
    es.close()
    return nc


def _host_prep(inp):
    import jax
    import jax.numpy as jnp
    f = lambda a: np.ascontiguousarray(np.asarray(a, dtype=np.float32))
    perm = np.concatenate([np.arange(0, D, 2), np.arange(1, D, 2)])
    m = {}
    m["aWq"] = f(inp["L0_attn_a_Wq"])[:, perm]
    m["aWk"] = f(inp["L0_attn_a_Wk"])[:, perm]
    m["aWv"] = f(inp["L0_attn_a_Wv"])
    m["aWo"] = f(inp["L0_attn_a_Wo"])
    m["abq"] = np.ascontiguousarray(f(inp["L0_attn_a_bq"])[perm].reshape(8, 128).T)
    m["abk"] = np.ascontiguousarray(f(inp["L0_attn_a_bk"])[perm].reshape(8, 128).T)
    m["abv"] = f(inp["L0_attn_a_bv"]).reshape(1, D)
    m["abo"] = f(inp["L0_attn_a_bo"]).reshape(1, D)
    m["bWqT"] = np.ascontiguousarray(f(inp["L0_attn_b_Wq"]).T)
    m["bWkT"] = np.ascontiguousarray(f(inp["L0_attn_b_Wk"]).T)
    m["bWvT"] = np.ascontiguousarray(f(inp["L0_attn_b_Wv"]).T)
    m["bWoT"] = np.ascontiguousarray(f(inp["L0_attn_b_Wo"]).T)
    m["bWfT"] = np.ascontiguousarray(f(inp["L0_attn_b_Wf"]).T)
    m["fWq"] = f(inp["L0_ffn_Wq"])
    m["fWk"] = f(inp["L0_ffn_Wk"])
    m["fWv"] = f(inp["L0_ffn_Wv"])
    m["fWo"] = f(inp["L0_ffn_Wo"])
    m["fWs1"] = f(inp["L0_ffn_Ws1"])
    m["fbs1"] = np.ascontiguousarray(f(inp["L0_ffn_bs1"]).reshape(2, 128).T)
    m["fWs2"] = np.ascontiguousarray(f(inp["L0_ffn_Ws2"]).reshape(2, 128).T)
    m["fbs2"] = f(inp["L0_ffn_bs2"]).reshape(1, 1)
    cpu = jax.devices("cpu")[0]
    with jax.default_device(cpu):
        fr = jnp.arange(D // 2, dtype=jnp.float32)
        fc = jnp.power(10000.0, -2.0 * fr / D)
        pos = jnp.arange(S, dtype=jnp.float32)
        freq = jnp.einsum("i,j->ij", pos, fc)
        sin_e = np.asarray(jnp.sin(freq), dtype=np.float32)
        cos_e = np.asarray(jnp.cos(freq), dtype=np.float32)
    m["ropesin"] = np.ascontiguousarray(sin_e.T.reshape(4, 128, S))
    m["ropecos"] = np.ascontiguousarray(cos_e.T.reshape(4, 128, S))
    m["identc"] = np.eye(128, dtype=np.float32)
    m["lowtri"] = np.tril(np.ones((128, 128), dtype=np.float32), -1)
    ca = np.zeros((128, 128), dtype=np.float32)
    kk = np.arange(128)[:, None]
    qq = np.arange(128)[None, :]
    ca[qq < kk] = -1e30
    m["causadd"] = ca
    m["tri01"] = (qq >= kk).astype(np.float32)
    return m


_N_SUB = 4


def kernel(**inputs):
    xfull = np.asarray(inputs["L0_attn_a_x"], dtype=np.float32)
    common = _host_prep(inputs)
    nc = build_nc(_N_SUB)
    in_maps = []
    for b in range(8):
        im = dict(common)
        im["x_in"] = np.ascontiguousarray(xfull[b])
        in_maps.append(im)
    res = run_bass_kernel_spmd(nc, in_maps, core_ids=list(range(8)))
    return np.stack([np.asarray(res.results[b]["x_out"]) for b in range(8)], axis=0)


# version 06
# speedup vs baseline: 1.1464x; 1.0935x over previous
import numpy as np
import concourse.bass as bass
import concourse.mybir as mybir
import bass_rust
from concourse.alu_op_type import AluOpType as ALU
from concourse.bass_utils import run_bass_kernel_spmd

F32 = mybir.dt.float32
BF16 = mybir.dt.bfloat16
AFT = mybir.ActivationFunctionType
AXX = mybir.AxisListType.X

S, D, H, HD, FD = 2048, 1024, 16, 64, 256
NT, NB, NCH = 16, 4, 8
ENGS = ("t", "v", "s", "g", "y")


class Sched:
    def __init__(self):
        self.items = {e: [] for e in ENGS}
        self.ndma = {e: 0 for e in ENGS}

    def op(self, e, fn, after=()):
        self.items[e].append([[t for t in after if t is not None], fn, False, False])
        return ("i", e, len(self.items[e]) - 1)

    def dma(self, e, fn, after=()):
        self.items[e].append([[t for t in after if t is not None], fn, False, True])
        self.ndma[e] += 1
        return ("d", e, self.ndma[e] * 16)

    def finalize(self, nc, es):
        for e in ENGS:
            for it in self.items[e]:
                for tok in it[0]:
                    if tok[0] in ("i", "h"):
                        assert not self.items[tok[1]][tok[2]][3]
                        self.items[tok[1]][tok[2]][2] = True
        self.val = {}
        for e in ENGS:
            c, vals = 0, []
            for it in self.items[e]:
                if it[2]:
                    c += 1
                vals.append(c)
            self.val[e] = vals
        self.psem = {e: es.enter_context(nc.semaphore(f"p_{e}")) for e in ENGS}
        self.dsem = {e: es.enter_context(nc.semaphore(f"d_{e}"))
                     for e in ENGS if self.ndma[e]}

    def emit(self, e, eng):
        waited = {}
        for waits, fn, inc, is_dma in self.items[e]:
            for tok in waits:
                if tok[0] in ("i", "h"):
                    pe, idx = tok[1], tok[2]
                    if pe == e and tok[0] == "i":
                        continue
                    v = self.val[pe][idx]
                    if waited.get(("i", pe), 0) < v:
                        eng.wait_ge(self.psem[pe], v)
                        waited[("i", pe)] = v
                else:
                    qe, v = tok[1], tok[2]
                    if waited.get(("d", qe), 0) < v:
                        eng.wait_ge(self.dsem[qe], v)
                        waited[("d", qe)] = v
            inst = fn(eng)
            if is_dma:
                inst.then_inc(self.dsem[e], 16)
            elif inc:
                inst.then_inc(self.psem[e], 1)


def hard(tok):
    assert tok[0] == "i"
    return ("h", tok[1], tok[2])


class Bank:
    def __init__(self):
        self.w = None
        self.r = None

    def for_write(self):
        out = [t for t in (self.w, self.r) if t is not None]
        return out

    def wrote(self, tok):
        self.w, self.r = tok, None

    def for_read(self):
        return [self.w] if self.w is not None else []

    def read(self, tok):
        self.r = tok


class StopBuild(Exception):
    pass


def build_nc(n_sub=4, dbg=None):
    nc = bass.Bass("TRN2", target_bir_lowering=False)
    B = Sched()

    def op(eng, meth, *args, after=(), **kw):
        return B.op(eng, (lambda e, m=meth, a=args, k=kw: getattr(e, m)(*a, **k)),
                    after=after)

    def dma(q, dst, src, after=()):
        return B.dma(q, (lambda e, d=dst, s=src: e.dma_start(out=d, in_=s)), after=after)

    dbg_outs = []

    def dbg_point(label, toks):
        if dbg != label:
            return
        dumps = [("xn", t_xn, 16384, BF16), ("qc", t_qc, 16384, BF16),
                 ("kp", t_kp, 4096, BF16), ("vp", t_vp, 6208, BF16),
                 ("et", t_et, 2048, BF16), ("scrf", t_scrf, 1024, F32),
                 ("xt", t_xt, 1024, BF16), ("st", t_st, 16, F32),
                 ("gtm", t_fgtm, 1024, F32), ("srow", t_fsrow, 128, F32),
                 ("mq", t_fmq, 8, F32), ("madd", t_fmadd, 128, F32),
                 ("og", t_fog, 1024, BF16), ("ctxf", t_fctx, 1024, BF16),
                 ("hh", t_fh, 256, F32), ("gfm", t_fgfm, 2048, BF16),
                 ("cmp", t_fcmp, 128, F32), ("cnt", t_fcnt, 8, F32), ("scol", t_fscol, 8, F32),
                 ("w1h", t_fw1h, 4096, BF16)]
        for nm, th, n, dt in dumps:
            o = nc.dram_tensor("dbg_" + nm, [128, n], F32, kind="ExternalOutput").ap()
            srcap = bass_rust.AP(th, 0, [[n, 128], [1, n]])
            dbg_outs.append(B.dma("g", (lambda e, oo=o, ss=srcap: e.dma_start(out=oo, in_=ss)),
                                  after=[t for t in toks if t is not None]))
        raise StopBuild()

    def mm(dst, lhs, rhs, start, stop, after=(), tp=None):
        return B.op("t", (lambda e, d=dst, l=lhs, r=rhs, st=start, sp=stop, t=tp:
                          e.matmul(d, l, r, start=st, stop=sp, tile_position=t)),
                    after=after)

    def tr(dst, src, after=()):
        return B.op("t", (lambda e, d=dst, s=src: e.transpose(d, s, identb_late[0])),
                    after=after)

    DI = lambda name, shape: nc.dram_tensor(name, shape, F32, kind="ExternalInput").ap()
    dx = DI("x_in", [S, D])
    dout = nc.dram_tensor("x_out", [S, D], F32, kind="ExternalOutput").ap()
    d_aWq, d_aWk, d_aWv, d_aWo = DI("aWq", [D, D]), DI("aWk", [D, D]), DI("aWv", [D, D]), DI("aWo", [D, D])
    d_abq, d_abk = DI("abq", [128, 8]), DI("abk", [128, 8])
    d_abv, d_abo = DI("abv", [1, D]), DI("abo", [1, D])
    d_bWq, d_bWk, d_bWv, d_bWo = DI("bWqT", [D, D]), DI("bWkT", [D, D]), DI("bWvT", [D, D]), DI("bWoT", [D, D])
    d_bWf = DI("bWfT", [HD, FD])
    d_fWq, d_fWk, d_fWv, d_fWo = DI("fWq", [D, D]), DI("fWk", [D, D]), DI("fWv", [D, D]), DI("fWo", [D, D])
    d_fWs1 = DI("fWs1", [D, 256])
    d_fbs1, d_fWs2, d_fbs2 = DI("fbs1", [128, 2]), DI("fWs2", [128, 2]), DI("fbs2", [1, 1])
    d_sin, d_cos = DI("ropesin", [4, 128, S]), DI("ropecos", [4, 128, S])
    d_ident, d_lowtri = DI("identc", [128, 128]), DI("lowtri", [128, 128])
    d_causadd = DI("causadd", [128, 128])
    d_tri01 = DI("tri01", [128, 128])

    _probe = nc.alloc_sbuf_tensor("sb_probe", [128, 8], F32)
    SB_BASE = nc.lookup_mloc(_probe).addr
    assert SB_BASE <= 16512 + 4096, SB_BASE
    cur = [SB_BASE]

    def alloc(name, shape, dtype, at=None):
        esz = 2 if dtype == BF16 else 4
        n = 1
        for d_ in shape[1:]:
            n *= d_
        if at is None:
            off = (cur[0] + 31) // 32 * 32
            cur[0] = off + n * esz
            assert cur[0] <= SB_BASE + 212800, f"SBUF overflow at {name}: {cur[0]}"
        else:
            off = at
        return nc.alloc_sbuf_tensor_at(name, shape, dtype, offset=off)

    t_x = alloc("xres", [128, NT, D], F32)
    QC_OFF = cur[0]
    t_qc = alloc("qc", [128, NCH, S], BF16)
    XN_OFF = cur[0]
    t_xn = alloc("xn", [128, NCH, S], BF16)
    t_kp = alloc("kp", [128, 2, S], BF16)
    t_vp = alloc("vp", [128, 6208], BF16)
    TBL_OFF = (cur[0] + 31) // 32 * 32
    t_tbl = alloc("tbl", [128, 2, 2, 512], F32)
    t_wq = alloc("wq_s", [128, 8, 2, 128], BF16)
    t_wk = alloc("wk_s", [128, 8, 2, 128], BF16)
    t_wv = alloc("wv_s", [128, 8, 2, 128], BF16)
    t_wB = alloc("wB", [128, 8, 1024], BF16)
    t_wsl = alloc("wsl", [128, 2, 8, 128], BF16)
    WSL_OFF = cur[0] - 4096
    t_bca = alloc("bca", [128, 512], F32, at=WSL_OFF)
    t_bias = alloc("biasrows", [128, 2, 1024], BF16)
    t_bqc = alloc("bq_col", [128, 8], F32)
    t_bkc = alloc("bk_col", [128, 8], F32)
    SCRF_OFF = (cur[0] + 31) // 32 * 32
    t_scrf = alloc("scr_f32", [128, 1024], F32)
    t_xt = alloc("xt_tile", [128, 1024], BF16)
    t_xe = alloc("xt2", [128, 1024], BF16, at=WSL_OFF + 2048)
    t_et = alloc("et", [128, 2, 1024], BF16)
    ET_OFF = cur[0] - 4096
    t_rr = alloc("rr", [128, 512], F32)
    t_st = alloc("stats", [128, 16], F32)
    t_id = alloc("identb", [128, 128], BF16)
    t_of = alloc("onesf", [128, 128], F32)
    t_ob = alloc("onesb", [128, 128], BF16)
    t_b1 = alloc("bs1c", [128, 2], F32)
    t_w2 = alloc("ws2c", [128, 2], F32)
    t_w2h = alloc("ws2hl", [128, 4], BF16)
    t_b2 = alloc("bs2c", [128, 1], F32)
    t_cm = alloc("causm", [128, 128], F32)
    t_tb = alloc("trib", [128, 128], BF16)
    t_eg = alloc("egen", [128, 128], BF16)
    t_pgn = alloc("pgen", [128, 128], BF16)
    t_cc = alloc("constcol", [128, 8], F32)
    t_qf4 = alloc("qfb4", [128, 4, 2, 512], BF16, at=TBL_OFF)
    t_et2 = alloc("et2", [128, 2, 1024], BF16, at=TBL_OFF)
    t_kf = alloc("kfb", [128, 2, 1024], BF16, at=ET_OFF)
    t_kv = alloc("kvsb", [128, 4, 2, 65], BF16, at=SCRF_OFF)
    t_ot = alloc("otm", [128, 256], BF16, at=SCRF_OFF + 2112)
    t_wft = alloc("wftb", [128, FD], BF16, at=SCRF_OFF + 2624)
    t_fws1 = alloc("f_ws1", [128, 8, 256], F32, at=QC_OFF)
    t_fw1h = alloc("f_w1h", [128, 2, 8, 256], BF16, at=QC_OFF + 8192)
    t_fgtm = alloc("f_gtm", [128, 1024], F32, at=QC_OFF + 16384)
    t_fghl = alloc("f_ghl", [128, 2, 1024], BF16, at=QC_OFF + 20480)
    t_fgfm = alloc("f_gfm", [128, 2, 8, 128], BF16, at=QC_OFF + 24576)
    t_fh = alloc("f_h", [128, 2, 128], F32, at=QC_OFF + 28672)
    t_fhhl = alloc("f_hhl", [128, 2, 2, 128], BF16, at=QC_OFF + 29696)
    t_fsrow = alloc("f_srow", [128, 128], F32, at=QC_OFF + 30720)
    t_fscol = alloc("f_scol", [128, 8], F32, at=QC_OFF + 31232)
    t_fcnt = alloc("f_cnt", [128, 8], F32, at=QC_OFF + 31264)
    t_fmq = alloc("f_mq", [128, 8], F32, at=QC_OFF + 31296)
    t_fmqb = alloc("f_mqb", [128, 16], BF16, at=QC_OFF + 31328)
    t_fmrow = alloc("f_mrow", [128, 128], F32, at=QC_OFF + 31360)
    t_fcmp = alloc("f_cmp", [128, 128], F32, at=XN_OFF)
    t_fmadd = alloc("f_madd", [128, 128], F32, at=XN_OFF + 512)
    t_fqg = alloc("f_qg", [128, 8, 128], BF16, at=XN_OFF + 1024)
    t_fkg = alloc("f_kg", [128, 8, 128], BF16, at=XN_OFF + 3072)
    t_fvg = alloc("f_vg", [128, 8, 128], BF16, at=XN_OFF + 5120)
    t_fctx = alloc("f_ctx", [128, 8, 128], BF16, at=XN_OFF + 7168)
    t_fog = alloc("f_og", [128, 1024], BF16, at=XN_OFF + 9216)
    t_fsm = alloc("f_sm", [128, 128], F32, at=XN_OFF + 11264)
    t_fpb = alloc("f_pb", [128, 128], BF16, at=XN_OFF + 11776)
    t_fpt = alloc("f_pt", [128, 128], BF16, at=XN_OFF + 12032)
    t_fxlo = alloc("f_xlo", [128, 1024], BF16, at=XN_OFF + 12288)
    t_lt = alloc("lowtrib", [128, 128], F32, at=XN_OFF + 14336)
    t_eg16 = alloc("egen16", [128, NT, 128], BF16, at=XN_OFF + 16384)
    t_pg16 = alloc("pgen16", [128, NT, 128], BF16, at=XN_OFF + 20480)

    h_pA0 = nc.alloc_psum_tensor("pA0", [128, 1024], F32)
    h_pA1 = nc.alloc_psum_tensor("pA1", [128, 1024], F32)
    h_pB0 = nc.alloc_psum_tensor("pB0", [128, 512], F32)
    h_pB1 = nc.alloc_psum_tensor("pB1", [128, 512], F32)
    h_pD0 = nc.alloc_psum_tensor("pD0", [128, 1024], BF16)
    h_pD1 = nc.alloc_psum_tensor("pD1", [128, 1024], BF16)
    pA = [h_pA0.ap(), h_pA1.ap()]
    pB = [h_pB0.ap(), h_pB1.ap()]
    pD = [h_pD0.ap(), h_pD1.ap()]
    pDf = [h_pD0.ap().bitcast(F32), h_pD1.ap().bitcast(F32)]
    bank = {"A0": Bank(), "A1": Bank(), "B0": Bank(), "B1": Bank(),
            "D0": Bank(), "D1": Bank()}

    identb_late = [None]
    x = t_x.ap(); qc = t_qc.ap(); xn = t_xn.ap(); kp = t_kp.ap(); vp = t_vp.ap()
    tbl = t_tbl.ap(); wq_s = t_wq.ap(); wk_s = t_wk.ap(); wv_s = t_wv.ap()
    wB = t_wB.ap(); wsl = t_wsl.ap(); biasr = t_bias.ap()
    bqc = t_bqc.ap(); bkc = t_bkc.ap(); scrf = t_scrf.ap(); xt = t_xt.ap()
    xlo = t_fxlo.ap(); et = t_et.ap(); rr = t_rr.ap(); bca = t_bca.ap()
    xt2 = t_xe.ap(); et2 = t_et2.ap()
    stats = t_st.ap(); identb = t_id.ap(); lowtri = t_lt.ap(); wft = t_wft.ap()
    identb_late[0] = identb[:]
    onesf = t_of.ap(); onesb = t_ob.ap()
    bs1c = t_b1.ap(); ws2c = t_w2.ap(); ws2hl = t_w2h.ap(); bs2c = t_b2.ap()
    causm = t_cm.ap(); trib = t_tb.ap()
    egen = t_eg.ap(); pgen = t_pgn.ap(); kvsb = t_kv.ap(); otm = t_ot.ap()
    qfb4 = t_qf4.ap(); kfb = t_kf.ap()
    f_ws1 = t_fws1.ap(); f_w1h = t_fw1h.ap(); f_gtm = t_fgtm.ap(); f_ghl = t_fghl.ap()
    f_gfm = t_fgfm.ap(); f_h = t_fh.ap(); f_hhl = t_fhhl.ap(); f_srow = t_fsrow.ap()
    f_scol = t_fscol.ap(); f_cnt = t_fcnt.ap(); f_mq = t_fmq.ap(); f_mqb = t_fmqb.ap()
    f_mrow = t_fmrow.ap(); f_cmp = t_fcmp.ap(); f_madd = t_fmadd.ap()
    f_qg = t_fqg.ap(); f_kg = t_fkg.ap(); f_vg = t_fvg.ap(); f_ctx = t_fctx.ap()
    f_og = t_fog.ap(); f_sm = t_fsm.ap(); f_pb = t_fpb.ap(); f_pt = t_fpt.ap()

    def rap(th, rowext, pstart, nparts, foff, dims):
        return bass_rust.AP(th, pstart * rowext + foff, [[rowext, nparts]] + dims)

    VA_TT = 386
    VA_OFF = [0, 65, 193, 258]
    VA_W = [65, 128, 65, 128]

    tok_x = [dma("y", x[:, tt, :], dx[tt * 128:(tt + 1) * 128, :]) for tt in range(NT)]
    tok_id = dma("g", identb[:], d_ident[:])
    tok_cau = dma("y", causm[:], d_causadd[:])
    tok_tri = dma("g", trib[:], d_tri01[:])
    tok_bqc = dma("y", bqc[:], d_abq[:])
    tok_bkc = dma("y", bkc[:], d_abk[:])
    tok_bvr = dma("g", biasr[0:1, 0, :], d_abv[:])
    tok_bor = dma("g", biasr[0:1, 1, :], d_abo[:])
    tok_b1 = dma("y", bs1c[:], d_fbs1[:])
    tok_w2 = dma("y", ws2c[:], d_fWs2[:])
    tok_b2 = dma("y", bs2c[0:1, 0:1], d_fbs2[:])
    tok_onesf = op("v", "memset", onesf[:], 1.0)
    tok_onesb = op("v", "memset", onesb[:], 1.0)
    constcol = t_cc.ap()
    tok_cc = op("v", "memset", constcol[:, 0:1], -16.0)
    tok_cc2 = op("v", "memset", constcol[:, 1:2], 63.5)

    def wslab_load(dst, dsrc, j, after=()):
        src4 = dsrc.rearrange("(a p) (c n) -> p a c n", p=128, n=128)
        toks = []
        for slot, jj in enumerate((j, j + 4)):
            toks.append(dma("g", dst[:, :, slot, :], src4[:, :, jj, :], after=after))
        return toks

    def wfull_load(dst, dsrc, after=()):
        return [dma("g", dst[:], dsrc.rearrange("(a p) n -> p a n", p=128), after=after)]

    def emit_ln_tile(tt, after_tok, dump_ap, dump_bank, want_f32, apply_after=()):
        so = (tt % 2) * 8
        xtile = rap(t_x, NT * D, 0, 128, tt * D, [[1, D]])
        t_r = op("v", "tensor_reduce", stats[:, so:so + 1], xtile, AXX, ALU.add,
                 negate=True, after=[after_tok])
        t_nm = op("s", "mul", stats[:, so + 1:so + 2], stats[:, so:so + 1], 1.0 / D,
                  after=[t_r])
        t_sq = op("s", "activation", dump_ap, xtile, AFT.Square,
                  bias=stats[:, so + 1:so + 2], scale=1.0,
                  accum_out=stats[:, so + 2:so + 3],
                  after=[hard(t_nm)] + (bank[dump_bank].for_write() if dump_bank else []))
        if dump_bank:
            bank[dump_bank].wrote(t_sq)
        t_var = op("v", "tensor_scalar", stats[:, so + 3:so + 4], stats[:, so + 2:so + 3],
                   1.0 / D, 1e-5, ALU.mult, ALU.add, after=[t_sq])
        t_sqrt = op("s", "activation", stats[:, so + 4:so + 5], stats[:, so + 3:so + 4],
                    AFT.Sqrt, after=[t_var])
        t_rstd = op("v", "reciprocal", stats[:, so + 5:so + 6], stats[:, so + 4:so + 5],
                    after=[t_sqrt])
        dst = scrf[:] if want_f32 else xt[:]
        t_ap = op("v", "tensor_scalar", dst, xtile, stats[:, so + 1:so + 2],
                  stats[:, so + 5:so + 6], ALU.add, ALU.mult,
                  after=[hard(t_rstd)] + list(apply_after))
        return t_ap

    def emit_xn_for_attn(tok_res):
        tok_xn = [None] * NT
        xt_guard = [None]
        for tt in range(NT):
            dmp = "A0" if tt % 2 == 0 else "A1"
            t_ap = emit_ln_tile(tt, tok_res[tt], pA[tt % 2][:], dmp, False,
                                apply_after=[xt_guard[0]])
            last_ev = None
            for half in range(2):
                db = "D0" if half == 0 else "D1"
                tts = []
                for c4 in range(4):
                    c = half * 4 + c4
                    tts.append(tr(pD[half][:, c4 * 128:(c4 + 1) * 128],
                                  xt[:, c * 128:(c + 1) * 128],
                                  after=[t_ap, tok_id] + (bank[db].for_write() if c4 == 0 else [])))
                bank[db].wrote(tts[-1])
                dstap = rap(t_xn, NCH * S, 0, 128, (half * 4) * S + tt * 128,
                            [[S, 4], [1, 128]])
                srcap = rap(h_pD0 if half == 0 else h_pD1, 1024, 0, 128, 0,
                            [[128, 4], [1, 128]])
                ev = op("v", "tensor_copy", dstap, srcap, after=[tts[-1]])
                bank[db].read(ev)
                last_ev = ev
                xt_guard[0] = tts[-1]
            tok_xn[tt] = last_ev
        return tok_xn

    def emit_attn_a(tok_res):
        tok_xn = emit_xn_for_attn(tok_res)
        dbg_point("xn", tok_xn)
        vp_init = []
        for hh in range(4):
            if hh % 2 == 0:
                vp_init.append(op("v", "memset",
                                  rap(t_vp, 6208, 0, 128, VA_OFF[hh] + 64,
                                      [[VA_TT, NT], [1, 1]]), 1.0))
            else:
                vp_init.append(op("v", "memset",
                                  rap(t_vp, 6208, 0, 128, VA_OFF[hh],
                                      [[VA_TT, NT], [1, 1]]), 1.0))
                vp_init.append(op("v", "memset",
                                  rap(t_vp, 6208, 0, 128, VA_OFF[hh] + 1,
                                      [[VA_TT, NT], [1, 63]]), 0.0))
        wo_tok = wfull_load(wB, d_aWo)
        PS = [pB[0][:, 0:512], pB[1][:, 0:512], pA[0][:, 0:512], pA[1][:, 0:512]]
        PSB = ["B0", "B1", "A0", "A1"]
        tbl_cons = [None, None]
        slab_tok = {"q": wslab_load(wq_s, d_aWq, 0),
                    "k": wslab_load(wk_s, d_aWk, 0),
                    "v": wslab_load(wv_s, d_aWv, 0)}
        slab_cons = {"q": None, "k": None, "v": None}
        et_cons = [None, None, None, None]
        last_ctx_ev = None
        rope_done = [None]
        v_ev_last = [None]

        for j in range(4):
            for phase in ("q", "k"):
                slab_t = {"q": t_wq, "k": t_wk}[phase]
                bcol = bqc if phase == "q" else bkc
                btok = tok_bqc if phase == "q" else tok_bkc
                for b in range(NB):
                    sslot = b % 2
                    t_s = dma("y", tbl[:, sslot, 0, :], d_sin[j, :, b * 512:(b + 1) * 512],
                              after=[tbl_cons[sslot], last_ctx_ev])
                    t_c = dma("y", tbl[:, sslot, 1, :], d_cos[j, :, b * 512:(b + 1) * 512],
                              after=[tbl_cons[sslot], last_ctx_ev])
                    si = 2 * (b % 2)
                    mm_t = []
                    for ci in range(2):
                        dst, bnm = PS[si + ci], PSB[si + ci]
                        first_after = ([tok_xn[4 * b + 3]] + slab_tok[phase]
                                       + bank[bnm].for_write())
                        last = None
                        for a in range(NCH):
                            last = mm(dst, rap(slab_t, 2048, 0, 128, a * 256 + ci * 128,
                                               [[1, 128]]),
                                      xn[:, a, b * 512:(b + 1) * 512],
                                      a == 0, a == NCH - 1,
                                      after=(first_after if a == 0 else []))
                        bank[bnm].wrote(last)
                        mm_t.append(last)
                    slab_cons[phase] = mm_t[-1]
                    Pe, Po = PS[si], PS[si + 1]
                    sin_ap, cos_ap = tbl[:, sslot, 0, :], tbl[:, sslot, 1, :]
                    if phase == "q":
                        d_lo = qc[:, j, b * 512:(b + 1) * 512]
                        d_hi = qc[:, j + 4, b * 512:(b + 1) * 512]
                    else:
                        d_lo = kp[:, 0, b * 512:(b + 1) * 512]
                        d_hi = kp[:, 1, b * 512:(b + 1) * 512]
                    be, bo_ = bcol[:, j:j + 1], bcol[:, j + 4:j + 5]
                    s1, s2 = scrf[:, 0:512], scrf[:, 512:1024]
                    op("v", "scalar_tensor_tensor", s1, Pe, be, cos_ap, ALU.add, ALU.mult,
                       after=[mm_t[0], t_c, btok])
                    op("v", "scalar_tensor_tensor", s2, Po, bo_, sin_ap, ALU.add, ALU.mult,
                       after=[mm_t[1], t_s])
                    op("v", "tensor_sub", d_lo, s1, s2)
                    t3 = op("v", "scalar_tensor_tensor", s1, Pe, be, sin_ap, ALU.add, ALU.mult)
                    t4 = op("v", "scalar_tensor_tensor", s2, Po, bo_, cos_ap, ALU.add, ALU.mult)
                    t_hi = op("v", "tensor_sub", d_hi, s1, s2)
                    tbl_cons[sslot] = t_hi
                    rope_done[0] = t_hi
                    bank[PSB[si]].read(t3)
                    bank[PSB[si + 1]].read(t4)
                if j < 3:
                    slab_tok[phase] = wslab_load(
                        {"q": wq_s, "k": wk_s}[phase],
                        {"q": d_aWq, "k": d_aWk}[phase], j + 1,
                        after=[slab_cons[phase]])
            dbg_point("qk", [rope_done[0]])
            for tt in range(NT):
                bnm = "B0" if tt % 2 == 0 else "B1"
                dstp = pB[tt % 2][:, 0:256]
                first_after = ([tok_xn[tt]] + slab_tok["v"] + vp_init
                               + bank[bnm].for_write())
                last = None
                for a in range(NCH):
                    last = mm(dstp, xn[:, a, tt * 128:(tt + 1) * 128], wv_s[:, a],
                              a == 0, False, after=(first_after if a == 0 else []))
                last = mm(dstp, onesb[0:1, 0:128],
                          rap(t_bias, 2048, 0, 1, j * 128, [[512, 2], [1, 128]]),
                          False, True, after=[tok_bvr, tok_onesb])
                bank[bnm].wrote(last)
                slab_cons["v"] = last
                pbh = h_pB0 if tt % 2 == 0 else h_pB1
                ev_e = op("v", "tensor_copy",
                          rap(t_vp, 6208, 0, 128, tt * VA_TT, [[193, 2], [1, 64]]),
                          rap(pbh, 512, 0, 128, 0, [[128, 2], [1, 64]]), after=[last])
                ev_o = op("v", "tensor_copy",
                          rap(t_vp, 6208, 0, 128, tt * VA_TT + 65 + 64, [[193, 2], [1, 64]]),
                          rap(pbh, 512, 0, 128, 64, [[128, 2], [1, 64]]))
                bank[bnm].read(ev_o)
                v_ev_last[0] = ev_o
            if j < 3:
                slab_tok["v"] = wslab_load(wv_s, d_aWv, j + 1, after=[slab_cons["v"]])
            dbg_point("v", [v_ev_last[0]])

            for hp in range(2):
                lanes = [2 * hp, 2 * hp + 1]
                qch = [j if hh < 2 else j + 4 for hh in lanes]
                ksl = [0 if hh < 2 else 1 for hh in lanes]
                for qt in range(NB):
                    nkt = 4 * qt + 4
                    np_ = (nkt + 1) // 2
                    ctx_started = [False, False]
                    pv_lane = [None, None]
                    pend = None

                    def emit_pv(p, exps, msks):
                        for L in range(2):
                            hh = lanes[L]
                            ctxp = pB[L]
                            cb = "B0" if L == 0 else "B1"
                            ebv = (et if p % 2 == 0 else et2)[:, L, :]
                            pv_last = None
                            for sub in range(2):
                                kt = 2 * p + sub
                                d_ = kt - 4 * qt
                                doff = max(0, d_) * 128
                                if L == 0:
                                    lhsv = rap(t_vp, 6208, 0, 128,
                                               kt * VA_TT + VA_OFF[hh], [[1, 65]])
                                    dstc = ctxp[0:65, doff:512]
                                else:
                                    lhsv = rap(t_vp, 6208, 0, 128,
                                               kt * VA_TT + VA_OFF[hh], [[1, 128]])
                                    dstc = ctxp[:, doff:512]
                                aft = [exps[L]] + msks[L]
                                if not ctx_started[L]:
                                    aft += bank[cb].for_write()
                                pv_last = mm(dstc, lhsv,
                                             ebv[:, sub * 512 + doff:(sub + 1) * 512],
                                             not ctx_started[L], kt == nkt - 1,
                                             after=aft)
                                ctx_started[L] = True
                            et_cons[2 * (p % 2) + L] = pv_last
                            pv_lane[L] = pv_last

                    for p in range(np_):
                        smm_lane = [None, None]
                        msk_lane = [[], []]
                        exp_l = [None, None]
                        for L in range(2):
                            prow = 64 * L
                            sreg = pA[L]
                            sb = "A0" if L == 0 else "A1"
                            smm = []
                            for sub in range(2):
                                kt = 2 * p + sub
                                lhs = rap(t_kp, 2 * S, prow, 64,
                                          ksl[L] * S + kt * 128, [[1, 128]])
                                rhs = rap(t_qc, NCH * S, prow, 64,
                                          qch[L] * S + qt * 512, [[1, 512]])
                                smm.append(mm(sreg[:, sub * 512:(sub + 1) * 512], lhs,
                                              rhs, True, True,
                                              after=(bank[sb].for_write()
                                                     + [rope_done[0], v_ev_last[0]]
                                                     if sub == 0 else [])))
                            bank[sb].wrote(smm[-1])
                            smm_lane[L] = smm
                        if pend is not None:
                            emit_pv(*pend)
                        for L in range(2):
                            sb = "A0" if L == 0 else "A1"
                            ebv = (et if p % 2 == 0 else et2)[:, L, :]
                            t_exp = op("s", "activation", ebv, pA[L][:],
                                       AFT.Exp, bias=constcol[:, 0:1], scale=0.125,
                                       after=[smm_lane[L][-1],
                                              et_cons[2 * (p % 2) + L], tok_cc])
                            bank[sb].read(t_exp)
                            exp_l[L] = t_exp
                        for L in range(2):
                            for sub in range(2):
                                kt = 2 * p + sub
                                d_ = kt - 4 * qt
                                if d_ >= 0:
                                    ebv = (et if p % 2 == 0 else et2)[:, L, :]
                                    sl = ebv[:, sub * 512 + d_ * 128:
                                             sub * 512 + d_ * 128 + 128]
                                    msk_lane[L].append(op("v", "tensor_tensor", sl, sl,
                                                          trib[:], ALU.mult,
                                                          after=[exp_l[L], tok_tri]))
                        pend = (p, exp_l, msk_lane)
                        if dbg == "et" and hp == 0 and qt == 0 and p == 0:
                            emit_pv(*pend)
                            dbg_point("et", pv_lane)
                    emit_pv(*pend)
                    for L in range(2):
                        hh = lanes[L]
                        ctxp = pB[L]
                        cb = "B0" if L == 0 else "B1"
                        bank[cb].wrote(pv_lane[L])
                        rowd = 64 if L == 0 else 0
                        t_ln = op("s", "activation", rr[rowd:rowd + 1, :],
                                  ctxp[rowd:rowd + 1, :], AFT.Ln, after=[pv_lane[L]])
                        t_rc = op("s", "activation", rr[rowd:rowd + 1, :],
                                  rr[rowd:rowd + 1, :], AFT.Exp, scale=-1.0,
                                  after=[hard(t_ln)])
                        db = "D0" if L == 0 else "D1"
                        dpf = pDf[L]
                        if L == 0:
                            t_bc = mm(dpf[0:64, :], onesf[64:65, 0:64], rr[64:65, :],
                                      True, True,
                                      after=[t_rc, tok_onesf] + bank[db].for_write())
                            rows = slice(0, 64)
                        else:
                            t_bc = mm(dpf[64:128, :], onesf[0:1, 0:64], rr[0:1, :],
                                      True, True,
                                      after=[t_rc, tok_onesf] + bank[db].for_write(),
                                      tp=(0, 64))
                            rows = slice(64, 128)
                        bank[db].wrote(t_bc)
                        t_be = op("s", "copy", bca[rows, :], dpf[rows, :], after=[t_bc])
                        bank[db].read(t_be)
                        dst = rap(t_qc, NCH * S, rows.start, 64,
                                  qch[L] * S + qt * 512, [[1, 512]])
                        srcc = ctxp[rows, 0:512] if L == 1 else ctxp[0:64, 0:512]
                        t_ev = op("v", "tensor_tensor", dst, srcc, bca[rows, :],
                                  ALU.mult, after=[t_be])
                        bank[cb].read(t_ev)
                        last_ctx_ev = t_ev
                        if dbg == "ctx" and j == 0 and hh == 0 and qt == 0:
                            dbg_point("ctx", [t_ev])

        tok_y = [None] * NT
        for tt in range(NT):
            for ch in range(2):
                bnm = "B0" if (tt * 2 + ch) % 2 == 0 else "B1"
                dstp = pB[0] if bnm == "B0" else pB[1]
                first_after = [last_ctx_ev] + wo_tok + bank[bnm].for_write()
                last = None
                for a in range(NCH):
                    last = mm(dstp[:, 0:512], qc[:, a, tt * 128:(tt + 1) * 128],
                              wB[:, a, ch * 512:(ch + 1) * 512],
                              a == 0, False, after=(first_after if a == 0 else []))
                last = mm(dstp[:, 0:512], onesb[0:1, 0:128],
                          rap(t_bias, 2048, 0, 1, 1024 + ch * 512, [[1, 512]]),
                          False, True, after=[tok_bor])
                bank[bnm].wrote(last)
                t_add = op("v", "tensor_tensor", x[:, tt, ch * 512:(ch + 1) * 512],
                           dstp[:, 0:512], x[:, tt, ch * 512:(ch + 1) * 512], ALU.add,
                           after=[last])
                bank[bnm].read(t_add)
                tok_y[tt] = t_add
        wb_free[0] = last
        return tok_y

    def emit_ffn(tok_res, second):
        eg16 = t_eg16.ap()
        pg16 = t_pg16.ap()
        eg_toks = []
        prev = tok_res[0]
        for tt in range(NT):
            eview = rap(t_eg16, NT * 128, 0, 128, tt * 128, [[16, 8], [1, 16]])
            t0 = op("g", "memset", eview, 1.0, after=[prev])
            t1 = op("g", "affine_select", out=eview, in_=eview,
                    pattern=[[-1, 8], [0, 16]], compare_op=ALU.is_ge, fill=0.0,
                    base=-8 * tt, channel_multiplier=1, after=[t0])
            t2 = op("g", "affine_select", out=eview, in_=eview,
                    pattern=[[1, 8], [0, 16]], compare_op=ALU.is_ge, fill=0.0,
                    base=8 * tt, channel_multiplier=-1, after=[t1])
            prev = t2
            eg_toks.append(t2)
        pg_toks = []
        for tt in range(NT):
            db = "D0" if tt % 2 == 0 else "D1"
            dp = pD[0] if db == "D0" else pD[1]
            t_tp = tr(dp[:, 0:128], eg16[:, tt, :],
                      after=[eg_toks[tt], tok_id] + bank[db].for_write())
            bank[db].wrote(t_tp)
            t_pg = op("v", "tensor_scalar", pg16[:, tt, :], dp[:, 0:128], 1.0 / 16.0,
                      None, ALU.mult, after=[t_tp])
            bank[db].read(t_pg)
            pg_toks.append(t_pg)

        pool_last = None
        for tt in range(NT):
            t_ap = emit_ln_tile(tt, tok_res[tt], pA[1][:], "A1", True)
            t_hi = op("v", "tensor_copy", xt[:], scrf[:], after=[hard(t_ap), pool_last])
            t_lo = op("v", "tensor_sub", xlo[:], scrf[:], xt[:], after=[hard(t_hi)])
            for src, src_tok in ((xt, t_hi), (xlo, t_lo)):
                for ch in range(2):
                    aft = [pg_toks[tt], src_tok]
                    if tt == 0 and src is xt and ch == 0:
                        aft += bank["A0"].for_write()
                    pool_last = mm(pA[0][:, ch * 512:(ch + 1) * 512], pg16[:, tt, :],
                                   src[:, ch * 512:(ch + 1) * 512],
                                   tt == 0 and src is xt,
                                   tt == NT - 1 and src is xlo and ch == 1,
                                   after=aft)
        bank["A0"].wrote(pool_last)
        tok_lt = dma("y", lowtri[:], d_lowtri[:], after=[pool_last])
        tok_ws1 = dma("y", f_ws1[:], d_fWs1.rearrange("(a p) n -> p a n", p=128),
                      after=[wb_free[0]] if wb_free[0] else [])

        t_g = op("v", "tensor_copy", f_gtm[:], pA[0][:], after=[pool_last])
        dbg_point("fG", [t_g])
        bank["A0"].read(t_g)
        t_gh = op("v", "tensor_copy", f_ghl[:, 0, :], f_gtm[:], after=[hard(t_g)])
        t_gl = op("v", "tensor_sub", f_ghl[:, 1, :], f_gtm[:], f_ghl[:, 0, :],
                  after=[hard(t_gh)])
        gt_toks = []
        for hl in range(2):
            src_tok = t_gh if hl == 0 else t_gl
            for half in range(2):
                db = "D0" if half == 0 else "D1"
                dp = pD[0] if half == 0 else pD[1]
                tts = []
                for c4 in range(4):
                    c = half * 4 + c4
                    tts.append(tr(dp[:, c4 * 128:(c4 + 1) * 128],
                                  f_ghl[:, hl, c * 128:(c + 1) * 128],
                                  after=[src_tok] + (bank[db].for_write() if c4 == 0 else [])))
                bank[db].wrote(tts[-1])
                dstap = rap(t_fgfm, 2048, 0, 128, hl * 1024 + (half * 4) * 128,
                            [[128, 4], [1, 128]])
                srcap = rap(h_pD0 if half == 0 else h_pD1, 1024, 0, 128, 0,
                            [[128, 4], [1, 128]])
                ev = op("v", "tensor_copy", dstap, srcap, after=[tts[-1]])
                bank[db].read(ev)
                gt_toks.append(ev)
        t_w1 = op("v", "tensor_copy", f_w1h[:, 0], f_ws1.rearrange("p a n -> p (a n)"),
                  after=[tok_ws1])
        t_w1l = op("v", "tensor_sub", f_w1h[:, 1], f_ws1.rearrange("p a n -> p (a n)"),
                   f_w1h[:, 0], after=[hard(t_w1)])
        t_hm = [None, None]
        for m_ in range(2):
            bnm = "B0"
            first = True
            last = None
            for a in range(NCH):
                for wi in range(2):
                    for gi in range(2):
                        aft = []
                        if first:
                            aft = [t_w1l] + gt_toks + bank[bnm].for_write()
                        last = mm(pB[0][:, 0:128],
                                  rap(t_fw1h, 4096, 0, 128, wi * 2048 + a * 256 + m_ * 128,
                                      [[1, 128]]),
                                  f_gfm[:, gi, a, :], first,
                                  a == NCH - 1 and wi == 1 and gi == 1, after=aft)
                        first = False
            bank[bnm].wrote(last)
            t_hm[m_] = op("s", "activation", f_h[:, m_, :], pB[0][:, 0:128], AFT.Relu,
                          bias=bs1c[:, m_:m_ + 1], after=[last, tok_b1])
            bank[bnm].read(t_hm[m_])
        t_hh = op("v", "tensor_copy", f_hhl[:, 0], f_h.rearrange("p a n -> p (a n)"),
                  after=[t_hm[0], t_hm[1]])
        t_hl = op("v", "tensor_sub", f_hhl[:, 1], f_h.rearrange("p a n -> p (a n)"),
                  f_hhl[:, 0], after=[hard(t_hh)])
        t_2h = op("v", "tensor_copy", ws2hl[:, 0:2], ws2c[:], after=[tok_w2])
        t_2l = op("v", "tensor_sub", ws2hl[:, 2:4], ws2c[:], ws2hl[:, 0:2],
                  after=[hard(t_2h)])
        first = True
        last = None
        for m_ in range(2):
            for wi in range(2):
                for gi in range(2):
                    aft = []
                    if first:
                        aft = [t_hl, t_2l] + bank["B1"].for_write()
                    last = mm(pB[1][0:1, 0:128], ws2hl[:, 2 * wi + m_:2 * wi + m_ + 1],
                              f_hhl[:, gi, m_, :], first,
                              m_ == 1 and wi == 1 and gi == 1, after=aft)
                    first = False
        bank["B1"].wrote(last)
        t_sr = op("v", "tensor_scalar", f_srow[0:1, 0:128], pB[1][0:1, 0:128],
                  bs2c[0:1, 0:1], None, ALU.add, after=[last, tok_b2])
        dbg_point("fscore", [t_sr, t_hm[0], t_hm[1]])
        bank["B1"].read(t_sr)
        t_sc = dma("y", f_scol[:, 0:1], f_srow[0:1, 0:128], after=[t_sr])
        t_sb = mm(pA[1][:, 0:128], onesf[0:1, 0:128], f_srow[0:1, 0:128], True, True,
                  after=[t_sr, tok_onesf] + bank["A1"].for_write())
        bank["A1"].wrote(t_sb)
        t_cmp = op("v", "tensor_scalar", f_cmp[:], pA[1][:, 0:128], f_scol[:, 0:1], None,
                   ALU.is_gt, after=[t_sb, t_sc])
        t_eq = op("v", "scalar_tensor_tensor", f_madd[:], pA[1][:, 0:128], f_scol[:, 0:1],
                  lowtri[:], ALU.is_equal, ALU.mult, after=[hard(t_cmp), t_sc, tok_lt])
        bank["A1"].read(t_eq)
        t_add_ = op("v", "tensor_add", f_cmp[:], f_cmp[:], f_madd[:],
                    after=[hard(t_eq)])
        t_red_ = op("v", "tensor_reduce", f_cnt[:, 0:1], f_cmp[:], AXX, ALU.add,
                    after=[hard(t_add_)])
        t_mq = op("v", "tensor_tensor", f_mq[:, 0:1], f_cnt[:, 0:1], constcol[:, 1:2],
                  ALU.is_lt, after=[tok_cc2, hard(t_red_)])
        dbg_point("fcount", [t_mq])
        t_mqb = op("v", "tensor_copy", f_mqb[:, 0:1], f_mq[:, 0:1], after=[hard(t_mq)])
        t_mr = mm(pB[1][0:1, 0:128], f_mqb[:, 0:1], identb[:], True, True,
                  after=[t_mqb] + bank["B1"].for_write())
        bank["B1"].wrote(t_mr)
        t_mre = op("v", "tensor_copy", f_mrow[0:1, 0:128], pB[1][0:1, 0:128], after=[t_mr])
        bank["B1"].read(t_mre)
        t_mb = mm(pA[1][:, 0:128], onesf[0:1, 0:128], f_mrow[0:1, 0:128], True, True,
                  after=[t_mre] + bank["A1"].for_write())
        bank["A1"].wrote(t_mb)
        t_madd = op("v", "tensor_scalar", f_madd[:], pA[1][:, 0:128], 1.0, 1e30,
                    ALU.subtract, ALU.mult, after=[t_mb])
        bank["A1"].read(t_madd)
        dbg_point("fmask", [t_madd, t_mq])

        wo_tok = wfull_load(wB, d_fWo, after=[wb_free[0]])
        sl_i = [0]
        sl_cons = [None, None]
        def stream_mm(dsrc, out_sb, token_major, after_extra):
            toks = []
            src4 = dsrc.rearrange("(a p) (c n) -> p a c n", p=128, n=128)
            for m_ in range(NCH):
                par_ = sl_i[0] % 2
                sl_i[0] += 1
                t_ld = dma("g", wsl[:, par_], src4[:, :, m_, :], after=[sl_cons[par_]])
                bnm = "B0" if m_ % 2 == 0 else "B1"
                dstp = pB[0] if bnm == "B0" else pB[1]
                last = None
                for a in range(NCH):
                    aft = []
                    if a == 0:
                        aft = [t_ld] + after_extra + bank[bnm].for_write()
                    if token_major:
                        last = mm(dstp[:, 0:128], f_gfm[:, 0, a, :], wsl[:, par_, a, :],
                                  a == 0, a == NCH - 1, after=aft)
                    else:
                        last = mm(dstp[:, 0:128], wsl[:, par_, a, :], f_gfm[:, 0, a, :],
                                  a == 0, a == NCH - 1, after=aft)
                bank[bnm].wrote(last)
                sl_cons[par_] = last
                ev = op("v", "tensor_copy", out_sb[:, m_, :], dstp[:, 0:128], after=[last])
                bank[bnm].read(ev)
                toks.append(ev)
            return toks

        tq = stream_mm(d_fWq, f_qg, False, gt_toks)
        tk = stream_mm(d_fWk, f_kg, False, [])
        tv = stream_mm(d_fWv, f_vg, True, [])

        ctx_toks = []
        for h in range(H):
            ch_, pr = h // 2, 64 * (h % 2)
            lq = rap(t_fqg, 1024, pr, 64, ch_ * 128, [[1, 128]])
            lk = rap(t_fkg, 1024, pr, 64, ch_ * 128, [[1, 128]])
            t_sg = mm(pB[0][:, 0:128], lq, lk, True, True,
                      after=[tq[-1], tk[-1], t_madd] + bank["B0"].for_write())
            bank["B0"].wrote(t_sg)
            t_sm = op("v", "tensor_add", f_sm[:], pB[0][:, 0:128], f_madd[:], after=[t_sg])
            bank["B0"].read(t_sm)
            t_nmx = op("v", "tensor_reduce", f_cnt[:, 1:2], f_sm[:], AXX, ALU.max,
                       negate=True)
            t_bias_ = op("s", "mul", f_cnt[:, 2:3], f_cnt[:, 1:2], 0.125, after=[t_nmx])
            t_p = op("s", "activation", f_pb[:], f_sm[:], AFT.Exp,
                     bias=f_cnt[:, 2:3], scale=0.125, accum_out=f_cnt[:, 3:4],
                     after=[hard(t_bias_)])
            t_rd = op("v", "reciprocal", f_cnt[:, 1:2], f_cnt[:, 3:4], after=[t_p])
            t_pn = op("v", "tensor_scalar", f_pb[:], f_pb[:], f_cnt[:, 1:2], None,
                      ALU.mult, after=[hard(t_rd)])
            db = "D0" if h % 2 == 0 else "D1"
            dp = pD[0] if h % 2 == 0 else pD[1]
            t_tp = tr(dp[:, 0:128], f_pb[:],
                      after=[t_pn] + bank[db].for_write())
            bank[db].wrote(t_tp)
            t_pt = op("v", "tensor_copy", f_pt[:], dp[:, 0:128], after=[t_tp])
            bank[db].read(t_pt)
            lv = rap(t_fvg, 1024, 0, 128, ch_ * 128 + pr, [[1, 64]])
            if h % 2 == 0:
                t_cx = mm(pB[1][0:64, 0:128], lv, f_pt[:], True, True,
                          after=[t_pt, tv[-1]] + bank["B1"].for_write())
                rows = slice(0, 64)
            else:
                t_cx = mm(pB[1][64:128, 0:128], lv, f_pt[:], True, True,
                          after=[t_pt, tv[-1]] + bank["B1"].for_write(), tp=(0, 64))
                rows = slice(64, 128)
            bank["B1"].wrote(t_cx)
            t_cv = op("v", "tensor_copy",
                      rap(t_fctx, 1024, rows.start, 64, ch_ * 128, [[1, 128]]),
                      pB[1][rows, 0:128], after=[t_cx])
            bank["B1"].read(t_cv)
            ctx_toks.append(t_cv)

        og_toks = []
        last_og_mm = [None]
        for ch in range(2):
            bnm = "B0"
            last = None
            for a in range(NCH):
                aft = []
                if a == 0:
                    aft = [ctx_toks[-1]] + wo_tok + bank[bnm].for_write()
                last = mm(pB[0][:, 0:512], f_ctx[:, a, :], wB[:, a, ch * 512:(ch + 1) * 512],
                          a == 0, a == NCH - 1, after=aft)
            bank[bnm].wrote(last)
            last_og_mm[0] = last
            t_og = op("v", "tensor_scalar", f_og[:, ch * 512:(ch + 1) * 512],
                      pB[0][:, 0:512], f_mq[:, 0:1], None, ALU.mult, after=[last])
            bank[bnm].read(t_og)
            og_toks.append(t_og)
        dbg_point("fog", og_toks + ctx_toks)
        tok_y = [None] * NT
        for tt in range(NT):
            for ch in range(2):
                bnm = "B1" if (tt * 2 + ch) % 2 == 0 else "B0"
                dstp = pB[1] if bnm == "B1" else pB[0]
                t_y = mm(dstp[:, 0:512], eg16[:, tt, :],
                         f_og[:, ch * 512:(ch + 1) * 512],
                         True, True,
                         after=[eg_toks[tt]] + og_toks + bank[bnm].for_write())
                bank[bnm].wrote(t_y)
                t_add = op("v", "tensor_tensor", x[:, tt, ch * 512:(ch + 1) * 512],
                           dstp[:, 0:512], x[:, tt, ch * 512:(ch + 1) * 512], ALU.add,
                           after=[t_y, tok_res[tt]])
                bank[bnm].read(t_add)
                tok_y[tt] = t_add
        wb_free[0] = last_og_mm[0]
        return tok_y

    def emit_attn_b(tok_res):
        tok_xn = emit_xn_for_attn(tok_res)
        vpi = [op("v", "memset",
                  rap(t_vp, 6208, 0, 128, hh * 65 + 64, [[260, NT], [1, 1]]), 1.0)
               for hh in range(4)]
        slab_tok = {"q": wslab_load(wq_s, d_bWq, 0),
                    "k": wslab_load(wk_s, d_bWk, 0),
                    "v": wslab_load(wv_s, d_bWv, 0)}
        slab_cons = {"q": None, "k": None, "v": None}
        wo_tok = wfull_load(wB, d_bWo, after=[wb_free[0]])
        tok_wft0 = dma("g", wft[0:HD, :], d_bWf[:], after=[wb_free[0]])
        tok_wft = dma("g", wft[HD:128, :], d_bWf[:], after=[wb_free[0]])
        PS = [pB[0][:, 0:512], pB[1][:, 0:512], pA[0][:, 0:512], pA[1][:, 0:512]]
        PSB = ["B0", "B1", "A0", "A1"]
        last_out_ev = None
        qk_ev = [None]
        v_ev = [None]
        for j in range(4):
            for phase in ("q", "k"):
                slab_t = {"q": t_wq, "k": t_wk}[phase]
                for b in range(NB):
                    si = 2 * (b % 2)
                    for ci in range(2):
                        dst, bnm = PS[si + ci], PSB[si + ci]
                        first_after = ([tok_xn[4 * b + 3]] + slab_tok[phase]
                                       + bank[bnm].for_write())
                        last = None
                        for a in range(NCH):
                            last = mm(dst, rap(slab_t, 2048, 0, 128, a * 256 + ci * 128,
                                               [[1, 128]]),
                                      xn[:, a, b * 512:(b + 1) * 512],
                                      a == 0, a == NCH - 1,
                                      after=(first_after if a == 0 else []))
                        bank[bnm].wrote(last)
                        slab_cons[phase] = last
                        if phase == "q":
                            dstsb = qc[:, (j, j + 4)[ci], b * 512:(b + 1) * 512]
                        else:
                            dstsb = kp[:, ci, b * 512:(b + 1) * 512]
                        ev = op("v", "tensor_copy", dstsb, dst, after=[last])
                        bank[bnm].read(ev)
                        qk_ev[0] = ev
                if j < 3:
                    slab_tok[phase] = wslab_load(
                        {"q": wq_s, "k": wk_s}[phase],
                        {"q": d_bWq, "k": d_bWk}[phase], j + 1, after=[slab_cons[phase]])
            for tt in range(NT):
                bnm = "B0" if tt % 2 == 0 else "B1"
                dstp = pB[tt % 2][:, 0:256]
                first_after = [tok_xn[tt]] + slab_tok["v"] + vpi + bank[bnm].for_write()
                last = None
                for a in range(NCH):
                    last = mm(dstp, xn[:, a, tt * 128:(tt + 1) * 128], wv_s[:, a],
                              a == 0, a == NCH - 1,
                              after=(first_after if a == 0 else []))
                bank[bnm].wrote(last)
                slab_cons["v"] = last
                pbh = h_pB0 if tt % 2 == 0 else h_pB1
                ev = op("v", "tensor_copy",
                        rap(t_vp, 6208, 0, 128, tt * 260, [[65, 4], [1, 64]]),
                        rap(pbh, 512, 0, 128, 0, [[64, 4], [1, 64]]), after=[last])
                bank[bnm].read(ev)
                v_ev[0] = ev
            if j < 3:
                slab_tok["v"] = wslab_load(wv_s, d_bWv, j + 1, after=[slab_cons["v"]])
            if j == 0:
                dbg_point("bqk", [qk_ev[0], v_ev[0]])
            HHROW = [0, 64, 0, 64]
            HHCH = [j, j, j + 4, j + 4]
            HHKS = [0, 0, 1, 1]
            kv_started = False
            kv_last = None
            kf_guard = [None, None]
            eb_guard = [None, None]
            for tt in range(NT):
                ebuf = xt[:] if tt % 2 == 0 else xt2[:]
                sb = "A0" if tt % 2 == 0 else "A1"
                pfp = pA[0] if tt % 2 == 0 else pA[1]
                pf_last = None
                PFCOL = [0, 512, 256, 768]
                for hh in range(4):
                    aft = [tok_wft0, tok_wft, qk_ev[0]]
                    if hh == 0:
                        aft += bank[sb].for_write()
                    pf_last = mm(pfp[:, PFCOL[hh]:PFCOL[hh] + 256],
                                 rap(t_kp, 2 * S, HHROW[hh], 64,
                                     HHKS[hh] * S + tt * 128, [[1, 128]]),
                                 wft[HHROW[hh]:HHROW[hh] + 64, :], True, True,
                                 after=aft)
                bank[sb].wrote(pf_last)
                t_e = op("s", "activation", ebuf, pfp[:], AFT.Exp,
                         after=[pf_last, eb_guard[tt % 2]])
                t_m1 = op("v", "tensor_scalar", ebuf, ebuf, 1.0, None, ALU.min,
                          after=[t_e])
                t_kf = op("v", "scalar_tensor_tensor", kfb[:, tt % 2, :], pfp[:],
                          0.0, ebuf, ALU.max, ALU.add,
                          after=[hard(t_m1), kf_guard[tt % 2]])
                bank[sb].read(t_kf)
                eb_guard[tt % 2] = t_kf
                if j == 0 and tt == 0:
                    dbg_point("bkf1", [t_kf])
                for hh in range(4):
                    for fc in range(2):
                        aft = [t_kf]
                        if not kv_started:
                            aft += (bank["B0"].for_write() + bank["B1"].for_write()
                                    + [v_ev[0]])
                        first_in_bank = (tt == 0 and hh % 2 == 0 and fc == 0)
                        kv_last = mm(pB[hh // 2][:, (hh % 2) * 130 + fc * 65:
                                                 (hh % 2) * 130 + fc * 65 + 65],
                                     kfb[:, tt % 2, PFCOL[hh] + fc * 128:
                                         PFCOL[hh] + fc * 128 + 128],
                                     rap(t_vp, 6208, 0, 128, tt * 260 + hh * 65,
                                         [[1, 65]]),
                                     first_in_bank,
                                     tt == NT - 1 and hh == 3 and fc == 1,
                                     after=aft)
                        kv_started = True
                kf_guard[tt % 2] = kv_last
                if j == 0 and tt == 0:
                    dbg_point("bkv2", [kv_last])
                if j == 0 and tt == 2:
                    dbg_point("bkv3", [kv_last])
            bank["B0"].wrote(kv_last)
            bank["B1"].wrote(kv_last)
            t_kv0 = op("v", "tensor_copy", kvsb[:, 0:2], pB[0][:, 0:260], after=[kv_last])
            t_kv = op("v", "tensor_copy", kvsb[:, 2:4], pB[1][:, 0:260], after=[kv_last])
            bank["B0"].read(t_kv0)
            bank["B1"].read(t_kv)
            if j == 0:
                dbg_point("bkv", [t_kv0, t_kv])
            out_guard = None
            for b in range(NB):
                qf_toks = []
                for hh in range(4):
                    sb = "A0" if hh % 2 == 0 else "A1"
                    dstp = pA[0] if hh % 2 == 0 else pA[1]
                    ebuf = xt[:] if hh % 2 == 0 else xt2[:]
                    last = None
                    for fc in range(2):
                        aft = [tok_wft0, tok_wft, qk_ev[0]]
                        if fc == 0:
                            aft += bank[sb].for_write()
                        last = mm(dstp[:, fc * 512:(fc + 1) * 512],
                                  rap(t_wft, FD, HHROW[hh], 64, fc * 128, [[1, 128]]),
                                  rap(t_qc, NCH * S, HHROW[hh], 64,
                                      HHCH[hh] * S + b * 512, [[1, 512]]),
                                  True, True, after=aft)
                    bank[sb].wrote(last)
                    t_e = op("s", "activation", ebuf, dstp[:], AFT.Exp,
                             after=[last, eb_guard[hh % 2]])
                    t_m1 = op("v", "tensor_scalar", ebuf, ebuf, 1.0, None, ALU.min,
                              after=[t_e])
                    t_qf = op("v", "scalar_tensor_tensor",
                              rap(t_qf4, 4096, 0, 128, hh * 1024, [[1, 1024]]),
                              dstp[:], 0.0, ebuf, ALU.max, ALU.add,
                              after=[hard(t_m1), out_guard])
                    bank[sb].read(t_qf)
                    eb_guard[hh % 2] = t_qf
                    qf_toks.append(t_qf)
                for t4 in range(4):
                    tt = b * 4 + t4
                    last = None
                    for hh in range(4):
                        for fc in range(2):
                            aft = [t_kv, qf_toks[hh]]
                            if hh == 0 and fc == 0:
                                aft += bank["B1"].for_write()
                            last = mm(pB[1][:, hh * 65:hh * 65 + 65],
                                      rap(t_qf4, 4096, 0, 128,
                                          hh * 1024 + fc * 512 + t4 * 128, [[1, 128]]),
                                      kvsb[:, hh, fc, :],
                                      fc == 0, fc == 1, after=aft)
                    bank["B1"].wrote(last)
                    out_guard = last
                    t_n = op("v", "tensor_scalar", rr[:, 0:4],
                             rap(h_pB1, 512, 0, 128, 64, [[65, 4]]),
                             1e-6, None, ALU.add, after=[last])
                    t_r = op("v", "reciprocal", rr[:, 4:8], rr[:, 0:4],
                             after=[hard(t_n)])
                    ot_toks = []
                    for hh in range(4):
                        blk, par2 = hh // 2, hh % 2
                        ot_toks.append(op("v", "tensor_scalar",
                                          otm[:, blk * 128 + par2 * 64:
                                              blk * 128 + par2 * 64 + 64],
                                          pB[1][:, hh * 65:hh * 65 + 64],
                                          rr[:, 4 + hh:5 + hh], None, ALU.mult,
                                          after=[hard(t_r)]))
                    bank["B1"].read(ot_toks[-1])
                    for blk in range(2):
                        db = "D0" if (tt * 2 + blk) % 2 == 0 else "D1"
                        dp = pD[0] if db == "D0" else pD[1]
                        dph = h_pD0 if db == "D0" else h_pD1
                        t_tp = tr(dp[:, 0:128], otm[:, blk * 128:(blk + 1) * 128],
                                  after=[ot_toks[2 * blk], ot_toks[2 * blk + 1]]
                                  + bank[db].for_write())
                        bank[db].wrote(t_tp)
                        ev = op("v", "tensor_copy",
                                rap(t_qc, NCH * S, 0, 128,
                                    HHCH[2 * blk] * S + tt * 128, [[1, 128]]),
                                rap(dph, 1024, 0, 128, 0, [[1, 128]]), after=[t_tp])
                        bank[db].read(ev)
                        last_out_ev = ev
            if j == 0:
                dbg_point("bout", [last_out_ev])
        tok_y = [None] * NT
        for tt in range(NT):
            for ch in range(2):
                bnm = "B0" if (tt * 2 + ch) % 2 == 0 else "B1"
                dstp = pB[0] if bnm == "B0" else pB[1]
                first_after = [last_out_ev] + wo_tok + bank[bnm].for_write()
                last = None
                for a in range(NCH):
                    last = mm(dstp[:, 0:512], qc[:, a, tt * 128:(tt + 1) * 128],
                              wB[:, a, ch * 512:(ch + 1) * 512],
                              a == 0, a == NCH - 1, after=(first_after if a == 0 else []))
                bank[bnm].wrote(last)
                t_add = op("v", "tensor_tensor", x[:, tt, ch * 512:(ch + 1) * 512],
                           dstp[:, 0:512], x[:, tt, ch * 512:(ch + 1) * 512], ALU.add,
                           after=[last])
                bank[bnm].read(t_add)
                tok_y[tt] = t_add
        wb_free[0] = last
        return tok_y

    wb_free = [None]
    toks = tok_x
    try:
        if n_sub >= 1:
            toks = emit_attn_a(toks)
        if n_sub >= 2:
            toks = emit_ffn(toks, False)
        if n_sub >= 3:
            toks = emit_attn_b(toks)
        if n_sub >= 4:
            toks = emit_ffn(toks, True)
    except StopBuild:
        pass

    st_toks = [B.dma("y", (lambda e, tt=tt: e.dma_start(
        out=dout[tt * 128:(tt + 1) * 128, :], in_=x[:, tt, :])), after=[toks[tt]])
        for tt in range(NT)]
    B.op("y", lambda e: e.nop(), after=st_toks + dbg_outs)

    import contextlib
    es = contextlib.ExitStack()
    B.finalize(nc, es)
    with nc.Block() as block:
        block.tensor(lambda e: B.emit("t", e))
        block.vector(lambda e: B.emit("v", e))
        block.scalar(lambda e: B.emit("s", e))
        block.gpsimd(lambda e: B.emit("g", e))
        block.sync(lambda e: B.emit("y", e))
    es.close()
    return nc


def _host_prep(inp):
    import jax
    import jax.numpy as jnp
    f = lambda a: np.ascontiguousarray(np.asarray(a, dtype=np.float32))
    perm = np.concatenate([np.arange(0, D, 2), np.arange(1, D, 2)])
    m = {}
    m["aWq"] = f(inp["L0_attn_a_Wq"])[:, perm]
    m["aWk"] = f(inp["L0_attn_a_Wk"])[:, perm]
    m["aWv"] = f(inp["L0_attn_a_Wv"])
    m["aWo"] = f(inp["L0_attn_a_Wo"])
    m["abq"] = np.ascontiguousarray(f(inp["L0_attn_a_bq"])[perm].reshape(8, 128).T)
    m["abk"] = np.ascontiguousarray(f(inp["L0_attn_a_bk"])[perm].reshape(8, 128).T)
    m["abv"] = f(inp["L0_attn_a_bv"]).reshape(1, D)
    m["abo"] = f(inp["L0_attn_a_bo"]).reshape(1, D)
    m["bWqT"] = np.ascontiguousarray(f(inp["L0_attn_b_Wq"]).T)
    m["bWkT"] = np.ascontiguousarray(f(inp["L0_attn_b_Wk"]).T)
    m["bWvT"] = np.ascontiguousarray(f(inp["L0_attn_b_Wv"]).T)
    m["bWoT"] = np.ascontiguousarray(f(inp["L0_attn_b_Wo"]).T)
    m["bWfT"] = np.ascontiguousarray(f(inp["L0_attn_b_Wf"]).T)
    m["fWq"] = f(inp["L0_ffn_Wq"])
    m["fWk"] = f(inp["L0_ffn_Wk"])
    m["fWv"] = f(inp["L0_ffn_Wv"])
    m["fWo"] = f(inp["L0_ffn_Wo"])
    m["fWs1"] = f(inp["L0_ffn_Ws1"])
    m["fbs1"] = np.ascontiguousarray(f(inp["L0_ffn_bs1"]).reshape(2, 128).T)
    m["fWs2"] = np.ascontiguousarray(f(inp["L0_ffn_Ws2"]).reshape(2, 128).T)
    m["fbs2"] = f(inp["L0_ffn_bs2"]).reshape(1, 1)
    cpu = jax.devices("cpu")[0]
    with jax.default_device(cpu):
        fr = jnp.arange(D // 2, dtype=jnp.float32)
        fc = jnp.power(10000.0, -2.0 * fr / D)
        pos = jnp.arange(S, dtype=jnp.float32)
        freq = jnp.einsum("i,j->ij", pos, fc)
        sin_e = np.asarray(jnp.sin(freq), dtype=np.float32)
        cos_e = np.asarray(jnp.cos(freq), dtype=np.float32)
    m["ropesin"] = np.ascontiguousarray(sin_e.T.reshape(4, 128, S))
    m["ropecos"] = np.ascontiguousarray(cos_e.T.reshape(4, 128, S))
    m["identc"] = np.eye(128, dtype=np.float32)
    m["lowtri"] = np.tril(np.ones((128, 128), dtype=np.float32), -1)
    ca = np.zeros((128, 128), dtype=np.float32)
    kk = np.arange(128)[:, None]
    qq = np.arange(128)[None, :]
    ca[qq < kk] = -1e30
    m["causadd"] = ca
    m["tri01"] = (qq >= kk).astype(np.float32)
    return m


_N_SUB = 4


def kernel(**inputs):
    xfull = np.asarray(inputs["L0_attn_a_x"], dtype=np.float32)
    common = _host_prep(inputs)
    nc = build_nc(_N_SUB)
    in_maps = []
    for b in range(8):
        im = dict(common)
        im["x_in"] = np.ascontiguousarray(xfull[b])
        in_maps.append(im)
    res = run_bass_kernel_spmd(nc, in_maps, core_ids=list(range(8)))
    return np.stack([np.asarray(res.results[b]["x_out"]) for b in range(8)], axis=0)


# version 07
# speedup vs baseline: 1.1607x; 1.0125x over previous
import numpy as np
import concourse.bass as bass
import concourse.mybir as mybir
import bass_rust
from concourse.alu_op_type import AluOpType as ALU
from concourse.bass_utils import run_bass_kernel_spmd

F32 = mybir.dt.float32
BF16 = mybir.dt.bfloat16
AFT = mybir.ActivationFunctionType
AXX = mybir.AxisListType.X

S, D, H, HD, FD = 2048, 1024, 16, 64, 256
NT, NB, NCH = 16, 4, 8
ENGS = ("t", "v", "s", "g", "y")


class Sched:
    def __init__(self):
        self.items = {e: [] for e in ENGS}
        self.ndma = {e: 0 for e in ENGS}

    def op(self, e, fn, after=()):
        self.items[e].append([[t for t in after if t is not None], fn, False, False])
        return ("i", e, len(self.items[e]) - 1)

    def dma(self, e, fn, after=()):
        self.items[e].append([[t for t in after if t is not None], fn, False, True])
        self.ndma[e] += 1
        return ("d", e, self.ndma[e] * 16)

    def finalize(self, nc, es):
        for e in ENGS:
            for it in self.items[e]:
                for tok in it[0]:
                    if tok[0] in ("i", "h"):
                        assert not self.items[tok[1]][tok[2]][3]
                        self.items[tok[1]][tok[2]][2] = True
        self.val = {}
        for e in ENGS:
            c, vals = 0, []
            for it in self.items[e]:
                if it[2]:
                    c += 1
                vals.append(c)
            self.val[e] = vals
        self.psem = {e: es.enter_context(nc.semaphore(f"p_{e}")) for e in ENGS}
        self.dsem = {e: es.enter_context(nc.semaphore(f"d_{e}"))
                     for e in ENGS if self.ndma[e]}

    def emit(self, e, eng):
        waited = {}
        for waits, fn, inc, is_dma in self.items[e]:
            for tok in waits:
                if tok[0] in ("i", "h"):
                    pe, idx = tok[1], tok[2]
                    if pe == e and tok[0] == "i":
                        continue
                    v = self.val[pe][idx]
                    if waited.get(("i", pe), 0) < v:
                        eng.wait_ge(self.psem[pe], v)
                        waited[("i", pe)] = v
                else:
                    qe, v = tok[1], tok[2]
                    if waited.get(("d", qe), 0) < v:
                        eng.wait_ge(self.dsem[qe], v)
                        waited[("d", qe)] = v
            inst = fn(eng)
            if is_dma:
                inst.then_inc(self.dsem[e], 16)
            elif inc:
                inst.then_inc(self.psem[e], 1)


def hard(tok):
    assert tok[0] == "i"
    return ("h", tok[1], tok[2])


class Bank:
    def __init__(self):
        self.w = None
        self.r = None

    def for_write(self):
        out = [t for t in (self.w, self.r) if t is not None]
        return out

    def wrote(self, tok):
        self.w, self.r = tok, None

    def for_read(self):
        return [self.w] if self.w is not None else []

    def read(self, tok):
        self.r = tok


class StopBuild(Exception):
    pass


def build_nc(n_sub=4, dbg=None):
    nc = bass.Bass("TRN2", target_bir_lowering=False)
    B = Sched()

    def op(eng, meth, *args, after=(), **kw):
        return B.op(eng, (lambda e, m=meth, a=args, k=kw: getattr(e, m)(*a, **k)),
                    after=after)

    def dma(q, dst, src, after=()):
        return B.dma(q, (lambda e, d=dst, s=src: e.dma_start(out=d, in_=s)), after=after)

    dbg_outs = []

    def dbg_point(label, toks):
        if dbg != label:
            return
        dumps = [("xn", t_xn, 16384, BF16), ("qc", t_qc, 16384, BF16),
                 ("kp", t_kp, 4096, BF16), ("vp", t_vp, 6208, BF16),
                 ("et", t_et, 2048, BF16), ("scrf", t_scrf, 1024, F32),
                 ("xt", t_xt, 1024, BF16), ("st", t_st, 16, F32),
                 ("gtm", t_fgtm, 1024, F32), ("srow", t_fsrow, 128, F32),
                 ("mq", t_fmq, 8, F32), ("madd", t_fmadd, 128, F32),
                 ("og", t_fog, 1024, BF16), ("ctxf", t_fctx, 1024, BF16),
                 ("hh", t_fh, 256, F32), ("gfm", t_fgfm, 2048, BF16),
                 ("cmp", t_fcmp, 128, F32), ("cnt", t_fcnt, 8, F32), ("scol", t_fscol, 8, F32),
                 ("w1h", t_fw1h, 4096, BF16)]
        for nm, th, n, dt in dumps:
            o = nc.dram_tensor("dbg_" + nm, [128, n], F32, kind="ExternalOutput").ap()
            srcap = bass_rust.AP(th, 0, [[n, 128], [1, n]])
            dbg_outs.append(B.dma("g", (lambda e, oo=o, ss=srcap: e.dma_start(out=oo, in_=ss)),
                                  after=[t for t in toks if t is not None]))
        raise StopBuild()

    def mm(dst, lhs, rhs, start, stop, after=(), tp=None):
        return B.op("t", (lambda e, d=dst, l=lhs, r=rhs, st=start, sp=stop, t=tp:
                          e.matmul(d, l, r, start=st, stop=sp, tile_position=t)),
                    after=after)

    def tr(dst, src, after=()):
        return B.op("t", (lambda e, d=dst, s=src: e.transpose(d, s, identb_late[0])),
                    after=after)

    DI = lambda name, shape: nc.dram_tensor(name, shape, F32, kind="ExternalInput").ap()
    dx = DI("x_in", [S, D])
    dout = nc.dram_tensor("x_out", [S, D], F32, kind="ExternalOutput").ap()
    d_aWq, d_aWk, d_aWv, d_aWo = DI("aWq", [D, D]), DI("aWk", [D, D]), DI("aWv", [D, D]), DI("aWo", [D, D])
    d_abq, d_abk = DI("abq", [128, 8]), DI("abk", [128, 8])
    d_abv, d_abo = DI("abv", [1, D]), DI("abo", [1, D])
    d_bWq, d_bWk, d_bWv, d_bWo = DI("bWqT", [D, D]), DI("bWkT", [D, D]), DI("bWvT", [D, D]), DI("bWoT", [D, D])
    d_bWf = DI("bWfT", [HD, FD])
    d_fWq, d_fWk, d_fWv, d_fWo = DI("fWq", [D, D]), DI("fWk", [D, D]), DI("fWv", [D, D]), DI("fWo", [D, D])
    d_fWs1 = DI("fWs1", [D, 256])
    d_fbs1, d_fWs2, d_fbs2 = DI("fbs1", [128, 2]), DI("fWs2", [128, 2]), DI("fbs2", [1, 1])
    d_sin, d_cos = DI("ropesin", [4, 128, S]), DI("ropecos", [4, 128, S])
    d_ident, d_lowtri = DI("identc", [128, 128]), DI("lowtri", [128, 128])
    d_causadd = DI("causadd", [128, 128])
    d_tri01 = DI("tri01", [128, 128])

    _probe = nc.alloc_sbuf_tensor("sb_probe", [128, 8], F32)
    SB_BASE = nc.lookup_mloc(_probe).addr
    assert SB_BASE <= 16512 + 4096, SB_BASE
    cur = [SB_BASE]

    def alloc(name, shape, dtype, at=None):
        esz = 2 if dtype == BF16 else 4
        n = 1
        for d_ in shape[1:]:
            n *= d_
        if at is None:
            off = (cur[0] + 31) // 32 * 32
            cur[0] = off + n * esz
            assert cur[0] <= SB_BASE + 212800, f"SBUF overflow at {name}: {cur[0]}"
        else:
            off = at
        return nc.alloc_sbuf_tensor_at(name, shape, dtype, offset=off)

    t_x = alloc("xres", [128, NT, D], F32)
    QC_OFF = cur[0]
    t_qc = alloc("qc", [128, NCH, S], BF16)
    XN_OFF = cur[0]
    t_xn = alloc("xn", [128, NCH, S], BF16)
    t_kp = alloc("kp", [128, 2, S], BF16)
    t_vp = alloc("vp", [128, 6208], BF16)
    TBL_OFF = (cur[0] + 31) // 32 * 32
    t_tbl = alloc("tbl", [128, 2, 2, 512], F32)
    t_wq = alloc("wq_s", [128, 8, 2, 128], BF16)
    t_wk = alloc("wk_s", [128, 8, 2, 128], BF16)
    t_wv = alloc("wv_s", [128, 8, 2, 128], BF16)
    t_wB = alloc("wB", [128, 8, 1024], BF16)
    t_wsl = alloc("wsl", [128, 2, 8, 128], BF16)
    WSL_OFF = cur[0] - 4096
    t_bca = alloc("bca", [128, 512], F32, at=WSL_OFF)
    t_bias = alloc("biasrows", [128, 2, 1024], BF16)
    t_bqc = alloc("bq_col", [128, 8], F32)
    t_bkc = alloc("bk_col", [128, 8], F32)
    SCRF_OFF = (cur[0] + 31) // 32 * 32
    t_scrf = alloc("scr_f32", [128, 1024], F32)
    t_xt = alloc("xt_tile", [128, 1024], BF16)
    t_xe = alloc("xt2", [128, 1024], BF16, at=WSL_OFF + 2048)
    t_et = alloc("et", [128, 2, 1024], BF16)
    ET_OFF = cur[0] - 4096
    t_rr = alloc("rr", [128, 512], F32)
    t_st = alloc("stats", [128, 16], F32)
    t_id = alloc("identb", [128, 128], BF16)
    t_of = alloc("onesf", [128, 128], F32)
    t_ob = alloc("onesb", [128, 128], BF16)
    t_b1 = alloc("bs1c", [128, 2], F32)
    t_w2 = alloc("ws2c", [128, 2], F32)
    t_w2h = alloc("ws2hl", [128, 4], BF16)
    t_b2 = alloc("bs2c", [128, 1], F32)
    t_cm = alloc("causm", [128, 128], F32)
    t_tb = alloc("trib", [128, 128], BF16)
    t_eg = alloc("egen", [128, 128], BF16)
    t_pgn = alloc("pgen", [128, 128], BF16)
    t_cc = alloc("constcol", [128, 8], F32)
    t_qf4 = alloc("qfb4", [128, 4, 2, 512], BF16, at=TBL_OFF)
    t_et2 = alloc("et2", [128, 2, 1024], BF16, at=TBL_OFF)
    t_kf = alloc("kfb", [128, 2, 1024], BF16, at=ET_OFF)
    t_kv = alloc("kvsb", [128, 4, 2, 65], BF16, at=SCRF_OFF)
    t_ot = alloc("otm", [128, 256], BF16, at=SCRF_OFF + 2112)
    t_wft = alloc("wftb", [128, FD], BF16, at=SCRF_OFF + 2624)
    t_fws1 = alloc("f_ws1", [128, 8, 256], F32, at=QC_OFF)
    t_fw1h = alloc("f_w1h", [128, 2, 8, 256], BF16, at=QC_OFF + 8192)
    t_fgtm = alloc("f_gtm", [128, 1024], F32, at=QC_OFF + 16384)
    t_fghl = alloc("f_ghl", [128, 2, 1024], BF16, at=QC_OFF + 20480)
    t_fgfm = alloc("f_gfm", [128, 2, 8, 128], BF16, at=QC_OFF + 24576)
    t_fh = alloc("f_h", [128, 2, 128], F32, at=QC_OFF + 28672)
    t_fhhl = alloc("f_hhl", [128, 2, 2, 128], BF16, at=QC_OFF + 29696)
    t_fsrow = alloc("f_srow", [128, 128], F32, at=QC_OFF + 30720)
    t_fscol = alloc("f_scol", [128, 8], F32, at=QC_OFF + 31232)
    t_fcnt = alloc("f_cnt", [128, 8], F32, at=QC_OFF + 31264)
    t_fmq = alloc("f_mq", [128, 8], F32, at=QC_OFF + 31296)
    t_fmqb = alloc("f_mqb", [128, 16], BF16, at=QC_OFF + 31328)
    t_fmrow = alloc("f_mrow", [128, 128], F32, at=QC_OFF + 31360)
    t_fcmp = alloc("f_cmp", [128, 128], F32, at=XN_OFF)
    t_fmadd = alloc("f_madd", [128, 128], F32, at=XN_OFF + 512)
    t_fqg = alloc("f_qg", [128, 8, 128], BF16, at=XN_OFF + 1024)
    t_fkg = alloc("f_kg", [128, 8, 128], BF16, at=XN_OFF + 3072)
    t_fvg = alloc("f_vg", [128, 8, 128], BF16, at=XN_OFF + 5120)
    t_fctx = alloc("f_ctx", [128, 8, 128], BF16, at=XN_OFF + 7168)
    t_fog = alloc("f_og", [128, 1024], BF16, at=XN_OFF + 9216)
    t_fsm = alloc("f_sm", [128, 128], F32, at=XN_OFF + 11264)
    t_fpb = alloc("f_pb", [128, 128], BF16, at=XN_OFF + 11776)
    t_fpt = alloc("f_pt", [128, 128], BF16, at=XN_OFF + 12032)
    t_fxlo = alloc("f_xlo", [128, 1024], BF16, at=XN_OFF + 12288)
    t_lt = alloc("lowtrib", [128, 128], F32, at=XN_OFF + 14336)
    t_eg16 = alloc("egen16", [128, NT, 128], BF16, at=XN_OFF + 16384)
    t_pg16 = alloc("pgen16", [128, NT, 128], BF16, at=XN_OFF + 20480)

    h_pA0 = nc.alloc_psum_tensor("pA0", [128, 1024], F32)
    h_pA1 = nc.alloc_psum_tensor("pA1", [128, 1024], F32)
    h_pB0 = nc.alloc_psum_tensor("pB0", [128, 512], F32)
    h_pB1 = nc.alloc_psum_tensor("pB1", [128, 512], F32)
    h_pD0 = nc.alloc_psum_tensor("pD0", [128, 1024], BF16)
    h_pD1 = nc.alloc_psum_tensor("pD1", [128, 1024], BF16)
    pA = [h_pA0.ap(), h_pA1.ap()]
    pB = [h_pB0.ap(), h_pB1.ap()]
    pD = [h_pD0.ap(), h_pD1.ap()]
    pDf = [h_pD0.ap().bitcast(F32), h_pD1.ap().bitcast(F32)]
    bank = {"A0": Bank(), "A1": Bank(), "B0": Bank(), "B1": Bank(),
            "D0": Bank(), "D1": Bank()}

    identb_late = [None]
    x = t_x.ap(); qc = t_qc.ap(); xn = t_xn.ap(); kp = t_kp.ap(); vp = t_vp.ap()
    tbl = t_tbl.ap(); wq_s = t_wq.ap(); wk_s = t_wk.ap(); wv_s = t_wv.ap()
    wB = t_wB.ap(); wsl = t_wsl.ap(); biasr = t_bias.ap()
    bqc = t_bqc.ap(); bkc = t_bkc.ap(); scrf = t_scrf.ap(); xt = t_xt.ap()
    xlo = t_fxlo.ap(); et = t_et.ap(); rr = t_rr.ap(); bca = t_bca.ap()
    xt2 = t_xe.ap(); et2 = t_et2.ap()
    stats = t_st.ap(); identb = t_id.ap(); lowtri = t_lt.ap(); wft = t_wft.ap()
    identb_late[0] = identb[:]
    onesf = t_of.ap(); onesb = t_ob.ap()
    bs1c = t_b1.ap(); ws2c = t_w2.ap(); ws2hl = t_w2h.ap(); bs2c = t_b2.ap()
    causm = t_cm.ap(); trib = t_tb.ap()
    egen = t_eg.ap(); pgen = t_pgn.ap(); kvsb = t_kv.ap(); otm = t_ot.ap()
    qfb4 = t_qf4.ap(); kfb = t_kf.ap()
    f_ws1 = t_fws1.ap(); f_w1h = t_fw1h.ap(); f_gtm = t_fgtm.ap(); f_ghl = t_fghl.ap()
    f_gfm = t_fgfm.ap(); f_h = t_fh.ap(); f_hhl = t_fhhl.ap(); f_srow = t_fsrow.ap()
    f_scol = t_fscol.ap(); f_cnt = t_fcnt.ap(); f_mq = t_fmq.ap(); f_mqb = t_fmqb.ap()
    f_mrow = t_fmrow.ap(); f_cmp = t_fcmp.ap(); f_madd = t_fmadd.ap()
    f_qg = t_fqg.ap(); f_kg = t_fkg.ap(); f_vg = t_fvg.ap(); f_ctx = t_fctx.ap()
    f_og = t_fog.ap(); f_sm = t_fsm.ap(); f_pb = t_fpb.ap(); f_pt = t_fpt.ap()

    def rap(th, rowext, pstart, nparts, foff, dims):
        return bass_rust.AP(th, pstart * rowext + foff, [[rowext, nparts]] + dims)

    VA_TT = 386
    VA_OFF = [0, 65, 193, 258]
    VA_W = [65, 128, 65, 128]

    tok_x = [dma("y", x[:, tt, :], dx[tt * 128:(tt + 1) * 128, :]) for tt in range(NT)]
    tok_id = dma("g", identb[:], d_ident[:])
    tok_cau = dma("y", causm[:], d_causadd[:])
    tok_tri = dma("g", trib[:], d_tri01[:])
    tok_bqc = dma("y", bqc[:], d_abq[:])
    tok_bkc = dma("y", bkc[:], d_abk[:])
    tok_bvr = dma("g", biasr[0:1, 0, :], d_abv[:])
    tok_bor = dma("g", biasr[0:1, 1, :], d_abo[:])
    tok_b1 = dma("y", bs1c[:], d_fbs1[:])
    tok_w2 = dma("y", ws2c[:], d_fWs2[:])
    tok_b2 = dma("y", bs2c[0:1, 0:1], d_fbs2[:])
    tok_onesf = op("v", "memset", onesf[:], 1.0)
    tok_onesb = op("v", "memset", onesb[:], 1.0)
    constcol = t_cc.ap()
    tok_cc = op("v", "memset", constcol[:, 0:1], -16.0)
    tok_cc2 = op("v", "memset", constcol[:, 1:2], 63.5)

    def wslab_load(dst, dsrc, j, after=()):
        src4 = dsrc.rearrange("(a p) (c n) -> p a c n", p=128, n=128)
        toks = []
        for slot, jj in enumerate((j, j + 4)):
            toks.append(dma("g", dst[:, :, slot, :], src4[:, :, jj, :], after=after))
        return toks

    def wfull_load(dst, dsrc, after=()):
        return [dma("g", dst[:], dsrc.rearrange("(a p) n -> p a n", p=128), after=after)]

    def emit_ln_tile(tt, after_tok, dump_ap, dump_bank, want_f32, apply_after=()):
        so = (tt % 2) * 8
        xtile = rap(t_x, NT * D, 0, 128, tt * D, [[1, D]])
        t_r = op("v", "tensor_reduce", stats[:, so:so + 1], xtile, AXX, ALU.add,
                 negate=True, after=[after_tok])
        t_nm = op("s", "mul", stats[:, so + 1:so + 2], stats[:, so:so + 1], 1.0 / D,
                  after=[t_r])
        t_sq = op("s", "activation", dump_ap, xtile, AFT.Square,
                  bias=stats[:, so + 1:so + 2], scale=1.0,
                  accum_out=stats[:, so + 2:so + 3],
                  after=[hard(t_nm)] + (bank[dump_bank].for_write() if dump_bank else []))
        if dump_bank:
            bank[dump_bank].wrote(t_sq)
        t_var = op("v", "tensor_scalar", stats[:, so + 3:so + 4], stats[:, so + 2:so + 3],
                   1.0 / D, 1e-5, ALU.mult, ALU.add, after=[t_sq])
        t_sqrt = op("s", "activation", stats[:, so + 4:so + 5], stats[:, so + 3:so + 4],
                    AFT.Sqrt, after=[t_var])
        t_rstd = op("v", "reciprocal", stats[:, so + 5:so + 6], stats[:, so + 4:so + 5],
                    after=[t_sqrt])
        dst = scrf[:] if want_f32 else xt[:]
        t_ap = op("v", "tensor_scalar", dst, xtile, stats[:, so + 1:so + 2],
                  stats[:, so + 5:so + 6], ALU.add, ALU.mult,
                  after=[hard(t_rstd)] + list(apply_after))
        return t_ap

    def emit_xn_for_attn(tok_res):
        tok_xn = [None] * NT
        xt_guard = [None]
        for tt in range(NT):
            dmp = "A0" if tt % 2 == 0 else "A1"
            t_ap = emit_ln_tile(tt, tok_res[tt], pA[tt % 2][:], dmp, False,
                                apply_after=[xt_guard[0]])
            last_ev = None
            for half in range(2):
                db = "D0" if half == 0 else "D1"
                tts = []
                for c4 in range(4):
                    c = half * 4 + c4
                    tts.append(tr(pD[half][:, c4 * 128:(c4 + 1) * 128],
                                  xt[:, c * 128:(c + 1) * 128],
                                  after=[t_ap, tok_id] + (bank[db].for_write() if c4 == 0 else [])))
                bank[db].wrote(tts[-1])
                dstap = rap(t_xn, NCH * S, 0, 128, (half * 4) * S + tt * 128,
                            [[S, 4], [1, 128]])
                srcap = rap(h_pD0 if half == 0 else h_pD1, 1024, 0, 128, 0,
                            [[128, 4], [1, 128]])
                ev = op("s", "copy", dstap, srcap, after=[tts[-1]])
                bank[db].read(ev)
                last_ev = ev
                xt_guard[0] = tts[-1]
            tok_xn[tt] = last_ev
        return tok_xn

    def emit_attn_a(tok_res):
        tok_xn = emit_xn_for_attn(tok_res)
        dbg_point("xn", tok_xn)
        vp_init = []
        for hh in range(4):
            if hh % 2 == 0:
                vp_init.append(op("v", "memset",
                                  rap(t_vp, 6208, 0, 128, VA_OFF[hh] + 64,
                                      [[VA_TT, NT], [1, 1]]), 1.0))
            else:
                vp_init.append(op("v", "memset",
                                  rap(t_vp, 6208, 0, 128, VA_OFF[hh],
                                      [[VA_TT, NT], [1, 1]]), 1.0))
                vp_init.append(op("v", "memset",
                                  rap(t_vp, 6208, 0, 128, VA_OFF[hh] + 1,
                                      [[VA_TT, NT], [1, 63]]), 0.0))
        wo_tok = wfull_load(wB, d_aWo)
        PS = [pB[0][:, 0:512], pB[1][:, 0:512], pA[0][:, 0:512], pA[1][:, 0:512]]
        PSB = ["B0", "B1", "A0", "A1"]
        tbl_cons = [None, None]
        slab_tok = {"q": wslab_load(wq_s, d_aWq, 0),
                    "k": wslab_load(wk_s, d_aWk, 0),
                    "v": wslab_load(wv_s, d_aWv, 0)}
        slab_cons = {"q": None, "k": None, "v": None}
        et_cons = [None, None, None, None]
        last_ctx_ev = None
        rope_done = [None]
        v_ev_last = [None]

        for j in range(4):
            for phase in ("q", "k"):
                slab_t = {"q": t_wq, "k": t_wk}[phase]
                bcol = bqc if phase == "q" else bkc
                btok = tok_bqc if phase == "q" else tok_bkc
                for b in range(NB):
                    sslot = b % 2
                    t_s = dma("y", tbl[:, sslot, 0, :], d_sin[j, :, b * 512:(b + 1) * 512],
                              after=[tbl_cons[sslot], last_ctx_ev])
                    t_c = dma("y", tbl[:, sslot, 1, :], d_cos[j, :, b * 512:(b + 1) * 512],
                              after=[tbl_cons[sslot], last_ctx_ev])
                    si = 2 * (b % 2)
                    mm_t = []
                    for ci in range(2):
                        dst, bnm = PS[si + ci], PSB[si + ci]
                        first_after = ([tok_xn[4 * b + 3]] + slab_tok[phase]
                                       + bank[bnm].for_write())
                        last = None
                        for a in range(NCH):
                            last = mm(dst, rap(slab_t, 2048, 0, 128, a * 256 + ci * 128,
                                               [[1, 128]]),
                                      xn[:, a, b * 512:(b + 1) * 512],
                                      a == 0, a == NCH - 1,
                                      after=(first_after if a == 0 else []))
                        bank[bnm].wrote(last)
                        mm_t.append(last)
                    slab_cons[phase] = mm_t[-1]
                    Pe, Po = PS[si], PS[si + 1]
                    sin_ap, cos_ap = tbl[:, sslot, 0, :], tbl[:, sslot, 1, :]
                    if phase == "q":
                        d_lo = qc[:, j, b * 512:(b + 1) * 512]
                        d_hi = qc[:, j + 4, b * 512:(b + 1) * 512]
                    else:
                        d_lo = kp[:, 0, b * 512:(b + 1) * 512]
                        d_hi = kp[:, 1, b * 512:(b + 1) * 512]
                    be, bo_ = bcol[:, j:j + 1], bcol[:, j + 4:j + 5]
                    s1, s2 = scrf[:, 0:512], scrf[:, 512:1024]
                    op("v", "scalar_tensor_tensor", s1, Pe, be, cos_ap, ALU.add, ALU.mult,
                       after=[mm_t[0], t_c, btok])
                    op("v", "scalar_tensor_tensor", s2, Po, bo_, sin_ap, ALU.add, ALU.mult,
                       after=[mm_t[1], t_s])
                    op("v", "tensor_sub", d_lo, s1, s2)
                    t3 = op("v", "scalar_tensor_tensor", s1, Pe, be, sin_ap, ALU.add, ALU.mult)
                    t4 = op("v", "scalar_tensor_tensor", s2, Po, bo_, cos_ap, ALU.add, ALU.mult)
                    t_hi = op("v", "tensor_sub", d_hi, s1, s2)
                    tbl_cons[sslot] = t_hi
                    rope_done[0] = t_hi
                    bank[PSB[si]].read(t3)
                    bank[PSB[si + 1]].read(t4)
                if j < 3:
                    slab_tok[phase] = wslab_load(
                        {"q": wq_s, "k": wk_s}[phase],
                        {"q": d_aWq, "k": d_aWk}[phase], j + 1,
                        after=[slab_cons[phase]])
            dbg_point("qk", [rope_done[0]])
            for tt in range(NT):
                bnm = "B0" if tt % 2 == 0 else "B1"
                dstp = pB[tt % 2][:, 0:256]
                first_after = ([tok_xn[tt]] + slab_tok["v"] + vp_init
                               + bank[bnm].for_write())
                last = None
                for a in range(NCH):
                    last = mm(dstp, xn[:, a, tt * 128:(tt + 1) * 128], wv_s[:, a],
                              a == 0, False, after=(first_after if a == 0 else []))
                last = mm(dstp, onesb[0:1, 0:128],
                          rap(t_bias, 2048, 0, 1, j * 128, [[512, 2], [1, 128]]),
                          False, True, after=[tok_bvr, tok_onesb])
                bank[bnm].wrote(last)
                slab_cons["v"] = last
                pbh = h_pB0 if tt % 2 == 0 else h_pB1
                ev_e = op("s", "copy",
                          rap(t_vp, 6208, 0, 128, tt * VA_TT, [[193, 2], [1, 64]]),
                          rap(pbh, 512, 0, 128, 0, [[128, 2], [1, 64]]), after=[last])
                ev_o = op("s", "copy",
                          rap(t_vp, 6208, 0, 128, tt * VA_TT + 65 + 64, [[193, 2], [1, 64]]),
                          rap(pbh, 512, 0, 128, 64, [[128, 2], [1, 64]]))
                bank[bnm].read(ev_o)
                v_ev_last[0] = ev_o
            if j < 3:
                slab_tok["v"] = wslab_load(wv_s, d_aWv, j + 1, after=[slab_cons["v"]])
            dbg_point("v", [v_ev_last[0]])

            for hp in range(2):
                lanes = [2 * hp, 2 * hp + 1]
                qch = [j if hh < 2 else j + 4 for hh in lanes]
                ksl = [0 if hh < 2 else 1 for hh in lanes]
                for qt in range(NB):
                    nkt = 4 * qt + 4
                    np_ = (nkt + 1) // 2
                    ctx_started = [False, False]
                    pv_lane = [None, None]
                    pend = None

                    def emit_pv(p, exps, msks):
                        for L in range(2):
                            hh = lanes[L]
                            ctxp = pB[L]
                            cb = "B0" if L == 0 else "B1"
                            ebv = (et if p % 2 == 0 else et2)[:, L, :]
                            pv_last = None
                            for sub in range(2):
                                kt = 2 * p + sub
                                d_ = kt - 4 * qt
                                doff = max(0, d_) * 128
                                if L == 0:
                                    lhsv = rap(t_vp, 6208, 0, 128,
                                               kt * VA_TT + VA_OFF[hh], [[1, 65]])
                                    dstc = ctxp[0:65, doff:512]
                                else:
                                    lhsv = rap(t_vp, 6208, 0, 128,
                                               kt * VA_TT + VA_OFF[hh], [[1, 128]])
                                    dstc = ctxp[:, doff:512]
                                aft = [exps[L]] + msks[L]
                                if not ctx_started[L]:
                                    aft += bank[cb].for_write()
                                pv_last = mm(dstc, lhsv,
                                             ebv[:, sub * 512 + doff:(sub + 1) * 512],
                                             not ctx_started[L], kt == nkt - 1,
                                             after=aft)
                                ctx_started[L] = True
                            et_cons[2 * (p % 2) + L] = pv_last
                            pv_lane[L] = pv_last

                    for p in range(np_):
                        smm_lane = [None, None]
                        msk_lane = [[], []]
                        exp_l = [None, None]
                        for L in range(2):
                            prow = 64 * L
                            sreg = pA[L]
                            sb = "A0" if L == 0 else "A1"
                            smm = []
                            for sub in range(2):
                                kt = 2 * p + sub
                                lhs = rap(t_kp, 2 * S, prow, 64,
                                          ksl[L] * S + kt * 128, [[1, 128]])
                                rhs = rap(t_qc, NCH * S, prow, 64,
                                          qch[L] * S + qt * 512, [[1, 512]])
                                smm.append(mm(sreg[:, sub * 512:(sub + 1) * 512], lhs,
                                              rhs, True, True,
                                              after=(bank[sb].for_write()
                                                     + [rope_done[0], v_ev_last[0]]
                                                     if sub == 0 else [])))
                            bank[sb].wrote(smm[-1])
                            smm_lane[L] = smm
                        if pend is not None:
                            emit_pv(*pend)
                        for L in range(2):
                            sb = "A0" if L == 0 else "A1"
                            ebv = (et if p % 2 == 0 else et2)[:, L, :]
                            t_exp = op("s", "activation", ebv, pA[L][:],
                                       AFT.Exp, bias=constcol[:, 0:1], scale=0.125,
                                       after=[smm_lane[L][-1],
                                              et_cons[2 * (p % 2) + L], tok_cc])
                            bank[sb].read(t_exp)
                            exp_l[L] = t_exp
                        for L in range(2):
                            for sub in range(2):
                                kt = 2 * p + sub
                                d_ = kt - 4 * qt
                                if d_ >= 0:
                                    ebv = (et if p % 2 == 0 else et2)[:, L, :]
                                    sl = ebv[:, sub * 512 + d_ * 128:
                                             sub * 512 + d_ * 128 + 128]
                                    msk_lane[L].append(op("v", "tensor_tensor", sl, sl,
                                                          trib[:], ALU.mult,
                                                          after=[exp_l[L], tok_tri]))
                        pend = (p, exp_l, msk_lane)
                        if dbg == "et" and hp == 0 and qt == 0 and p == 0:
                            emit_pv(*pend)
                            dbg_point("et", pv_lane)
                    emit_pv(*pend)
                    for L in range(2):
                        hh = lanes[L]
                        ctxp = pB[L]
                        cb = "B0" if L == 0 else "B1"
                        bank[cb].wrote(pv_lane[L])
                        rowd = 64 if L == 0 else 0
                        t_ln = op("s", "activation", rr[rowd:rowd + 1, :],
                                  ctxp[rowd:rowd + 1, :], AFT.Ln, after=[pv_lane[L]])
                        t_rc = op("s", "activation", rr[rowd:rowd + 1, :],
                                  rr[rowd:rowd + 1, :], AFT.Exp, scale=-1.0,
                                  after=[hard(t_ln)])
                        db = "D0" if L == 0 else "D1"
                        dpf = pDf[L]
                        if L == 0:
                            t_bc = mm(dpf[0:64, :], onesf[64:65, 0:64], rr[64:65, :],
                                      True, True,
                                      after=[t_rc, tok_onesf] + bank[db].for_write())
                            rows = slice(0, 64)
                        else:
                            t_bc = mm(dpf[64:128, :], onesf[0:1, 0:64], rr[0:1, :],
                                      True, True,
                                      after=[t_rc, tok_onesf] + bank[db].for_write(),
                                      tp=(0, 64))
                            rows = slice(64, 128)
                        bank[db].wrote(t_bc)
                        t_be = op("s", "copy", bca[rows, :], dpf[rows, :], after=[t_bc])
                        bank[db].read(t_be)
                        dst = rap(t_qc, NCH * S, rows.start, 64,
                                  qch[L] * S + qt * 512, [[1, 512]])
                        srcc = ctxp[rows, 0:512] if L == 1 else ctxp[0:64, 0:512]
                        t_ev = op("v", "tensor_tensor", dst, srcc, bca[rows, :],
                                  ALU.mult, after=[t_be])
                        bank[cb].read(t_ev)
                        last_ctx_ev = t_ev
                        if dbg == "ctx" and j == 0 and hh == 0 and qt == 0:
                            dbg_point("ctx", [t_ev])

        tok_y = [None] * NT
        for tt in range(NT):
            for ch in range(2):
                bnm = "B0" if (tt * 2 + ch) % 2 == 0 else "B1"
                dstp = pB[0] if bnm == "B0" else pB[1]
                first_after = [last_ctx_ev] + wo_tok + bank[bnm].for_write()
                last = None
                for a in range(NCH):
                    last = mm(dstp[:, 0:512], qc[:, a, tt * 128:(tt + 1) * 128],
                              wB[:, a, ch * 512:(ch + 1) * 512],
                              a == 0, False, after=(first_after if a == 0 else []))
                last = mm(dstp[:, 0:512], onesb[0:1, 0:128],
                          rap(t_bias, 2048, 0, 1, 1024 + ch * 512, [[1, 512]]),
                          False, True, after=[tok_bor])
                bank[bnm].wrote(last)
                t_add = op("v", "tensor_tensor", x[:, tt, ch * 512:(ch + 1) * 512],
                           dstp[:, 0:512], x[:, tt, ch * 512:(ch + 1) * 512], ALU.add,
                           after=[last])
                bank[bnm].read(t_add)
                tok_y[tt] = t_add
        wb_free[0] = last
        return tok_y

    def emit_ffn(tok_res, second):
        eg16 = t_eg16.ap()
        pg16 = t_pg16.ap()
        eg_toks = []
        prev = tok_res[0]
        for tt in range(NT):
            eview = rap(t_eg16, NT * 128, 0, 128, tt * 128, [[16, 8], [1, 16]])
            t0 = op("g", "memset", eview, 1.0, after=[prev])
            t1 = op("g", "affine_select", out=eview, in_=eview,
                    pattern=[[-1, 8], [0, 16]], compare_op=ALU.is_ge, fill=0.0,
                    base=-8 * tt, channel_multiplier=1, after=[t0])
            t2 = op("g", "affine_select", out=eview, in_=eview,
                    pattern=[[1, 8], [0, 16]], compare_op=ALU.is_ge, fill=0.0,
                    base=8 * tt, channel_multiplier=-1, after=[t1])
            prev = t2
            eg_toks.append(t2)
        pg_toks = []
        for tt in range(NT):
            db = "D0" if tt % 2 == 0 else "D1"
            dp = pD[0] if db == "D0" else pD[1]
            t_tp = tr(dp[:, 0:128], eg16[:, tt, :],
                      after=[eg_toks[tt], tok_id] + bank[db].for_write())
            bank[db].wrote(t_tp)
            t_pg = op("v", "tensor_scalar", pg16[:, tt, :], dp[:, 0:128], 1.0 / 16.0,
                      None, ALU.mult, after=[t_tp])
            bank[db].read(t_pg)
            pg_toks.append(t_pg)

        pool_last = None
        for tt in range(NT):
            t_ap = emit_ln_tile(tt, tok_res[tt], pA[1][:], "A1", True)
            t_hi = op("v", "tensor_copy", xt[:], scrf[:], after=[hard(t_ap), pool_last])
            t_lo = op("v", "tensor_sub", xlo[:], scrf[:], xt[:], after=[hard(t_hi)])
            for src, src_tok in ((xt, t_hi), (xlo, t_lo)):
                for ch in range(2):
                    aft = [pg_toks[tt], src_tok]
                    if tt == 0 and src is xt and ch == 0:
                        aft += bank["A0"].for_write()
                    pool_last = mm(pA[0][:, ch * 512:(ch + 1) * 512], pg16[:, tt, :],
                                   src[:, ch * 512:(ch + 1) * 512],
                                   tt == 0 and src is xt,
                                   tt == NT - 1 and src is xlo and ch == 1,
                                   after=aft)
        bank["A0"].wrote(pool_last)
        tok_lt = dma("y", lowtri[:], d_lowtri[:], after=[pool_last])
        tok_ws1 = dma("y", f_ws1[:], d_fWs1.rearrange("(a p) n -> p a n", p=128),
                      after=[wb_free[0]] if wb_free[0] else [])

        t_g = op("v", "tensor_copy", f_gtm[:], pA[0][:], after=[pool_last])
        dbg_point("fG", [t_g])
        bank["A0"].read(t_g)
        t_gh = op("v", "tensor_copy", f_ghl[:, 0, :], f_gtm[:], after=[hard(t_g)])
        t_gl = op("v", "tensor_sub", f_ghl[:, 1, :], f_gtm[:], f_ghl[:, 0, :],
                  after=[hard(t_gh)])
        gt_toks = []
        for hl in range(2):
            src_tok = t_gh if hl == 0 else t_gl
            for half in range(2):
                db = "D0" if half == 0 else "D1"
                dp = pD[0] if half == 0 else pD[1]
                tts = []
                for c4 in range(4):
                    c = half * 4 + c4
                    tts.append(tr(dp[:, c4 * 128:(c4 + 1) * 128],
                                  f_ghl[:, hl, c * 128:(c + 1) * 128],
                                  after=[src_tok] + (bank[db].for_write() if c4 == 0 else [])))
                bank[db].wrote(tts[-1])
                dstap = rap(t_fgfm, 2048, 0, 128, hl * 1024 + (half * 4) * 128,
                            [[128, 4], [1, 128]])
                srcap = rap(h_pD0 if half == 0 else h_pD1, 1024, 0, 128, 0,
                            [[128, 4], [1, 128]])
                ev = op("v", "tensor_copy", dstap, srcap, after=[tts[-1]])
                bank[db].read(ev)
                gt_toks.append(ev)
        t_w1 = op("v", "tensor_copy", f_w1h[:, 0], f_ws1.rearrange("p a n -> p (a n)"),
                  after=[tok_ws1])
        t_w1l = op("v", "tensor_sub", f_w1h[:, 1], f_ws1.rearrange("p a n -> p (a n)"),
                   f_w1h[:, 0], after=[hard(t_w1)])
        t_hm = [None, None]
        for m_ in range(2):
            bnm = "B0"
            first = True
            last = None
            for a in range(NCH):
                for wi in range(2):
                    for gi in range(2):
                        aft = []
                        if first:
                            aft = [t_w1l] + gt_toks + bank[bnm].for_write()
                        last = mm(pB[0][:, 0:128],
                                  rap(t_fw1h, 4096, 0, 128, wi * 2048 + a * 256 + m_ * 128,
                                      [[1, 128]]),
                                  f_gfm[:, gi, a, :], first,
                                  a == NCH - 1 and wi == 1 and gi == 1, after=aft)
                        first = False
            bank[bnm].wrote(last)
            t_hm[m_] = op("s", "activation", f_h[:, m_, :], pB[0][:, 0:128], AFT.Relu,
                          bias=bs1c[:, m_:m_ + 1], after=[last, tok_b1])
            bank[bnm].read(t_hm[m_])
        t_hh = op("v", "tensor_copy", f_hhl[:, 0], f_h.rearrange("p a n -> p (a n)"),
                  after=[t_hm[0], t_hm[1]])
        t_hl = op("v", "tensor_sub", f_hhl[:, 1], f_h.rearrange("p a n -> p (a n)"),
                  f_hhl[:, 0], after=[hard(t_hh)])
        t_2h = op("v", "tensor_copy", ws2hl[:, 0:2], ws2c[:], after=[tok_w2])
        t_2l = op("v", "tensor_sub", ws2hl[:, 2:4], ws2c[:], ws2hl[:, 0:2],
                  after=[hard(t_2h)])
        first = True
        last = None
        for m_ in range(2):
            for wi in range(2):
                for gi in range(2):
                    aft = []
                    if first:
                        aft = [t_hl, t_2l] + bank["B1"].for_write()
                    last = mm(pB[1][0:1, 0:128], ws2hl[:, 2 * wi + m_:2 * wi + m_ + 1],
                              f_hhl[:, gi, m_, :], first,
                              m_ == 1 and wi == 1 and gi == 1, after=aft)
                    first = False
        bank["B1"].wrote(last)
        t_sr = op("v", "tensor_scalar", f_srow[0:1, 0:128], pB[1][0:1, 0:128],
                  bs2c[0:1, 0:1], None, ALU.add, after=[last, tok_b2])
        dbg_point("fscore", [t_sr, t_hm[0], t_hm[1]])
        bank["B1"].read(t_sr)
        t_sc = dma("y", f_scol[:, 0:1], f_srow[0:1, 0:128], after=[t_sr])
        t_sb = mm(pA[1][:, 0:128], onesf[0:1, 0:128], f_srow[0:1, 0:128], True, True,
                  after=[t_sr, tok_onesf] + bank["A1"].for_write())
        bank["A1"].wrote(t_sb)
        t_cmp = op("v", "tensor_scalar", f_cmp[:], pA[1][:, 0:128], f_scol[:, 0:1], None,
                   ALU.is_gt, after=[t_sb, t_sc])
        t_eq = op("v", "scalar_tensor_tensor", f_madd[:], pA[1][:, 0:128], f_scol[:, 0:1],
                  lowtri[:], ALU.is_equal, ALU.mult, after=[hard(t_cmp), t_sc, tok_lt])
        bank["A1"].read(t_eq)
        t_add_ = op("v", "tensor_add", f_cmp[:], f_cmp[:], f_madd[:],
                    after=[hard(t_eq)])
        t_red_ = op("v", "tensor_reduce", f_cnt[:, 0:1], f_cmp[:], AXX, ALU.add,
                    after=[hard(t_add_)])
        t_mq = op("v", "tensor_tensor", f_mq[:, 0:1], f_cnt[:, 0:1], constcol[:, 1:2],
                  ALU.is_lt, after=[tok_cc2, hard(t_red_)])
        dbg_point("fcount", [t_mq])
        t_mqb = op("v", "tensor_copy", f_mqb[:, 0:1], f_mq[:, 0:1], after=[hard(t_mq)])
        t_mr = mm(pB[1][0:1, 0:128], f_mqb[:, 0:1], identb[:], True, True,
                  after=[t_mqb] + bank["B1"].for_write())
        bank["B1"].wrote(t_mr)
        t_mre = op("v", "tensor_copy", f_mrow[0:1, 0:128], pB[1][0:1, 0:128], after=[t_mr])
        bank["B1"].read(t_mre)
        t_mb = mm(pA[1][:, 0:128], onesf[0:1, 0:128], f_mrow[0:1, 0:128], True, True,
                  after=[t_mre] + bank["A1"].for_write())
        bank["A1"].wrote(t_mb)
        t_madd = op("v", "tensor_scalar", f_madd[:], pA[1][:, 0:128], 1.0, 1e30,
                    ALU.subtract, ALU.mult, after=[t_mb])
        bank["A1"].read(t_madd)
        dbg_point("fmask", [t_madd, t_mq])

        wo_tok = wfull_load(wB, d_fWo, after=[wb_free[0]])
        sl_i = [0]
        sl_cons = [None, None]
        def stream_mm(dsrc, out_sb, token_major, after_extra):
            toks = []
            src4 = dsrc.rearrange("(a p) (c n) -> p a c n", p=128, n=128)
            for m_ in range(NCH):
                par_ = sl_i[0] % 2
                sl_i[0] += 1
                t_ld = dma("g", wsl[:, par_], src4[:, :, m_, :], after=[sl_cons[par_]])
                bnm = "B0" if m_ % 2 == 0 else "B1"
                dstp = pB[0] if bnm == "B0" else pB[1]
                last = None
                for a in range(NCH):
                    aft = []
                    if a == 0:
                        aft = [t_ld] + after_extra + bank[bnm].for_write()
                    if token_major:
                        last = mm(dstp[:, 0:128], f_gfm[:, 0, a, :], wsl[:, par_, a, :],
                                  a == 0, a == NCH - 1, after=aft)
                    else:
                        last = mm(dstp[:, 0:128], wsl[:, par_, a, :], f_gfm[:, 0, a, :],
                                  a == 0, a == NCH - 1, after=aft)
                bank[bnm].wrote(last)
                sl_cons[par_] = last
                ev = op("v", "tensor_copy", out_sb[:, m_, :], dstp[:, 0:128], after=[last])
                bank[bnm].read(ev)
                toks.append(ev)
            return toks

        tq = stream_mm(d_fWq, f_qg, False, gt_toks)
        tk = stream_mm(d_fWk, f_kg, False, [])
        tv = stream_mm(d_fWv, f_vg, True, [])

        ctx_toks = []
        for h in range(H):
            ch_, pr = h // 2, 64 * (h % 2)
            lq = rap(t_fqg, 1024, pr, 64, ch_ * 128, [[1, 128]])
            lk = rap(t_fkg, 1024, pr, 64, ch_ * 128, [[1, 128]])
            t_sg = mm(pB[0][:, 0:128], lq, lk, True, True,
                      after=[tq[-1], tk[-1], t_madd] + bank["B0"].for_write())
            bank["B0"].wrote(t_sg)
            t_sm = op("v", "tensor_add", f_sm[:], pB[0][:, 0:128], f_madd[:], after=[t_sg])
            bank["B0"].read(t_sm)
            t_nmx = op("v", "tensor_reduce", f_cnt[:, 1:2], f_sm[:], AXX, ALU.max,
                       negate=True)
            t_bias_ = op("s", "mul", f_cnt[:, 2:3], f_cnt[:, 1:2], 0.125, after=[t_nmx])
            t_p = op("s", "activation", f_pb[:], f_sm[:], AFT.Exp,
                     bias=f_cnt[:, 2:3], scale=0.125, accum_out=f_cnt[:, 3:4],
                     after=[hard(t_bias_)])
            t_rd = op("v", "reciprocal", f_cnt[:, 1:2], f_cnt[:, 3:4], after=[t_p])
            t_pn = op("v", "tensor_scalar", f_pb[:], f_pb[:], f_cnt[:, 1:2], None,
                      ALU.mult, after=[hard(t_rd)])
            db = "D0" if h % 2 == 0 else "D1"
            dp = pD[0] if h % 2 == 0 else pD[1]
            t_tp = tr(dp[:, 0:128], f_pb[:],
                      after=[t_pn] + bank[db].for_write())
            bank[db].wrote(t_tp)
            t_pt = op("v", "tensor_copy", f_pt[:], dp[:, 0:128], after=[t_tp])
            bank[db].read(t_pt)
            lv = rap(t_fvg, 1024, 0, 128, ch_ * 128 + pr, [[1, 64]])
            if h % 2 == 0:
                t_cx = mm(pB[1][0:64, 0:128], lv, f_pt[:], True, True,
                          after=[t_pt, tv[-1]] + bank["B1"].for_write())
                rows = slice(0, 64)
            else:
                t_cx = mm(pB[1][64:128, 0:128], lv, f_pt[:], True, True,
                          after=[t_pt, tv[-1]] + bank["B1"].for_write(), tp=(0, 64))
                rows = slice(64, 128)
            bank["B1"].wrote(t_cx)
            t_cv = op("v", "tensor_copy",
                      rap(t_fctx, 1024, rows.start, 64, ch_ * 128, [[1, 128]]),
                      pB[1][rows, 0:128], after=[t_cx])
            bank["B1"].read(t_cv)
            ctx_toks.append(t_cv)

        og_toks = []
        last_og_mm = [None]
        for ch in range(2):
            bnm = "B0"
            last = None
            for a in range(NCH):
                aft = []
                if a == 0:
                    aft = [ctx_toks[-1]] + wo_tok + bank[bnm].for_write()
                last = mm(pB[0][:, 0:512], f_ctx[:, a, :], wB[:, a, ch * 512:(ch + 1) * 512],
                          a == 0, a == NCH - 1, after=aft)
            bank[bnm].wrote(last)
            last_og_mm[0] = last
            t_og = op("v", "tensor_scalar", f_og[:, ch * 512:(ch + 1) * 512],
                      pB[0][:, 0:512], f_mq[:, 0:1], None, ALU.mult, after=[last])
            bank[bnm].read(t_og)
            og_toks.append(t_og)
        dbg_point("fog", og_toks + ctx_toks)
        tok_y = [None] * NT
        for tt in range(NT):
            for ch in range(2):
                bnm = "B1" if (tt * 2 + ch) % 2 == 0 else "B0"
                dstp = pB[1] if bnm == "B1" else pB[0]
                t_y = mm(dstp[:, 0:512], eg16[:, tt, :],
                         f_og[:, ch * 512:(ch + 1) * 512],
                         True, True,
                         after=[eg_toks[tt]] + og_toks + bank[bnm].for_write())
                bank[bnm].wrote(t_y)
                t_add = op("v", "tensor_tensor", x[:, tt, ch * 512:(ch + 1) * 512],
                           dstp[:, 0:512], x[:, tt, ch * 512:(ch + 1) * 512], ALU.add,
                           after=[t_y, tok_res[tt]])
                bank[bnm].read(t_add)
                tok_y[tt] = t_add
        wb_free[0] = last_og_mm[0]
        return tok_y

    def emit_attn_b(tok_res):
        tok_xn = emit_xn_for_attn(tok_res)
        vpi = [op("v", "memset",
                  rap(t_vp, 6208, 0, 128, hh * 65 + 64, [[260, NT], [1, 1]]), 1.0)
               for hh in range(4)]
        slab_tok = {"q": wslab_load(wq_s, d_bWq, 0),
                    "k": wslab_load(wk_s, d_bWk, 0),
                    "v": wslab_load(wv_s, d_bWv, 0)}
        slab_cons = {"q": None, "k": None, "v": None}
        wo_tok = wfull_load(wB, d_bWo, after=[wb_free[0]])
        tok_wft0 = dma("g", wft[0:HD, :], d_bWf[:], after=[wb_free[0]])
        tok_wft = dma("g", wft[HD:128, :], d_bWf[:], after=[wb_free[0]])
        PS = [pB[0][:, 0:512], pB[1][:, 0:512], pA[0][:, 0:512], pA[1][:, 0:512]]
        PSB = ["B0", "B1", "A0", "A1"]
        last_out_ev = None
        qk_ev = [None]
        v_ev = [None]
        for j in range(4):
            for phase in ("q", "k"):
                slab_t = {"q": t_wq, "k": t_wk}[phase]
                for b in range(NB):
                    si = 2 * (b % 2)
                    for ci in range(2):
                        dst, bnm = PS[si + ci], PSB[si + ci]
                        first_after = ([tok_xn[4 * b + 3]] + slab_tok[phase]
                                       + bank[bnm].for_write())
                        last = None
                        for a in range(NCH):
                            last = mm(dst, rap(slab_t, 2048, 0, 128, a * 256 + ci * 128,
                                               [[1, 128]]),
                                      xn[:, a, b * 512:(b + 1) * 512],
                                      a == 0, a == NCH - 1,
                                      after=(first_after if a == 0 else []))
                        bank[bnm].wrote(last)
                        slab_cons[phase] = last
                        if phase == "q":
                            dstsb = qc[:, (j, j + 4)[ci], b * 512:(b + 1) * 512]
                        else:
                            dstsb = kp[:, ci, b * 512:(b + 1) * 512]
                        ev = op("s", "copy", dstsb, dst, after=[last])
                        bank[bnm].read(ev)
                        qk_ev[0] = ev
                if j < 3:
                    slab_tok[phase] = wslab_load(
                        {"q": wq_s, "k": wk_s}[phase],
                        {"q": d_bWq, "k": d_bWk}[phase], j + 1, after=[slab_cons[phase]])
            for tt in range(NT):
                bnm = "B0" if tt % 2 == 0 else "B1"
                dstp = pB[tt % 2][:, 0:256]
                first_after = [tok_xn[tt]] + slab_tok["v"] + vpi + bank[bnm].for_write()
                last = None
                for a in range(NCH):
                    last = mm(dstp, xn[:, a, tt * 128:(tt + 1) * 128], wv_s[:, a],
                              a == 0, a == NCH - 1,
                              after=(first_after if a == 0 else []))
                bank[bnm].wrote(last)
                slab_cons["v"] = last
                pbh = h_pB0 if tt % 2 == 0 else h_pB1
                ev = op("s", "copy",
                        rap(t_vp, 6208, 0, 128, tt * 260, [[65, 4], [1, 64]]),
                        rap(pbh, 512, 0, 128, 0, [[64, 4], [1, 64]]), after=[last])
                bank[bnm].read(ev)
                v_ev[0] = ev
            if j < 3:
                slab_tok["v"] = wslab_load(wv_s, d_bWv, j + 1, after=[slab_cons["v"]])
            if j == 0:
                dbg_point("bqk", [qk_ev[0], v_ev[0]])
            HHROW = [0, 64, 0, 64]
            HHCH = [j, j, j + 4, j + 4]
            HHKS = [0, 0, 1, 1]
            kv_started = False
            kv_last = None
            kf_guard = [None, None]
            eb_guard = [None, None]
            for tt in range(NT):
                ebuf = xt[:] if tt % 2 == 0 else xt2[:]
                sb = "A0" if tt % 2 == 0 else "A1"
                pfp = pA[0] if tt % 2 == 0 else pA[1]
                pf_last = None
                PFCOL = [0, 512, 256, 768]
                for hh in range(4):
                    aft = [tok_wft0, tok_wft, qk_ev[0]]
                    if hh == 0:
                        aft += bank[sb].for_write()
                    pf_last = mm(pfp[:, PFCOL[hh]:PFCOL[hh] + 256],
                                 rap(t_kp, 2 * S, HHROW[hh], 64,
                                     HHKS[hh] * S + tt * 128, [[1, 128]]),
                                 wft[HHROW[hh]:HHROW[hh] + 64, :], True, True,
                                 after=aft)
                bank[sb].wrote(pf_last)
                t_e = op("s", "activation", ebuf, pfp[:], AFT.Exp,
                         after=[pf_last, eb_guard[tt % 2]])
                t_m1 = op("v", "tensor_scalar", ebuf, ebuf, 1.0, None, ALU.min,
                          after=[t_e])
                t_kf = op("v", "scalar_tensor_tensor", kfb[:, tt % 2, :], pfp[:],
                          0.0, ebuf, ALU.max, ALU.add,
                          after=[hard(t_m1), kf_guard[tt % 2]])
                bank[sb].read(t_kf)
                eb_guard[tt % 2] = t_kf
                if j == 0 and tt == 0:
                    dbg_point("bkf1", [t_kf])
                for hh in range(4):
                    for fc in range(2):
                        aft = [t_kf]
                        if not kv_started:
                            aft += (bank["B0"].for_write() + bank["B1"].for_write()
                                    + [v_ev[0]])
                        first_in_bank = (tt == 0 and hh % 2 == 0 and fc == 0)
                        kv_last = mm(pB[hh // 2][:, (hh % 2) * 130 + fc * 65:
                                                 (hh % 2) * 130 + fc * 65 + 65],
                                     kfb[:, tt % 2, PFCOL[hh] + fc * 128:
                                         PFCOL[hh] + fc * 128 + 128],
                                     rap(t_vp, 6208, 0, 128, tt * 260 + hh * 65,
                                         [[1, 65]]),
                                     first_in_bank,
                                     tt == NT - 1 and hh == 3 and fc == 1,
                                     after=aft)
                        kv_started = True
                kf_guard[tt % 2] = kv_last
                if j == 0 and tt == 0:
                    dbg_point("bkv2", [kv_last])
                if j == 0 and tt == 2:
                    dbg_point("bkv3", [kv_last])
            bank["B0"].wrote(kv_last)
            bank["B1"].wrote(kv_last)
            t_kv0 = op("s", "copy", kvsb[:, 0:2], pB[0][:, 0:260], after=[kv_last])
            t_kv = op("s", "copy", kvsb[:, 2:4], pB[1][:, 0:260], after=[kv_last])
            bank["B0"].read(t_kv0)
            bank["B1"].read(t_kv)
            if j == 0:
                dbg_point("bkv", [t_kv0, t_kv])
            out_guard = None
            for b in range(NB):
                qf_toks = []
                for hh in range(4):
                    sb = "A0" if hh % 2 == 0 else "A1"
                    dstp = pA[0] if hh % 2 == 0 else pA[1]
                    ebuf = xt[:] if hh % 2 == 0 else xt2[:]
                    last = None
                    for fc in range(2):
                        aft = [tok_wft0, tok_wft, qk_ev[0]]
                        if fc == 0:
                            aft += bank[sb].for_write()
                        last = mm(dstp[:, fc * 512:(fc + 1) * 512],
                                  rap(t_wft, FD, HHROW[hh], 64, fc * 128, [[1, 128]]),
                                  rap(t_qc, NCH * S, HHROW[hh], 64,
                                      HHCH[hh] * S + b * 512, [[1, 512]]),
                                  True, True, after=aft)
                    bank[sb].wrote(last)
                    t_e = op("s", "activation", ebuf, dstp[:], AFT.Exp,
                             after=[last, eb_guard[hh % 2]])
                    t_m1 = op("v", "tensor_scalar", ebuf, ebuf, 1.0, None, ALU.min,
                              after=[t_e])
                    t_qf = op("v", "scalar_tensor_tensor",
                              rap(t_qf4, 4096, 0, 128, hh * 1024, [[1, 1024]]),
                              dstp[:], 0.0, ebuf, ALU.max, ALU.add,
                              after=[hard(t_m1), out_guard])
                    bank[sb].read(t_qf)
                    eb_guard[hh % 2] = t_qf
                    qf_toks.append(t_qf)
                for t4 in range(4):
                    tt = b * 4 + t4
                    obn = "B0" if t4 % 2 == 0 else "B1"
                    obp = pB[0] if t4 % 2 == 0 else pB[1]
                    obh = h_pB0 if t4 % 2 == 0 else h_pB1
                    last = None
                    for hh in range(4):
                        for fc in range(2):
                            aft = [t_kv0, t_kv, qf_toks[hh]]
                            if hh == 0 and fc == 0:
                                aft += bank[obn].for_write()
                            last = mm(obp[:, hh * 65:hh * 65 + 65],
                                      rap(t_qf4, 4096, 0, 128,
                                          hh * 1024 + fc * 512 + t4 * 128, [[1, 128]]),
                                      kvsb[:, hh, fc, :],
                                      fc == 0, fc == 1, after=aft)
                    bank[obn].wrote(last)
                    out_guard = last
                    rrof = 0 if t4 % 2 == 0 else 8
                    t_n = op("v", "tensor_scalar", rr[:, rrof:rrof + 4],
                             rap(obh, 512, 0, 128, 64, [[65, 4]]),
                             1e-6, None, ALU.add, after=[last])
                    t_r = op("v", "reciprocal", rr[:, rrof + 4:rrof + 8],
                             rr[:, rrof:rrof + 4], after=[hard(t_n)])
                    ot_toks = []
                    for hh in range(4):
                        blk, par2 = hh // 2, hh % 2
                        ot_toks.append(op("v", "tensor_scalar",
                                          otm[:, blk * 128 + par2 * 64:
                                              blk * 128 + par2 * 64 + 64],
                                          obp[:, hh * 65:hh * 65 + 64],
                                          rr[:, rrof + 4 + hh:rrof + 5 + hh], None,
                                          ALU.mult, after=[hard(t_r)]))
                    bank[obn].read(ot_toks[-1])
                    for blk in range(2):
                        db = "D0" if (tt * 2 + blk) % 2 == 0 else "D1"
                        dp = pD[0] if db == "D0" else pD[1]
                        dph = h_pD0 if db == "D0" else h_pD1
                        t_tp = tr(dp[:, 0:128], otm[:, blk * 128:(blk + 1) * 128],
                                  after=[ot_toks[2 * blk], ot_toks[2 * blk + 1]]
                                  + bank[db].for_write())
                        bank[db].wrote(t_tp)
                        ev = op("s", "copy",
                                rap(t_qc, NCH * S, 0, 128,
                                    HHCH[2 * blk] * S + tt * 128, [[1, 128]]),
                                rap(dph, 1024, 0, 128, 0, [[1, 128]]), after=[t_tp])
                        bank[db].read(ev)
                        last_out_ev = ev
            if j == 0:
                dbg_point("bout", [last_out_ev])
        tok_y = [None] * NT
        for tt in range(NT):
            for ch in range(2):
                bnm = "B0" if (tt * 2 + ch) % 2 == 0 else "B1"
                dstp = pB[0] if bnm == "B0" else pB[1]
                first_after = [last_out_ev] + wo_tok + bank[bnm].for_write()
                last = None
                for a in range(NCH):
                    last = mm(dstp[:, 0:512], qc[:, a, tt * 128:(tt + 1) * 128],
                              wB[:, a, ch * 512:(ch + 1) * 512],
                              a == 0, a == NCH - 1, after=(first_after if a == 0 else []))
                bank[bnm].wrote(last)
                t_add = op("v", "tensor_tensor", x[:, tt, ch * 512:(ch + 1) * 512],
                           dstp[:, 0:512], x[:, tt, ch * 512:(ch + 1) * 512], ALU.add,
                           after=[last])
                bank[bnm].read(t_add)
                tok_y[tt] = t_add
        wb_free[0] = last
        return tok_y

    wb_free = [None]
    toks = tok_x
    try:
        if n_sub >= 1:
            toks = emit_attn_a(toks)
        if n_sub >= 2:
            toks = emit_ffn(toks, False)
        if n_sub >= 3:
            toks = emit_attn_b(toks)
        if n_sub >= 4:
            toks = emit_ffn(toks, True)
    except StopBuild:
        pass

    st_toks = [B.dma("y", (lambda e, tt=tt: e.dma_start(
        out=dout[tt * 128:(tt + 1) * 128, :], in_=x[:, tt, :])), after=[toks[tt]])
        for tt in range(NT)]
    B.op("y", lambda e: e.nop(), after=st_toks + dbg_outs)

    import contextlib
    es = contextlib.ExitStack()
    B.finalize(nc, es)
    with nc.Block() as block:
        block.tensor(lambda e: B.emit("t", e))
        block.vector(lambda e: B.emit("v", e))
        block.scalar(lambda e: B.emit("s", e))
        block.gpsimd(lambda e: B.emit("g", e))
        block.sync(lambda e: B.emit("y", e))
    es.close()
    return nc


def _host_prep(inp):
    import jax
    import jax.numpy as jnp
    f = lambda a: np.ascontiguousarray(np.asarray(a, dtype=np.float32))
    perm = np.concatenate([np.arange(0, D, 2), np.arange(1, D, 2)])
    m = {}
    m["aWq"] = f(inp["L0_attn_a_Wq"])[:, perm]
    m["aWk"] = f(inp["L0_attn_a_Wk"])[:, perm]
    m["aWv"] = f(inp["L0_attn_a_Wv"])
    m["aWo"] = f(inp["L0_attn_a_Wo"])
    m["abq"] = np.ascontiguousarray(f(inp["L0_attn_a_bq"])[perm].reshape(8, 128).T)
    m["abk"] = np.ascontiguousarray(f(inp["L0_attn_a_bk"])[perm].reshape(8, 128).T)
    m["abv"] = f(inp["L0_attn_a_bv"]).reshape(1, D)
    m["abo"] = f(inp["L0_attn_a_bo"]).reshape(1, D)
    m["bWqT"] = np.ascontiguousarray(f(inp["L0_attn_b_Wq"]).T)
    m["bWkT"] = np.ascontiguousarray(f(inp["L0_attn_b_Wk"]).T)
    m["bWvT"] = np.ascontiguousarray(f(inp["L0_attn_b_Wv"]).T)
    m["bWoT"] = np.ascontiguousarray(f(inp["L0_attn_b_Wo"]).T)
    m["bWfT"] = np.ascontiguousarray(f(inp["L0_attn_b_Wf"]).T)
    m["fWq"] = f(inp["L0_ffn_Wq"])
    m["fWk"] = f(inp["L0_ffn_Wk"])
    m["fWv"] = f(inp["L0_ffn_Wv"])
    m["fWo"] = f(inp["L0_ffn_Wo"])
    m["fWs1"] = f(inp["L0_ffn_Ws1"])
    m["fbs1"] = np.ascontiguousarray(f(inp["L0_ffn_bs1"]).reshape(2, 128).T)
    m["fWs2"] = np.ascontiguousarray(f(inp["L0_ffn_Ws2"]).reshape(2, 128).T)
    m["fbs2"] = f(inp["L0_ffn_bs2"]).reshape(1, 1)
    cpu = jax.devices("cpu")[0]
    with jax.default_device(cpu):
        fr = jnp.arange(D // 2, dtype=jnp.float32)
        fc = jnp.power(10000.0, -2.0 * fr / D)
        pos = jnp.arange(S, dtype=jnp.float32)
        freq = jnp.einsum("i,j->ij", pos, fc)
        sin_e = np.asarray(jnp.sin(freq), dtype=np.float32)
        cos_e = np.asarray(jnp.cos(freq), dtype=np.float32)
    m["ropesin"] = np.ascontiguousarray(sin_e.T.reshape(4, 128, S))
    m["ropecos"] = np.ascontiguousarray(cos_e.T.reshape(4, 128, S))
    m["identc"] = np.eye(128, dtype=np.float32)
    m["lowtri"] = np.tril(np.ones((128, 128), dtype=np.float32), -1)
    ca = np.zeros((128, 128), dtype=np.float32)
    kk = np.arange(128)[:, None]
    qq = np.arange(128)[None, :]
    ca[qq < kk] = -1e30
    m["causadd"] = ca
    m["tri01"] = (qq >= kk).astype(np.float32)
    return m


_N_SUB = 4


def kernel(**inputs):
    xfull = np.asarray(inputs["L0_attn_a_x"], dtype=np.float32)
    common = _host_prep(inputs)
    nc = build_nc(_N_SUB)
    in_maps = []
    for b in range(8):
        im = dict(common)
        im["x_in"] = np.ascontiguousarray(xfull[b])
        in_maps.append(im)
    res = run_bass_kernel_spmd(nc, in_maps, core_ids=list(range(8)))
    return np.stack([np.asarray(res.results[b]["x_out"]) for b in range(8)], axis=0)
